# version 08
# baseline (speedup 1.0000x reference)
import numpy as np
import contextlib
import concourse.bass as bass
import concourse.mybir as mybir
from concourse.bass_utils import run_bass_kernel_spmd
from concourse.alu_op_type import AluOpType as OP

F32 = mybir.dt.float32
BF16 = mybir.dt.bfloat16
AF = mybir.ActivationFunctionType
AX = mybir.AxisListType

NCORES = 8
TPC = 1024
NT = 8
NTB = 4
D = 2048
KC = 16
S = 4096
B = 2
SPC = 512
EPS0 = 1e-6
NSL = 256

ENGS = ("pe", "act", "dve", "pool", "sp")
_BLOCK_ATTR = {"pe": "tensor", "act": "scalar", "dve": "vector", "pool": "gpsimd", "sp": "sync"}

DEBUG_TAP = None
_NO_CC = False


class Prog:
    def __init__(self):
        self.items = {e: [] for e in ENGS}
        self.semcount = {}
        self.last_wait = {e: {} for e in ENGS}
        self.own = {}

    def _bump(self, sem, inc):
        self.semcount[sem] = self.semcount.get(sem, 0) + inc
        return self.semcount[sem]

    SELF_CHAIN = ("dve", "act", "pool")

    def op(self, eng, fn, is_dma=False):
        if eng in self.SELF_CHAIN and not is_dma:
            sem = eng + "_c"
            cnt = self.semcount.get(sem, 0)
            if cnt and self.last_wait[eng].get(sem, 0) < cnt:
                self.last_wait[eng][sem] = cnt
                self.items[eng].append(["w", sem, cnt])
            self.items[eng].append(["op", fn, [], is_dma])
            self.milestone(eng)
            return
        self.items[eng].append(["op", fn, [], is_dma])

    def opm(self, eng, fn, kind="cmp", sem=None, inc=None):
        self.op(eng, fn)
        return self.milestone(eng, kind=kind, sem=sem, inc=inc)

    def dma(self, eng, out_ap, in_ap, stream=None, **kw):
        self.op(eng, lambda E, o=out_ap, i=in_ap, kw=kw: E.dma_start(out=o, in_=i, **kw),
                is_dma=True)
        return self.milestone(eng, kind="dma", sem=(stream or eng) + "_d", inc=16)

    def milestone(self, eng, kind="cmp", sem=None, inc=None):
        sem = sem or (eng + "_c")
        inc = inc if inc is not None else 1
        if kind == "cmp":
            self.own.setdefault(sem, eng)
        for it in reversed(self.items[eng]):
            if it[0] == "op":
                if it[3] and kind != "cc" and inc % 16 != 0:
                    inc = 16
                    if kind == "cmp":
                        kind = "dma"
                        sem = sem if sem.endswith("_d") else sem[:-2] + "_d"
                v = self._bump(sem, inc)
                for j, (s0, i0) in enumerate(it[2]):
                    if s0 == sem:
                        it[2][j] = (sem, i0 + inc)
                        return (sem, v, kind)
                it[2].append((sem, inc))
                return (sem, v, kind)
        raise RuntimeError(f"milestone on empty engine {eng}")

    def cur(self, eng):
        for it in reversed(self.items[eng]):
            if it[0] == "op":
                if it[3] and it[2]:
                    s0 = it[2][0][0]
                    return (s0, self.semcount[s0], "dma")
                if it[2]:
                    sem = eng + "_c"
                    for s, _ in it[2]:
                        if s == sem:
                            return (sem, self.semcount[sem], "cmp")
                return self.milestone(eng)
        return None

    def barrier(self, dst_engs=("pe", "act", "dve", "pool"), src_engs=("pe", "act", "dve", "pool")):
        toks = [self.cur(e) for e in src_engs]
        toks += [(s, v, "dma") for s, v in self.semcount.items() if s.endswith("_d")]
        for d in dst_engs:
            self.wait_all(d, toks)

    def wait(self, eng, tok):
        if tok is None:
            return
        sem, val, kind = tok
        if kind == "cmp" and self.own.get(sem) == eng:
            return
        if kind in ("dma", "cc"):
            val = self.semcount[sem]
        if self.last_wait[eng].get(sem, 0) >= val:
            return
        self.last_wait[eng][sem] = val
        self.items[eng].append(["w", sem, val])

    def wait_all(self, eng, toks):
        for t in list(toks):
            self.wait(eng, t)

    def cc(self, fn, sem):
        self.op("pool", fn, is_dma=True)
        return self.milestone("pool", kind="cc", sem=sem, inc=1)

    def raw_wait(self, eng, sem, val):
        self.items[eng].append(["w", sem, val])

    def check_deadlock(self):
        ptr = {e: 0 for e in ENGS}
        sem = {}
        progress = True
        while progress:
            progress = False
            for e in ENGS:
                while ptr[e] < len(self.items[e]):
                    it = self.items[e][ptr[e]]
                    if it[0] == "op":
                        for s, inc in it[2]:
                            sem[s] = sem.get(s, 0) + inc
                        ptr[e] += 1
                        progress = True
                    else:
                        if sem.get(it[1], 0) >= it[2]:
                            ptr[e] += 1
                            progress = True
                        else:
                            break
        stuck = {e for e in ENGS if ptr[e] < len(self.items[e])}
        if stuck:
            det = []
            for e in stuck:
                it = self.items[e][ptr[e]]
                det.append(f"{e}@{ptr[e]}/{len(self.items[e])} waits {it[1]}>={it[2]}"
                           f" (has {sem.get(it[1], 0)})")
            raise RuntimeError("deadlock in built program: " + "; ".join(det))

    def emit(self, nc, block, stack):
        self.check_deadlock()
        sems = {s: stack.enter_context(nc.semaphore(s)) for s in self.semcount}

        def mk(e):
            def body(eng):
                for it in self.items[e]:
                    if it[0] == "op":
                        inst = it[1](eng)
                        for s, inc in it[2]:
                            inst.then_inc(sems[s], inc)
                    else:
                        eng.wait_ge(sems[it[1]], it[2])
            return body

        for e in ENGS:
            getattr(block, _BLOCK_ATTR[e])(mk(e))

    def stats(self):
        return {e: len(self.items[e]) for e in ENGS}


class Res:
    __slots__ = ("w", "r")

    def __init__(self):
        self.w = None
        self.r = []

    def note_write(self, tok):
        self.w = tok
        self.r = []

    def note_read(self, tok):
        if tok is not None:
            self.r.append(tok)
            if len(self.r) > 8:
                self.r[:] = self.r[-8:]


class Arena:

    def __init__(self, handle, n_f32):
        self.h = handle
        self.n = n_f32
        self.marks = []
        self.off = 0

    def push(self):
        self.marks.append(self.off)

    def pop(self):
        self.off = self.marks.pop()

    def take_f32(self, *shape):
        n = int(np.prod(shape))
        assert self.off + n <= self.n, f"arena overflow: {self.off}+{n} > {self.n}"
        v = self.h.ap()[:, self.off:self.off + n]
        self.off += n
        if len(shape) > 1:
            pat = " ".join(f"d{i}" for i in range(len(shape)))
            v = v.rearrange(f"p ({pat}) -> p {pat}",
                            **{f"d{i}": shape[i] for i in range(len(shape) - 1)})
        return v

    def take_bf16(self, *shape):
        n = int(np.prod(shape))
        assert n % 2 == 0
        assert self.off + n // 2 <= self.n, f"arena overflow bf16: {self.off}+{n // 2} > {self.n}"
        v = self.h.ap()[:, self.off:self.off + n // 2].bitcast(BF16)
        self.off += n // 2
        if len(shape) > 1:
            pat = " ".join(f"d{i}" for i in range(len(shape)))
            v = v.rearrange(f"p ({pat}) -> p {pat}",
                            **{f"d{i}": shape[i] for i in range(len(shape) - 1)})
        return v



def tile_proj_weight(w_kn, nsl=NSL):
    K, N = w_kn.shape
    kc, ns = K // 128, N // nsl
    return np.ascontiguousarray(w_kn.reshape(kc, 128, ns, nsl).transpose(2, 1, 0, 3))



def build_program():
    nc = bass.Bass("TRN2", target_bir_lowering=False)
    stack = contextlib.ExitStack()

    def din(name, shape, dtype=F32):
        return nc.dram_tensor(name, list(shape), dtype, kind="ExternalInput").ap()

    X = din("x", [NT, 128, D])
    OUT = nc.dram_tensor("out", [NT, 128, D], F32, kind="ExternalOutput").ap()
    IDENT = din("ident", [128, 128])
    DECAY = din("decay", [128, NT])

    W_l0 = {k: din(f"l0_{k}", [8, 128, KC, NSL]) for k in ("wq", "wk", "wv", "wo")}
    B_l0 = din("l0_bias", [4, 1, D])
    IOTA8 = din("iota8", [1, 8])
    F0_w13 = din("f0_w13", [9, 8, 128, KC, 256])
    F0_wdn = din("f0_wdn", [9, 2, 128, 8, 1024])
    F0_gate = din("f0_gate", [128, KC, 8])
    F0_eb = din("f0_eb", [1, 8])
    W_l1 = {k: din(f"l1_{k}", [8, 128, KC, NSL]) for k in ("wq", "wlk", "wlv", "wo")}
    B_l1 = din("l1_bias", [4, 1, D])
    L1_wc = din("l1_wc", [128, KC, 1])
    L1_bc = din("l1_bc", [1, 1])
    L1_wkv = din("l1_wkv", [128, KC, 256])
    B_l1_kv = din("l1_bkv", [1, 256])
    L1_qn = din("l1_qn", [1, 128])
    L1_kn = din("l1_kn", [1, 128])
    L1_sinkT = din("l1_sinkT", [128, 16])
    L1_sinkv = din("l1_sinkv", [1, D])
    COSQ = din("cosq", [128, NTB, 32])
    SINQ = din("sinq", [128, NTB, 32])
    COSH = din("cosh", [128, 32])
    SINH = din("sinh", [128, 32])
    COSCK = din("cosck", [32, 32])
    SINCK = din("sinck", [32, 32])
    CMASK = din("cmask", [128, NTB, 257])
    LMASK = din("lmask", [128, 2, 256])
    PATTERN = din("pattern", [128, 8])
    HALOIDX = din("haloidx", [128, 2], mybir.dt.uint32)
    F1_w13 = din("f1_w13", [9, 11, 128, KC, 256])
    F1_wdn = din("f1_wdn", [9, 2, 128, 11, 1024])
    F1_gate = din("f1_gate", [128, KC, 8])
    hx_in = nc.dram_tensor("hx_in", [256, D], BF16)
    hx_out = nc.dram_tensor("hx_out", [2048, D], BF16)
    ccin = nc.dram_tensor("ccin", [64, 256], F32)
    ccout = nc.dram_tensor("ccout", [512, 256], F32)

    dbg = None
    if DEBUG_TAP is not None:
        dbg = nc.dram_tensor("dbg", [NT, 128, D], F32, kind="ExternalOutput").ap()

    used = [0]

    def salloc(name, shape, dtype):
        t = stack.enter_context(nc.sbuf_tensor(name, list(shape), dtype))
        used[0] += int(np.prod(shape[1:])) * mybir.dt.size(dtype)
        return t

    xres_t = salloc("xres", [128, NT * D], F32)
    xres = xres_t.ap().rearrange("p (t d) -> p t d", t=NT)
    xT = salloc("xT", [128, KC, TPC], BF16).ap()
    wbuf = salloc("wbuf", [128, 2, KC, NSL], BF16).ap()
    stg_t = salloc("stg", [128, D], F32)
    stg = stg_t.ap()
    bias_bf = salloc("bias_bf", [1, 2, D], BF16).ap()
    stats = salloc("stats", [128, NT, 8], F32).ap()
    ssel = salloc("ssel", [128, NT, 8], F32).ap()
    small = salloc("small", [128, 64], F32).ap()
    ident_bf = salloc("ident_bf", [128, 128], BF16).ap()
    ones_bf = salloc("ones_bf", [128, 128], BF16).ap()
    ones_f = salloc("ones_f", [1, 128], F32).ap()
    ident_f = salloc("ident_f", [128, 128], F32).ap()
    gtile0 = salloc("gtile0", [128, KC, 8], F32).ap()
    gtile1 = salloc("gtile1", [128, KC, 8], F32).ap()
    ebias_t = salloc("ebias_t", [128, 8], F32).ap()
    iota_t = salloc("iota_t", [128, 8], F32).ap()
    rev_t = salloc("rev_t", [128, 8], F32).ap()
    decay_sb = salloc("decay_sb", [128, NT], F32).ap()

    arena_f32 = (212700 - used[0]) // 4
    arena_t = salloc("arena", [128, arena_f32], F32)
    ar = Arena(arena_t, arena_f32)

    psum = [stack.enter_context(nc.psum_tensor(f"ps{i}", [128, 512], F32)) for i in range(6)]
    psum_bf = [stack.enter_context(nc.psum_tensor(f"psbf{i}", [128, 1024], BF16)) for i in range(2)]
    pbank = [Res() for _ in range(6)]
    pbank_bf = [Res() for _ in range(2)]
    wpar = [Res(), Res()]
    bias_res = [Res(), Res()]
    stgreg = [Res() for _ in range(4)]

    kv_in = nc.dram_tensor("kv_in", [128, 32 * 129], BF16)
    kv_out = nc.dram_tensor("kv_out", [NCORES * 128, 32 * 129], BF16)
    xspill = nc.dram_tensor("xspill", [NT, 128, D], F32)

    p = Prog()

    def ln_stats(src_ap, it, col, sq_scratch=None):
        sq = stg if sq_scratch is None else sq_scratch
        m_raw = stats[:, it, col + 2:col + 3]
        s2_raw = stats[:, it, col + 3:col + 4]
        p.opm("dve", lambda E, a=src_ap, o=m_raw: E.tensor_reduce(o, a, AX.X, OP.add))
        t2 = p.opm("act", lambda E, a=src_ap, o=s2_raw, sq=sq: E.activation(
            sq[:, :], a, AF.Square, accum_out=o))
        mcol = stats[:, it, col:col + 1]
        rcol = stats[:, it, col + 1:col + 2]
        p.wait("dve", t2)
        p.op("dve", lambda E, i=m_raw, o=mcol: E.tensor_scalar(o, i, 1.0 / D, 0.0, OP.mult, OP.bypass))
        p.op("dve", lambda E, m=mcol: E.tensor_tensor(small[:, 0:1], m, m, OP.mult))
        p.op("dve", lambda E, i=s2_raw: E.tensor_scalar(small[:, 1:2], i, 1.0 / D, small[:, 0:1],
                                                        OP.mult, OP.subtract))
        tv = p.opm("dve", lambda E: E.tensor_scalar(small[:, 1:2], small[:, 1:2], 1e-5, 0.0,
                                                    OP.add, OP.bypass))
        p.wait("act", tv)
        ts = p.opm("act", lambda E: E.activation(small[:, 2:3], small[:, 1:2], AF.Sqrt))
        p.wait("dve", ts)
        return p.opm("dve", lambda E, o=rcol: E.reciprocal(o, small[:, 2:3]))

    def transpose_tokmajor_to(get_src_tile, dstT, ntiles=NT, dst_tok0=0):
        p.wait("pe", p.cur("dve"))
        p.wait("pe", p.cur("act"))
        for it in range(ntiles):
            srct, tok = get_src_tile(it)
            p.wait("pe", tok)
            for g in range(2):
                pb = psum_bf[g]
                br = pbank_bf[g]
                p.wait_all("pe", br.r)
                for j in range(8):
                    k = g * 8 + j
                    p.op("pe", lambda E, srct=srct, k=k, j=j, pb=pb: E.transpose(
                        pb[:, j * 128:(j + 1) * 128], srct[:, k * 128:(k + 1) * 128],
                        ident_bf[:, :]))
                tq = p.milestone("pe")
                br.note_write(tq)
                p.wait("dve", tq)
                tc = p.opm("dve", lambda E, it=it, g=g, pb=pb, dstT=dstT, dst_tok0=dst_tok0: E.tensor_copy(
                    dstT[:, g * 8:(g + 1) * 8, dst_tok0 + it * 128:dst_tok0 + (it + 1) * 128],
                    pb[:, :].rearrange("p (j n) -> p j n", j=8)))
                br.note_read(tc)

    def emit_ln_to_xT(src_f32, scol, norm_buf, single_tile=False):
        p.wait("act", p.cur("dve"))
        p.wait("dve", p.cur("act"))
        for it in range(NT):
            ln_stats(src_f32[:, it, :], it, scol)

        def srctile(it):
            mcol = stats[:, it, scol:scol + 1]
            rcol = stats[:, it, scol + 1:scol + 2]
            dst = norm_buf if single_tile else norm_buf[:, it, :]
            if single_tile:
                p.wait("dve", p.cur("pe"))
            tn = p.opm("dve", lambda E, it=it, m=mcol, r=rcol, dst=dst: E.tensor_scalar(
                dst, src_f32[:, it, :], m, r, OP.subtract, OP.mult))
            return dst, tn
        transpose_tokmajor_to(srctile, xT)

    def load_bias(bias_dram_row, slot, width=D):
        p.wait_all("pool", bias_res[slot].r)
        t = p.dma("pool", bias_bf[:, slot, 0:width], bias_dram_row, stream=f"bias{slot}")
        bias_res[slot].note_write(t)
        return t

    def emit_proj(wblob, bias_slot, evac, dout=D, src=None, tok0=0, ntiles=NT):
        if src is None:
            src = xT
        p.wait("pe", p.cur("dve"))
        nsl_count = dout // NSL
        for ns in range(nsl_count):
            par = ns % 2
            wb = wbuf[:, par]
            p.wait_all("pool", wpar[par].r)
            tw = p.dma("pool", wb, wblob[ns, :, :, :], stream=f"wg{par}")
            wpar[par].note_write(tw)
            p.wait("pe", tw)
            if bias_slot is not None:
                p.wait("pe", bias_res[bias_slot].w)
            for it in range(ntiles):
                bank = psum[it % 4]
                bres = pbank[it % 4]
                p.wait_all("pe", bres.r)
                for k in range(KC):
                    p.op("pe", lambda E, it=it, k=k, bank=bank, wb=wb, src=src, tok0=tok0: E.matmul(
                        bank[:, 0:NSL], src[:, k, tok0 + it * 128:tok0 + (it + 1) * 128],
                        wb[:, k, :], start=(k == 0), stop=(bias_slot is None and k == KC - 1)))
                if bias_slot is not None:
                    tmm = p.opm("pe", lambda E, bank=bank, ns=ns, bs=bias_slot: E.matmul(
                        bank[:, 0:NSL], ones_bf[0:1, :],
                        bias_bf[0:1, bs, ns * NSL:(ns + 1) * NSL], start=False, stop=True))
                    bias_res[bias_slot].note_read(tmm)
                else:
                    tmm = p.milestone("pe")
                bres.note_write(tmm)
                wpar[par].note_read(tmm)
                tev = evac(it, ns, bank, tmm)
                bres.note_read(tev)

    t_x = [p.dma("sp", xres[:, it, :], X[it], stream="xio") for it in range(NT)]
    t_id = p.dma("sp", ident_f, IDENT, stream="misc")
    t_dec = p.dma("sp", decay_sb[:, :], DECAY, stream="misc")
    t_ones = p.opm("pool", lambda E: E.memset(ones_bf[:, :], 1.0))
    t_onesf = p.opm("pool", lambda E: E.memset(ones_f[:, :], 1.0))
    t_g0 = p.dma("sp", gtile0, F0_gate, stream="misc")
    t_g1 = p.dma("sp", gtile1, F1_gate, stream="misc")
    t_eb = p.dma("sp", small[0:1, 32:40], F0_eb, stream="misc")
    t_io = p.dma("sp", small[0:1, 40:48], IOTA8, stream="misc")
    p.wait("dve", t_id)
    tid_bf = p.opm("dve", lambda E: E.tensor_copy(ident_bf[:, :], ident_f))
    p.wait("pe", t_eb)
    p.wait("pe", t_io)
    p.wait("pe", t_onesf)
    tmb = p.opm("pe", lambda E: E.matmul(psum[4][:, 0:8], ones_f[0:1, :],
                                         small[0:1, 32:40], start=True, stop=True))
    pbank[4].note_write(tmb)
    p.wait("dve", tmb)
    te1 = p.opm("dve", lambda E: E.tensor_copy(ebias_t, psum[4][:, 0:8]))
    pbank[4].note_read(te1)
    tmb2 = p.opm("pe", lambda E: E.matmul(psum[5][:, 0:8], ones_f[0:1, :],
                                          small[0:1, 40:48], start=True, stop=True))
    pbank[5].note_write(tmb2)
    p.wait("dve", tmb2)
    te2 = p.opm("dve", lambda E: E.tensor_copy(iota_t, psum[5][:, 0:8]))
    pbank[5].note_read(te2)
    p.op("dve", lambda E: E.tensor_scalar(rev_t, iota_t, -1.0, 8.0, OP.mult, OP.add))

    ar.push()
    bufA = ar.take_bf16(NT, D)
    bufB = ar.take_bf16(NT, D)

    tb0 = load_bias(B_l0[1], 0)
    tb1 = load_bias(B_l0[2], 1)

    p.wait_all("dve", t_x)
    p.wait_all("act", t_x)
    p.wait("act", tid_bf)
    p.wait("pe", tid_bf)
    emit_ln_to_xT(xres, 0, bufA)

    def evac_relu_decay(dstbuf):
        def f(it, ns, bank, tmm):
            p.wait("dve", tmm)
            p.wait("dve", t_dec)
            return p.opm("dve", lambda E, it=it, ns=ns, bank=bank: E.tensor_scalar(
                dstbuf[:, it, ns * NSL:(ns + 1) * NSL], bank[:, 0:NSL],
                0.0, decay_sb[:, it:it + 1], OP.max, OP.mult))
        return f

    def evac_plain_bf(dstbuf):
        def f(it, ns, bank, tmm):
            p.wait("act", tmm)
            return p.opm("act", lambda E, it=it, ns=ns, bank=bank: E.activation(
                dstbuf[:, it, ns * NSL:(ns + 1) * NSL], bank[:, 0:NSL], AF.Copy))
        return f

    p.wait("pe", t_ones)
    emit_proj(W_l0["wk"], 0, evac_relu_decay(bufB))
    tb0 = load_bias(B_l0[0], 0)
    emit_proj(W_l0["wv"], 1, evac_plain_bf(bufA))

    p.wait("pe", p.cur("dve"))
    p.wait("pe", p.cur("act"))
    for b in range(B):
        for h in range(16):
            pair = b * 16 + h
            bank = psum[4 + (pair % 2)]
            bres = pbank[4 + (pair % 2)]
            p.wait_all("pe", bres.r)
            for j in range(NTB):
                it = b * NTB + j
                p.op("pe", lambda E, it=it, h=h, bank=bank, j=j: E.matmul(
                    bank[:, 0:128], bufB[:, it, h * 128:(h + 1) * 128],
                    bufA[:, it, h * 128:(h + 1) * 128], start=(j == 0), stop=False))
                p.op("pe", lambda E, it=it, h=h, bank=bank, j=j: E.matmul(
                    bank[:, 128:129], bufB[:, it, h * 128:(h + 1) * 128],
                    ones_bf[:, 0:1], start=False, stop=(j == NTB - 1)))
            tm = p.milestone("pe")
            bres.note_write(tm)
            p.wait("dve", tm)
            par2 = pair % 2
            stg_res2 = stgreg[par2]
            p.wait_all("dve", stg_res2.r)
            stg_kv = stg[:, 1100 + par2 * 80:1100 + par2 * 80 + 65].bitcast(BF16)[:, 0:129]
            tc = p.opm("dve", lambda E, bank=bank, o=stg_kv: E.tensor_copy(o, bank[:, 0:129]))
            bres.note_read(tc)
            stg_res2.note_write(tc)
            p.wait("pool", tc)
            td = p.dma("pool", kv_in.ap()[:, pair * 129:(pair + 1) * 129],
                       stg_kv, stream="kvb")
            stg_res2.note_read(td)
    p.wait("pool", td)
    if _NO_CC:
        for r in range(NCORES):
            p.op("pool", lambda E, r=r: E.dma_start(
                out=kv_out.ap()[r * 128:(r + 1) * 128, :], in_=kv_in.ap()), is_dma=True)
            p.milestone("pool", kind="cc", sem="cc0", inc=16)
        n_cc = 128
    else:
        p.op("pool", lambda E: E.collective_compute(
            "AllGather", OP.bypass, replica_groups=[list(range(NCORES))],
            ins=[kv_in.ap().opt()], outs=[kv_out.ap().opt()]), is_dma=True)
        p.milestone("pool", kind="cc", sem="cc0", inc=1)
        n_cc = 1

    emit_proj(W_l0["wq"], 0, evac_relu_decay(bufB))
    qT3 = bufA.rearrange("p t d -> p (t d)").rearrange("p (t h n) -> p t h n", t=NT, h=16)

    kvall = wbuf.rearrange("p a k n -> p (a k n)")[:, 0:32 * 129].rearrange(
        "p (q n) -> p q n", q=32)
    p.raw_wait("pool", "cc0", n_cc)
    p.wait_all("pool", [wpar[0].r[-1] if wpar[0].r else None,
                        wpar[1].r[-1] if wpar[1].r else None])
    p.wait("pool", p.cur("dve"))
    p.wait("pool", p.cur("act"))
    for qtr in range(4):
        cw = 1032
        dst = stg[:, 0:cw]
        for r in range(NCORES):
            t_acc = p.dma("pool", dst,
                          kv_out.ap()[r * 128:(r + 1) * 128,
                                      qtr * cw:(qtr + 1) * cw],
                          stream="kvb",
                          **({} if r == 0 else {"accum_op": OP.add}))
            p.wait("pool", t_acc)
        p.wait("dve", t_acc)
        tkc = p.opm("dve", lambda E, qtr=qtr, dst=dst, cw=cw: E.tensor_copy(
            kvall.rearrange("p q n -> p (q n)")[:, qtr * cw:(qtr + 1) * cw], dst))
        p.wait("pool", tkc)
    t_kv = p.cur("dve")
    p.wait("pe", t_kv)

    def qsrc(it):
        return bufB[:, it, :], None
    p.wait("pe", p.cur("dve"))
    for it in range(NT):
        for g in range(2):
            pb = psum_bf[g]
            br = pbank_bf[g]
            p.wait_all("pe", br.r)
            for j in range(8):
                h = g * 8 + j
                p.op("pe", lambda E, it=it, h=h, j=j, pb=pb: E.transpose(
                    pb[:, j * 128:(j + 1) * 128], bufB[:, it, h * 128:(h + 1) * 128],
                    ident_bf[:, :]))
            tq = p.milestone("pe")
            br.note_write(tq)
            p.wait("dve", tq)
            tc = p.opm("dve", lambda E, it=it, g=g, pb=pb: E.tensor_copy(
                qT3[:, it, g * 8:(g + 1) * 8, :], pb[:, :].rearrange("p (j n) -> p j n", j=8)))
            br.note_read(tc)


    p.wait("pe", p.cur("dve"))
    kv_read = []
    for b in range(B):
        for h in range(16):
            pair = b * 16 + h
            for j in range(NTB):
                it = b * NTB + j
                bi = (pair * NTB + j) % 4
                bank, bres = psum[bi], pbank[bi]
                p.wait_all("pe", bres.r)
                tm = p.opm("pe", lambda E, it=it, h=h, pair=pair, bank=bank: E.matmul(
                    bank[:, 0:129], qT3[:, it, h, :], kvall[:, pair, :],
                    start=True, stop=True))
                bres.note_write(tm)
                p.wait("dve", tm)
                c0, c1 = bi * 2, bi * 2 + 1
                p.op("dve", lambda E, bank=bank, c0=c0: E.tensor_scalar(
                    small[:, 8 + c0:9 + c0], bank[:, 128:129], EPS0, 0.0, OP.add, OP.bypass))
                p.op("dve", lambda E, c0=c0, c1=c1: E.reciprocal(
                    small[:, 8 + c1:9 + c1], small[:, 8 + c0:9 + c0]))
                tz = p.opm("dve", lambda E, it=it, h=h, bank=bank, c1=c1: E.tensor_scalar(
                    bufB[:, it, h * 128:(h + 1) * 128], bank[:, 0:128],
                    small[:, 8 + c1:9 + c1], 0.0, OP.mult, OP.bypass))
                bres.note_read(tz)
                kv_read.append(tz)

    t_pe_kv = p.cur("pe")
    wpar[0].note_read(t_pe_kv)
    wpar[1].note_read(t_pe_kv)
    p.wait_all("pool", kv_read[-4:])
    transpose_tokmajor_to(lambda it: (bufB[:, it, :], None), xT)
    tb1 = load_bias(B_l0[3], 1)

    p.barrier()
    ar.pop()
    ar.push()
    acc = ar.take_f32(NT, D)

    def evac_to_acc(it, ns, bank, tmm):
        p.wait("act", tmm)
        return p.opm("act", lambda E, it=it, ns=ns, bank=bank: E.activation(
            acc[:, it, ns * NSL:(ns + 1) * NSL], bank[:, 0:NSL], AF.Copy))

    emit_proj(W_l0["wo"], 1, evac_to_acc)

    p.wait("dve", p.cur("act"))
    for it in range(NT):
        ln_stats(acc[:, it, :], it, 4)
        mcol = stats[:, it, 4:5]
        rcol = stats[:, it, 5:6]
        tn = p.opm("dve", lambda E, it=it, m=mcol, r=rcol: E.tensor_scalar(
            stg[:, :], acc[:, it, :], m, r, OP.subtract, OP.mult))
        ta = p.opm("dve", lambda E, it=it: E.tensor_tensor(
            xres[:, it, :], xres[:, it, :], stg[:, :], OP.add))
        p.wait("act", ta)
    ar.pop()


    def emit_moe_ffn(w13, wdn, gate_sb, nhc, gate_af, scale_input,
                     seed_xln, router_softmax):
        HID = nhc * 128
        p.barrier()
        p.barrier(dst_engs=("sp",), src_engs=("pe", "act", "dve"))
        ar.push()
        lnstage = ar.take_bf16(D)
        emit_ln_to_xT(xres, 0, lnstage, single_tile=True)
        gtile = gate_sb
        rsc = ar.take_f32(4, 8)
        sq2 = ar.take_f32(D)
        xT32 = sq2.rearrange("p (k n) -> p k n", k=KC)
        acc = xres

        for it in range(NT):
            mcol = stats[:, it, 0:1]
            rcol = stats[:, it, 1:2]
            txn = p.opm("dve", lambda E, it=it, m=mcol, r=rcol: E.tensor_scalar(
                stg[:, :], xres[:, it, :], m, r, OP.subtract, OP.mult))
            p.wait("pe", txn)
            for rd in range(4):
                bank = psum[4 + rd % 2]
                bres = pbank[4 + rd % 2]
                p.wait_all("pe", bres.r)
                for j in range(4):
                    k = rd * 4 + j
                    p.op("pe", lambda E, k=k, j=j, bank=bank: E.transpose(
                        bank[:, j * 128:(j + 1) * 128], stg[:, k * 128:(k + 1) * 128],
                        ident_f))
                tq = p.milestone("pe")
                bres.note_write(tq)
                p.wait("dve", tq)
                tc = p.opm("dve", lambda E, rd=rd, bank=bank: E.tensor_copy(
                    xT32[:, rd * 4:(rd + 1) * 4, :],
                    bank[:, :].rearrange("p (j n) -> p j n", j=4)))
                bres.note_read(tc)
            p.wait("pe", p.cur("dve"))
            p.wait_all("pe", pbank[4].r)
            for k in range(KC):
                p.op("pe", lambda E, k=k: E.matmul(
                    psum[4][:, 0:8], xT32[:, k, :], gtile[:, k, :],
                    start=(k == 0), stop=(k == KC - 1)))
            tsc = p.milestone("pe")
            pbank[4].note_write(tsc)
            scores = rsc[:, 0, :]
            key = rsc[:, 1, :]
            tmp = rsc[:, 2, :]
            msel = rsc[:, 3, :]
            if router_softmax:
                p.wait("dve", tsc)
                p.op("dve", lambda E: E.tensor_reduce(small[:, 0:1], psum[4][:, 0:8],
                                                      AX.X, OP.max))
                p.op("dve", lambda E: E.tensor_scalar(small[:, 1:2], small[:, 0:1],
                                                      -1.0, 0.0, OP.mult, OP.bypass))
                tmx = p.cur("dve")
                p.wait("act", tmx)
                tex = p.opm("act", lambda E: E.activation(
                    key, psum[4][:, 0:8], AF.Exp, bias=small[:, 1:2],
                    accum_out=small[:, 2:3]))
                p.wait("dve", tex)
                p.op("dve", lambda E: E.reciprocal(small[:, 3:4], small[:, 2:3]))
                tsf = p.opm("dve", lambda E: E.tensor_scalar(
                    scores, key, small[:, 3:4], 0.0, OP.mult, OP.bypass))
                pbank[4].note_read(tex)
                p.op("dve", lambda E: E.tensor_copy(key, scores))
            else:
                p.wait("act", tsc)
                tsg = p.opm("act", lambda E: E.activation(scores, psum[4][:, 0:8], AF.Sigmoid))
                pbank[4].note_read(tsg)
                p.wait("dve", tsg)
                p.op("dve", lambda E: E.tensor_tensor(key, scores, ebias_t, OP.add))
            p.op("dve", lambda E: E.tensor_reduce(small[:, 0:1], key, AX.X, OP.max))
            p.op("dve", lambda E: E.tensor_scalar(tmp, key, small[:, 0:1], 0.0,
                                                  OP.is_ge, OP.bypass))
            p.op("dve", lambda E: E.tensor_tensor(tmp, tmp, rev_t, OP.mult))
            p.op("dve", lambda E: E.tensor_reduce(small[:, 1:2], tmp, AX.X, OP.max))
            p.op("dve", lambda E: E.tensor_scalar(small[:, 1:2], small[:, 1:2], -1.0, 8.0,
                                                  OP.mult, OP.add))
            p.op("dve", lambda E: E.tensor_scalar(msel, iota_t, small[:, 1:2], 0.0,
                                                  OP.is_equal, OP.bypass))
            p.op("dve", lambda E: E.tensor_scalar(tmp, msel, -1e9, 0.0, OP.mult, OP.bypass))
            p.op("dve", lambda E: E.tensor_tensor(tmp, key, tmp, OP.add))
            p.op("dve", lambda E: E.tensor_reduce(small[:, 0:1], tmp, AX.X, OP.max))
            p.op("dve", lambda E: E.tensor_scalar(tmp, tmp, small[:, 0:1], 0.0,
                                                  OP.is_ge, OP.bypass))
            p.op("dve", lambda E: E.tensor_tensor(tmp, tmp, rev_t, OP.mult))
            p.op("dve", lambda E: E.tensor_reduce(small[:, 1:2], tmp, AX.X, OP.max))
            p.op("dve", lambda E: E.tensor_scalar(small[:, 1:2], small[:, 1:2], -1.0, 8.0,
                                                  OP.mult, OP.add))
            p.op("dve", lambda E: E.tensor_scalar(tmp, iota_t, small[:, 1:2], 0.0,
                                                  OP.is_equal, OP.bypass))
            p.op("dve", lambda E: E.tensor_tensor(msel, msel, tmp, OP.add))
            p.op("dve", lambda E, it=it: E.tensor_tensor(ssel[:, it, :], scores, msel,
                                                         OP.mult))

        t_sp = []
        p.wait("sp", p.cur("dve"))
        for it in range(NT):
            t_sp.append(p.dma("sp", xspill[it], xres[:, it, :], stream="spill"))
        if seed_xln:
            for it in range(NT):
                p.wait("dve", t_sp[it])
                mcol = stats[:, it, 0:1]
                rcol = stats[:, it, 1:2]
                p.opm("dve", lambda E, it=it, m=mcol, r=rcol: E.tensor_scalar(
                    acc[:, it, :], xres[:, it, :], m, r, OP.subtract, OP.mult))

        h_fm = ar.take_bf16(nhc, 1024)
        wdn_sb = ar.take_bf16(nhc, 1024)
        sbc = ar.take_f32(1024)
        stgA = ar.take_f32(1024)
        stgB = ar.take_f32(1024)
        srow = ar.take_f32(1024)
        h_res = Res()
        wdn_res = Res()
        for m in range(9):
            e = m - 1
            scaled = m > 0
            if scaled:
                p.wait("pe", p.cur("dve"))
                for g in range(2):
                    bank = psum[4 + g]
                    bres = pbank[4 + g]
                    p.wait_all("pe", bres.r)
                    for j in range(4):
                        it = g * 4 + j
                        sap = ssel[:, it, e:e + 1]
                        p.op("pe", lambda E, j=j, bank=bank, sap=sap: E.transpose(
                            bank[0:1, j * 128:(j + 1) * 128], sap, ident_f))
                    tq = p.milestone("pe")
                    bres.note_write(tq)
                    p.wait("dve", tq)
                    tc = p.opm("dve", lambda E, g=g, bank=bank: E.tensor_copy(
                        srow[0:1, g * 512:(g + 1) * 512], bank[0:1, 0:512]))
                    bres.note_read(tc)
                p.wait("pe", p.cur("dve"))
                for g in range(2):
                    bank = psum[4 + g]
                    bres = pbank[4 + g]
                    p.wait_all("pe", bres.r)
                    tb = p.opm("pe", lambda E, g=g, bank=bank: E.matmul(
                        bank[:, :], ones_f[0:1, :], srow[0:1, g * 512:(g + 1) * 512],
                        start=True, stop=True))
                    bres.note_write(tb)
                    p.wait("dve", tb)
                    tc = p.opm("dve", lambda E, g=g, bank=bank: E.tensor_copy(
                        sbc[:, g * 512:(g + 1) * 512], bank[:, :]))
                    bres.note_read(tc)
            p.wait("pe", p.cur("dve"))
            for hc in range(nhc):
                par = hc % 2
                wb = wbuf[:, par]
                p.wait_all("pool", wpar[par].r)
                tw = p.dma("pool", wb[:, :, :], w13[m, hc], stream=f"wg{par}")
                wpar[par].note_write(tw)
                p.wait("pe", tw)
                for half in range(2):
                    bankG, brG = psum[half], pbank[half]
                    bankU, brU = psum[2 + half], pbank[2 + half]
                    p.wait_all("pe", brG.r)
                    p.wait_all("pe", brU.r)
                    for k in range(KC):
                        p.op("pe", lambda E, k=k, half=half, bankG=bankG, wb=wb: E.matmul(
                            bankG[:, :], wb[:, k, 0:128], xT[:, k, half * 512:(half + 1) * 512],
                            start=(k == 0), stop=(k == KC - 1)))
                    for k in range(KC):
                        p.op("pe", lambda E, k=k, half=half, bankU=bankU, wb=wb: E.matmul(
                            bankU[:, :], wb[:, k, 128:256], xT[:, k, half * 512:(half + 1) * 512],
                            start=(k == 0), stop=(k == KC - 1)))
                    tgu = p.milestone("pe")
                    brG.note_write(tgu)
                    brU.note_write(tgu)
                    wpar[par].note_read(tgu)
                    hs = half * 512
                    hdst = h_fm[:, hc, hs:hs + 512]
                    p.wait("dve", tgu)
                    p.wait("act", tgu)
                    p.wait_all("dve", h_res.r)
                    if scale_input and scaled:
                        p.op("dve", lambda E, bankG=bankG, hs=hs: E.tensor_tensor(
                            stgA[:, hs:hs + 512], bankG[:, :], sbc[:, hs:hs + 512], OP.mult))
                        tga = p.cur("dve")
                        p.wait("act", tga)
                        tact = p.opm("act", lambda E, hs=hs: E.activation(
                            stgA[:, hs:hs + 512], stgA[:, hs:hs + 512], gate_af))
                        p.wait("dve", tact)
                        p.op("dve", lambda E, bankU=bankU, hs=hs: E.tensor_tensor(
                            stgB[:, hs:hs + 512], bankU[:, :], sbc[:, hs:hs + 512], OP.mult))
                        th = p.opm("dve", lambda E, hdst=hdst, hs=hs: E.tensor_tensor(
                            hdst, stgA[:, hs:hs + 512], stgB[:, hs:hs + 512], OP.mult))
                    else:
                        tact = p.opm("act", lambda E, bankG=bankG, hs=hs: E.activation(
                            stgA[:, hs:hs + 512], bankG[:, :], gate_af))
                        p.wait("dve", tact)
                        if scaled:
                            p.op("dve", lambda E, bankU=bankU, hs=hs: E.tensor_tensor(
                                stgB[:, hs:hs + 512], stgA[:, hs:hs + 512], bankU[:, :],
                                OP.mult))
                            th = p.opm("dve", lambda E, hdst=hdst, hs=hs: E.tensor_tensor(
                                hdst, stgB[:, hs:hs + 512], sbc[:, hs:hs + 512], OP.mult))
                        else:
                            th = p.opm("dve", lambda E, hdst=hdst, bankU=bankU, hs=hs: E.tensor_tensor(
                                hdst, stgA[:, hs:hs + 512], bankU[:, :], OP.mult))
                    brG.note_read(th)
                    brU.note_read(th)
                    h_res.note_write(th)
            p.wait("pe", p.cur("dve"))
            for oh in range(2):
                p.wait_all("pool", wdn_res.r)
                twd = p.dma("pool", wdn_sb, wdn[m, oh], stream="w2")
                wdn_res.note_write(twd)
                p.wait("pe", twd)
                for it in range(NT):
                    for ns2 in range(2):
                        bi = (it % 2) * 2 + ns2
                        bank, bres = psum[bi], pbank[bi]
                        p.wait_all("pe", bres.r)
                        for hc in range(nhc):
                            p.op("pe", lambda E, hc=hc, it=it, ns2=ns2, bank=bank: E.matmul(
                                bank[:, :], h_fm[:, hc, it * 128:(it + 1) * 128],
                                wdn_sb[:, hc, ns2 * 512:(ns2 + 1) * 512],
                                start=(hc == 0), stop=(hc == nhc - 1)))
                        tdn = p.milestone("pe")
                        bres.note_write(tdn)
                        wdn_res.note_read(tdn)
                        h_res.note_read(tdn)
                        off = oh * 1024 + ns2 * 512
                        if m == 0 and not seed_xln:
                            p.wait("act", tdn)
                            if not seed_xln:
                                p.wait_all("act", [t_sp[it]])
                            tev = p.opm("act", lambda E, it=it, off=off, bank=bank: E.activation(
                                acc[:, it, off:off + 512], bank[:, :], AF.Copy))
                            bres.note_read(tev)
                            p.wait("dve", tev)
                        else:
                            p.wait("dve", tdn)
                            p.wait_all("dve", [t_sp[it]])
                            tev = p.opm("dve", lambda E, it=it, off=off, bank=bank: E.tensor_tensor(
                                acc[:, it, off:off + 512], acc[:, it, off:off + 512],
                                bank[:, :], OP.add))
                            bres.note_read(tev)

        p.barrier()
        t_rl_prev = None
        for it in range(NT):
            if t_rl_prev is not None:
                p.wait("sp", t_rl_prev)
                p.wait("act", t_rl_prev)
            p.wait("sp", p.cur("dve"))
            t_rl = p.dma("sp", stg[:, :], xspill[it], stream="spill")
            ln_stats(acc[:, it, :], it, 4, sq_scratch=sq2)
            mcol = stats[:, it, 4:5]
            rcol = stats[:, it, 5:6]
            p.opm("dve", lambda E, it=it, m=mcol, r=rcol: E.tensor_scalar(
                sq2[:, :], acc[:, it, :], m, r, OP.subtract, OP.mult))
            p.wait("dve", t_rl)
            tfin = p.opm("dve", lambda E, it=it: E.tensor_tensor(
                acc[:, it, :], sq2[:, :], stg[:, :], OP.add))
            t_rl_prev = tfin
        ar.pop()


    def emit_l1_attn():
        from concourse.bass import IndirectOffsetOnAxis
        SCALE = 1.0 / np.sqrt(128.0)
        p.barrier()
        p.barrier(dst_engs=("sp",), src_engs=("pe", "act", "dve"))
        ar.push()
        cosq = ar.take_f32(NTB, 32)
        sinq = ar.take_f32(NTB, 32)
        cosh = ar.take_f32(32)
        sinh = ar.take_f32(32)
        cosck = ar.take_f32(32)
        sinck = ar.take_f32(32)
        qnb = ar.take_f32(128)
        knb = ar.take_f32(128)
        cmask = ar.take_bf16(NTB, 258)
        lmask = ar.take_bf16(2, 256)
        pattern = ar.take_f32(8)
        haloidx = ar.take_f32(2).bitcast(mybir.dt.uint32)
        wc_bf = ar.take_bf16(KC)
        sinkT_bf = ar.take_bf16(16)
        sinkv_row = ar.take_bf16(D)
        onesrow = ar.take_bf16(256)

        tl = []
        tl.append(p.dma("sp", cosq, COSQ, stream="l1c"))
        tl.append(p.dma("sp", sinq, SINQ, stream="l1c"))
        tl.append(p.dma("sp", cosh, COSH, stream="l1c"))
        tl.append(p.dma("sp", sinh, SINH, stream="l1c"))
        tl.append(p.dma("sp", cosck[0:32, 0:32], COSCK, stream="l1c"))
        tl.append(p.dma("sp", sinck[0:32, 0:32], SINCK, stream="l1c"))
        tl.append(p.dma("sp", pattern, PATTERN, stream="l1c"))
        tl.append(p.dma("sp", haloidx, HALOIDX, stream="l1c"))
        tqn = p.dma("sp", stg[0:1, 0:128], L1_qn, stream="l1c")
        tkn = p.dma("sp", stg[0:1, 128:256], L1_kn, stream="l1c")
        bc_bf = small[0:1, 48:49].bitcast(BF16)[0:1, 0:1]
        tw = []
        tw.append(p.dma("pool", small[0:1, 48:49].bitcast(BF16)[0:1, 0:1], L1_bc, stream="l1w"))
        tw.append(p.dma("pool", cmask[:, :, 0:257], CMASK, stream="l1w"))
        tw.append(p.dma("pool", lmask, LMASK, stream="l1w"))
        tw.append(p.dma("pool", wc_bf, L1_wc.rearrange("p k u -> p (k u)"), stream="l1w"))
        tw.append(p.dma("pool", sinkT_bf, L1_sinkT, stream="l1w"))
        tw.append(p.dma("pool", sinkv_row[0:1, :], L1_sinkv, stream="l1w"))
        tor = p.opm("pool", lambda E: E.memset(onesrow[0:1, :], 1.0))
        p.wait_all("pe", tl)
        p.wait_all("pe", tw)
        p.wait("pe", tor)
        p.wait("pe", tqn)
        p.wait("pe", tkn)
        p.wait_all("pe", pbank[4].r)
        tb1 = p.opm("pe", lambda E: E.matmul(psum[4][:, 0:128], ones_f[0:1, :],
                                             stg[0:1, 0:128], start=True, stop=True))
        pbank[4].note_write(tb1)
        p.wait("dve", tb1)
        tc1 = p.opm("dve", lambda E: E.tensor_copy(qnb, psum[4][:, 0:128]))
        pbank[4].note_read(tc1)
        p.wait_all("pe", pbank[5].r)
        tb2 = p.opm("pe", lambda E: E.matmul(psum[5][:, 0:128], ones_f[0:1, :],
                                             stg[0:1, 128:256], start=True, stop=True))
        pbank[5].note_write(tb2)
        p.wait("dve", tb2)
        tc2 = p.opm("dve", lambda E: E.tensor_copy(knb, psum[5][:, 0:128]))
        pbank[5].note_read(tc2)

        xT_halo = ar.take_bf16(B, KC, 128)

        ar.push()
        lnst = ar.take_bf16(D)
        halo_tm = ar.take_bf16(B, D)
        emit_ln_to_xT(xres, 0, lnst, single_tile=True)
        tdd = None
        for b in range(B):
            it = b * NTB + (NTB - 1)
            mcol = stats[:, it, 0:1]
            rcol = stats[:, it, 1:2]
            if tdd is not None:
                p.wait("dve", tdd)
            ttl = p.opm("dve", lambda E, it=it, m=mcol, r=rcol: E.tensor_scalar(
                lnst, xres[:, it, :], m, r, OP.subtract, OP.mult))
            p.wait("pool", ttl)
            tdd = p.dma("pool", hx_in.ap()[b * 128:(b + 1) * 128, :], lnst, stream="hx")
        p.wait("pool", tdd)
        p.op("pool", lambda E: E.collective_compute(
            "AllGather", OP.bypass, replica_groups=[list(range(NCORES))],
            ins=[hx_in.ap().opt()], outs=[hx_out.ap().opt()]), is_dma=True)
        p.milestone("pool", kind="cc", sem="cc1", inc=1)

        ccpack = ar.take_f32(256)
        tokstage = ar.take_bf16(D)
        entT = ar.take_bf16(KC, 32)
        wcol = ar.take_f32(4)
        wmat_bf = small[:, 52:56].bitcast(BF16)
        p.wait_all("pool", wpar[0].r)
        twkv = p.dma("pool", wbuf[:, 0], L1_wkv, stream="wg0")
        wpar[0].note_write(twkv)
        tkv_b = load_bias(B_l1_kv, 1, width=256)
        for b in range(B):
            p.wait("pe", p.cur("dve"))
            p.wait_all("pe", pbank[4].r)
            for k in range(KC):
                p.op("pe", lambda E, k=k, b=b: E.matmul(
                    psum[4][0:1, 0:512], wc_bf[:, k:k + 1], xT[:, k, b * 512:(b + 1) * 512],
                    start=(k == 0), stop=False))
            p.op("pe", lambda E: E.matmul(
                psum[4][0:1, 0:256], bc_bf, onesrow[0:1, :], start=False, stop=False))
            tsc = p.opm("pe", lambda E: E.matmul(
                psum[4][0:1, 256:512], bc_bf, onesrow[0:1, :], start=False, stop=True))
            pbank[4].note_write(tsc)
            p.wait("dve", tsc)
            r3 = stg[0:1, 0:512].rearrange("o (c r) -> o c r", c=32)
            tcp = p.opm("dve", lambda E: E.tensor_copy(stg[0:1, 0:512], psum[4][0:1, 0:512]))
            pbank[4].note_read(tcp)
            p.op("dve", lambda E, r3=r3: E.tensor_reduce(
                stg[0:1, 512:544].rearrange("o (c u) -> o c u", c=32), r3, AX.X, OP.max))
            p.op("dve", lambda E, r3=r3: E.tensor_tensor(
                r3, r3, stg[0:1, 512:544].rearrange("o (c u) -> o c u", c=32).broadcast_to(
                    (1, 32, 16)), OP.subtract))
            tex = p.cur("dve")
            p.wait("act", tex)
            tex2 = p.opm("act", lambda E: E.activation(stg[0:1, 0:512], stg[0:1, 0:512], AF.Exp))
            p.wait("dve", tex2)
            p.op("dve", lambda E, r3=r3: E.tensor_reduce(
                stg[0:1, 512:544].rearrange("o (c u) -> o c u", c=32), r3, AX.X, OP.add))
            p.op("dve", lambda E: E.reciprocal(stg[0:1, 544:576], stg[0:1, 512:544]))
            tnw = p.opm("dve", lambda E, r3=r3: E.tensor_tensor(
                r3, r3, stg[0:1, 544:576].rearrange("o (c u) -> o c u", c=32).broadcast_to(
                    (1, 32, 16)), OP.mult))
            p.wait("pe", tnw)
            p.wait_all("pe", pbank[5].r)
            for jt in range(NTB):
                p.op("pe", lambda E, jt=jt: E.transpose(
                    psum[5][:, jt:jt + 1], stg[0:1, jt * 128:(jt + 1) * 128], ident_f[0:1, 0:1]))
            twc = p.milestone("pe")
            pbank[5].note_write(twc)
            p.wait("dve", twc)
            twc2 = p.opm("dve", lambda E: E.tensor_copy(wcol, psum[5][:, 0:4]))
            pbank[5].note_read(twc2)
            for jt in range(NTB):
                it = b * NTB + jt
                twm = p.opm("dve", lambda E, jt=jt: E.tensor_scalar(
                    wmat_bf, pattern, wcol[:, jt:jt + 1], 0.0, OP.mult, OP.bypass))
                mcol = stats[:, it, 0:1]
                rcol = stats[:, it, 1:2]
                ttk = p.opm("dve", lambda E, it=it, m=mcol, r=rcol: E.tensor_scalar(
                    tokstage, xres[:, it, :], m, r, OP.subtract, OP.mult))
                p.wait("pe", twm)
                p.wait("pe", ttk)
                for ns in range(4):
                    bank = psum[ns]
                    bres = pbank[ns]
                    p.wait_all("pe", bres.r)
                    tme = p.opm("pe", lambda E, ns=ns, bank=bank: E.matmul(
                        bank[0:8, 0:512], wmat_bf,
                        tokstage[:, ns * 512:(ns + 1) * 512], start=True, stop=True))
                    bres.note_write(tme)
                    p.wait("dve", tme)
                    tev = p.opm("dve", lambda E, ns=ns, bank=bank: E.tensor_copy(
                        stg[0:8, ns * 512:(ns + 1) * 512], bank[0:8, 0:512]))
                    bres.note_read(tev)
                p.wait("pe", p.cur("dve"))
                p.wait_all("pe", pbank[4].r)
                for k in range(KC):
                    p.op("pe", lambda E, k=k: E.transpose(
                        psum[4][:, k * 8:(k + 1) * 8], stg[0:8, k * 128:(k + 1) * 128],
                        ident_f[0:8, 0:8]))
                tet = p.milestone("pe")
                pbank[4].note_write(tet)
                p.wait("dve", tet)
                tet2 = p.opm("dve", lambda E, jt=jt: E.tensor_copy(
                    entT[:, :, jt * 8:(jt + 1) * 8],
                    psum[4][:, 0:128].rearrange("p (k e) -> p k e", k=KC)))
                pbank[4].note_read(tet2)
            p.wait("pe", p.cur("dve"))
            p.wait("pe", twkv)
            p.wait("pe", tkv_b)
            for half, boff in ((0, 0), (1, 128)):
                bank = psum[4]
                bres = pbank[4]
                p.wait_all("pe", bres.r)
                for k in range(KC):
                    p.op("pe", lambda E, k=k, boff=boff, bank=bank: E.matmul(
                        bank[0:32, 0:128], entT[:, k, :], wbuf[:, 0, k, boff:boff + 128],
                        start=(k == 0), stop=False))
                tbb = p.opm("pe", lambda E, boff=boff, bank=bank: E.matmul(
                    bank[0:32, 0:128], ones_bf[0:1, 0:32], bias_bf[0:1, 1, boff:boff + 128],
                    start=False, stop=True))
                bres.note_write(tbb)
                wpar[0].note_read(tbb)
                bias_res[1].note_read(tbb)
                p.wait("dve", tbb)
                if half == 0:
                    ckv = stg[0:32, 1024:1152]
                    tck = p.opm("dve", lambda E, bank=bank, ckv=ckv: E.tensor_copy(
                        ckv, bank[0:32, 0:128]))
                    bres.note_read(tck)
                    p.wait("act", tck)
                    tsq = p.opm("act", lambda E, ckv=ckv: E.activation(
                        stg[0:32, 1152:1280], ckv, AF.Square, accum_out=small[0:32, 60:61]))
                    p.wait("dve", tsq)
                    p.op("dve", lambda E: E.tensor_scalar(
                        small[0:32, 61:62], small[0:32, 60:61], 1.0 / 128, 1e-6, OP.mult, OP.add))
                    tvv = p.cur("dve")
                    p.wait("act", tvv)
                    tsr = p.opm("act", lambda E: E.activation(small[0:32, 62:63],
                                                              small[0:32, 61:62], AF.Sqrt))
                    p.wait("dve", tsr)
                    p.op("dve", lambda E: E.reciprocal(small[0:32, 63:64], small[0:32, 62:63]))
                    p.op("dve", lambda E, ckv=ckv: E.tensor_scalar(
                        ckv, ckv, small[0:32, 63:64], 0.0, OP.mult, OP.bypass))
                    p.op("dve", lambda E, ckv=ckv: E.tensor_tensor(ckv, ckv, knb[0:32, :], OP.mult))
                    x1 = ckv[:, 0:32]
                    x2 = ckv[:, 32:64]
                    p.op("dve", lambda E, x1=x1: E.tensor_tensor(
                        stg[0:32, 1280:1312], x1, cosck[0:32, 0:32], OP.mult))
                    p.op("dve", lambda E, x2=x2: E.tensor_tensor(
                        stg[0:32, 1312:1344], x2, sinck[0:32, 0:32], OP.mult))
                    p.op("dve", lambda E, x1=x1: E.tensor_tensor(
                        stg[0:32, 1344:1376], x1, sinck[0:32, 0:32], OP.mult))
                    p.op("dve", lambda E, x2=x2: E.tensor_tensor(
                        stg[0:32, 1376:1408], x2, cosck[0:32, 0:32], OP.mult))
                    p.op("dve", lambda E, x1=x1: E.tensor_tensor(
                        x1, stg[0:32, 1280:1312], stg[0:32, 1312:1344], OP.subtract))
                    p.op("dve", lambda E, x2=x2: E.tensor_tensor(
                        x2, stg[0:32, 1344:1376], stg[0:32, 1376:1408], OP.add))
                    p.opm("dve", lambda E, b=b, ckv=ckv: E.tensor_copy(
                        ccpack[b * 32:(b + 1) * 32, 0:128], ckv))
                else:
                    tpk = p.opm("dve", lambda E, b=b, bank=bank: E.tensor_copy(
                        ccpack[b * 32:(b + 1) * 32, 128:256], bank[0:32, 0:128]))
                    bres.note_read(tpk)
        p.wait("pool", p.cur("dve"))
        tcc = p.dma("pool", ccin.ap(), ccpack[0:64, :], stream="hx")
        p.wait("pool", tcc)
        p.op("pool", lambda E: E.collective_compute(
            "AllGather", OP.bypass, replica_groups=[list(range(NCORES))],
            ins=[ccin.ap().opt()], outs=[ccout.ap().opt()]), is_dma=True)
        p.milestone("pool", kind="cc", sem="cc2", inc=1)

        p.raw_wait("pool", "cc1", 1)
        tg_last = None
        for b in range(B):
            p.op("pool", lambda E, b=b: E.indirect_dma_start(
                out=halo_tm[:, b, :], out_offset=None, in_=hx_out.ap(),
                in_offset=IndirectOffsetOnAxis(ap=haloidx[:, b:b + 1], axis=0)),
                is_dma=True)
            tg_last = p.milestone("pool", kind="dma", sem="hx_d", inc=16)
        for b in range(B):
            p.wait("pe", tg_last)
            for g in range(2):
                pb = psum_bf[g]
                br = pbank_bf[g]
                p.wait_all("pe", br.r)
                for j in range(8):
                    k = g * 8 + j
                    p.op("pe", lambda E, b=b, k=k, j=j, pb=pb: E.transpose(
                        pb[:, j * 128:(j + 1) * 128], halo_tm[:, b, k * 128:(k + 1) * 128],
                        ident_bf[:, :]))
                tq2 = p.milestone("pe")
                br.note_write(tq2)
                p.wait("dve", tq2)
                tc3 = p.opm("dve", lambda E, b=b, g=g, pb=pb: E.tensor_copy(
                    xT_halo[:, b, g * 8:(g + 1) * 8, :],
                    pb[:, :].rearrange("p (j n) -> p j n", j=8)))
                br.note_read(tc3)
        p.barrier()
        ar.pop()

        wflat = wbuf.rearrange("p a k n -> p (a k n)")
        qTs = wflat[:, 0:256].rearrange("p (s n) -> p s n", s=2)
        lkTs = wflat[:, 256:768].rearrange("p (s n) -> p s n", s=2)
        atn = wflat[:, 768:1800].rearrange("p (s n) -> p s n", s=2)
        atT = wflat[:, 1800:2824].rearrange("p (s n) -> p s n", s=2)
        ask = wflat[:, 2824:3080].rearrange("p (s n) -> p s n", s=2)
        mst = wflat[:, 4096:6144]
        ckT_w = wflat[:, 6144:6400]
        cvall_w = wflat[:, 6400:6656].rearrange("p (c n) -> p c n", c=2)
        mst_guard = [None]

        def load_ckcv(b):
            p.raw_wait("pool", "cc2", 1)
            p.wait("pool", p.cur("dve"))
            p.wait("pool", p.cur("pe"))
            v4 = ccout.ap().rearrange("(r b q) n -> r b q n", r=NCORES, b=B)
            for cch in range(2):
                for r2 in range(4):
                    r = cch * 4 + r2
                    tr1 = p.dma("pool", stg[r2 * 32:(r2 + 1) * 32, cch * 128:(cch + 1) * 128],
                                v4[r, b, :, 0:128], stream="hx")
                    tr2 = p.dma("pool", stg[r2 * 32:(r2 + 1) * 32,
                                            256 + cch * 128:256 + (cch + 1) * 128],
                                v4[r, b, :, 128:256], stream="hx")
            p.wait("dve", tr1)
            p.wait("dve", tr2)
            p.opm("dve", lambda E: E.tensor_copy(
                cvall_w, stg[:, 256:512].rearrange("p (c n) -> p c n", c=2)))
            tkb = p.opm("dve", lambda E: E.tensor_copy(
                stg[:, 512:640].bitcast(BF16), stg[:, 0:256]))
            p.wait("pe", tkb)
            for c in range(2):
                pb = psum_bf[c]
                br = pbank_bf[c]
                p.wait_all("pe", br.r)
                tt = p.opm("pe", lambda E, c=c, pb=pb: E.transpose(
                    pb[:, 0:128], stg[:, 512:640].bitcast(BF16)[:, c * 128:(c + 1) * 128],
                    ident_bf[:, :]))
                br.note_write(tt)
                p.wait("dve", tt)
                te = p.opm("dve", lambda E, c=c, pb=pb: E.tensor_copy(
                    ckT_w[:, c * 128:(c + 1) * 128], pb[:, 0:128]))
                br.note_read(te)
            p.wait("pe", te)

        def evac_lv(dstbuf, slot_of):
            def f(jt, ns, bank, tmm):
                slot = slot_of(jt)
                p.wait("act", tmm)
                return p.opm("act", lambda E, slot=slot, ns=ns, bank=bank: E.activation(
                    dstbuf[:, slot, ns * NSL:(ns + 1) * NSL], bank[:, 0:NSL], AF.Copy))
            return f

        def bulk_norm_rope(tile_bf, nb, cosT, sinT, rtmp0):
            rtmp = rtmp0
            t3 = tile_bf.rearrange("p (h d) -> p h d", h=16)
            p.wait("act", p.cur("dve"))
            ta1 = p.opm("act", lambda E, tile_bf=tile_bf: E.activation(
                stg[:, :], tile_bf, AF.Square))
            p.wait("dve", ta1)
            p.op("dve", lambda E: E.tensor_reduce(
                small[:, 0:16].rearrange("p (h u) -> p h u", h=16),
                stg[:, :].rearrange("p (h d) -> p h d", h=16), AX.X, OP.add))
            tsv = p.opm("dve", lambda E: E.tensor_scalar(
                small[:, 0:16], small[:, 0:16], 1.0 / 128, 1e-6, OP.mult, OP.add))
            p.wait("act", tsv)
            ta2 = p.opm("act", lambda E: E.activation(small[:, 16:32], small[:, 0:16], AF.Sqrt))
            p.wait("dve", ta2)
            p.op("dve", lambda E: E.reciprocal(small[:, 16:32], small[:, 16:32]))
            s3 = stg[:, :].rearrange("p (h d) -> p h d", h=16)
            p.op("dve", lambda E, t3=t3, s3=s3: E.tensor_tensor(
                s3, t3, small[:, 16:32].rearrange("p (h u) -> p h u", h=16).broadcast_to(
                    (128, 16, 128)), OP.mult))
            p.op("dve", lambda E, s3=s3, nb=nb: E.tensor_tensor(
                s3, s3, nb.rearrange("p (u d) -> p u d", u=1).broadcast_to((128, 16, 128)),
                OP.mult))
            x1 = s3[:, :, 0:32]
            x2 = s3[:, :, 32:64]
            cB = cosT.rearrange("p (u d) -> p u d", u=1).broadcast_to((128, 16, 32))
            sB = sinT.rearrange("p (u d) -> p u d", u=1).broadcast_to((128, 16, 32))
            p.op("dve", lambda E, x2=x2, sB=sB, rtmp=rtmp: E.tensor_tensor(rtmp, x2, sB, OP.mult))
            p.op("dve", lambda E, t3=t3, x1=x1, cB=cB: E.tensor_tensor(
                t3[:, :, 0:32], x1, cB, OP.mult))
            p.op("dve", lambda E, t3=t3, rtmp=rtmp: E.tensor_tensor(
                t3[:, :, 0:32], t3[:, :, 0:32], rtmp, OP.subtract))
            p.op("dve", lambda E, x2=x2, cB=cB, rtmp=rtmp: E.tensor_tensor(rtmp, x2, cB, OP.mult))
            p.op("dve", lambda E, t3=t3, x1=x1, sB=sB: E.tensor_tensor(
                t3[:, :, 32:64], x1, sB, OP.mult))
            p.op("dve", lambda E, t3=t3, rtmp=rtmp: E.tensor_tensor(
                t3[:, :, 32:64], t3[:, :, 32:64], rtmp, OP.add))
            return p.opm("dve", lambda E, t3=t3, s3=s3: E.tensor_copy(
                t3[:, :, 64:128], s3[:, :, 64:128]))

        for b in range(B):
            ar.push()
            qtm = ar.take_bf16(NTB, D)
            lktm = ar.take_bf16(5, D)
            lvtm = ar.take_bf16(5, D)
            rtmp = ar.take_bf16(16, 32)
            tb_q = load_bias(B_l1[0], 0)
            p.wait("pe", tb_q)
            emit_proj(W_l1["wq"], 0, evac_lv(qtm, lambda j: j),
                      src=xT, tok0=b * 512, ntiles=NTB)
            tb_lk = load_bias(B_l1[1], 1)
            p.wait("pe", tb_lk)
            for jt in range(NTB):
                bulk_norm_rope(qtm[:, jt, :], qnb, cosq[:, jt, :], sinq[:, jt, :], rtmp)
            emit_proj(W_l1["wlk"], 1, evac_lv(lktm, lambda j: j + 1),
                      src=xT, tok0=b * 512, ntiles=NTB)
            emit_proj(W_l1["wlk"], 1, evac_lv(lktm, lambda j: 0),
                      src=xT_halo[:, b], tok0=0, ntiles=1)
            tb_lv = load_bias(B_l1[2], 0)
            p.wait("pe", tb_lv)
            for sl5 in range(5):
                bulk_norm_rope(lktm[:, sl5, :], knb,
                               cosh if sl5 == 0 else cosq[:, sl5 - 1, :],
                               sinh if sl5 == 0 else sinq[:, sl5 - 1, :], rtmp)
            emit_proj(W_l1["wlv"], 0, evac_lv(lvtm, lambda j: j + 1),
                      src=xT, tok0=b * 512, ntiles=NTB)
            emit_proj(W_l1["wlv"], 0, evac_lv(lvtm, lambda j: 0),
                      src=xT_halo[:, b], tok0=0, ntiles=1)
            p.barrier()
            load_ckcv(b)
            sres = [Res(), Res()]
            for jt in range(NTB):
                it = b * NTB + jt
                if mst_guard[0] is not None:
                    p.wait("act", mst_guard[0])
                fe = {}

                def frontend_a(h, jt=jt, qtm=qtm, lktm=lktm):
                    sl = h % 2
                    pb = psum_bf[sl]
                    br = pbank_bf[sl]
                    p.wait_all("pe", br.r)
                    p.op("pe", lambda E, h=h, pb=pb, jt=jt, qtm=qtm: E.transpose(
                        pb[:, 0:128], qtm[:, jt, h * 128:(h + 1) * 128], ident_bf[:, :]))
                    p.op("pe", lambda E, h=h, pb=pb, jt=jt, lktm=lktm: E.transpose(
                        pb[:, 128:256], lktm[:, jt, h * 128:(h + 1) * 128], ident_bf[:, :]))
                    tq3 = p.opm("pe", lambda E, h=h, pb=pb, jt=jt, lktm=lktm: E.transpose(
                        pb[:, 256:384], lktm[:, jt + 1, h * 128:(h + 1) * 128], ident_bf[:, :]))
                    br.note_write(tq3)
                    p.wait("act", tq3)
                    p.wait_all("act", sres[sl].r)
                    p.op("act", lambda E, sl=sl, pb=pb: E.activation(
                        qTs[:, sl, :], pb[:, 0:128], AF.Copy))
                    tq5 = p.opm("act", lambda E, sl=sl, pb=pb: E.activation(
                        lkTs[:, sl, :], pb[:, 128:384], AF.Copy))
                    br.note_read(tq5)
                    sres[sl].note_write(tq5)
                    bank_s = psum[sl]
                    bres_s = pbank[sl]
                    p.wait("pe", tq5)
                    p.wait_all("pe", bres_s.r)
                    p.op("pe", lambda E, sl=sl, bank_s=bank_s: E.matmul(
                        bank_s[:, 0:256], qTs[:, sl, :], ckT_w, start=True, stop=False))
                    tcs = p.opm("pe", lambda E, h=h, sl=sl, bank_s=bank_s: E.matmul(
                        bank_s[:, 256:257], qTs[:, sl, :], sinkT_bf[:, h:h + 1],
                        start=False, stop=True))
                    bres_s.note_write(tcs)
                    bank_l = psum[2 + sl]
                    bres_l = pbank[2 + sl]
                    p.wait_all("pe", bres_l.r)
                    tls = p.opm("pe", lambda E, sl=sl, bank_l=bank_l: E.matmul(
                        bank_l[:, 0:256], qTs[:, sl, :], lkTs[:, sl, :], start=True, stop=True))
                    bres_l.note_write(tls)
                    fe[(h, "mm")] = (tcs, tls)

                def frontend_s(h, jt=jt):
                    sl = h % 2
                    bank_s = psum[sl]
                    bres_s = pbank[sl]
                    bank_l = psum[2 + sl]
                    bres_l = pbank[2 + sl]
                    tcs, tls = fe[(h, "mm")]
                    if h >= 2:
                        p.wait("dve", fe[h - 2])
                    so = sl * 640
                    sc0, sc1 = 40 + sl * 6, 42 + sl * 6
                    sm3 = stg[:, so:so + 514].rearrange("p (w n) -> p w n", w=2)
                    p.wait("dve", tcs)
                    p.wait("dve", tls)
                    p.op("dve", lambda E, bank_s=bank_s, sm3=sm3, jt=jt: E.tensor_tensor(
                        sm3[:, 0:1, :], bank_s[:, 0:257].rearrange("p (u n) -> p u n", u=1),
                        cmask[:, jt, 0:257].rearrange("p (u n) -> p u n", u=1), OP.add))
                    tml = p.opm("dve", lambda E, bank_l=bank_l, sm3=sm3, jt=jt: E.tensor_tensor(
                        sm3[:, 1:2, 0:256], bank_l[:, 0:256].rearrange("p (u n) -> p u n", u=1),
                        lmask[:, min(jt, 1), :].rearrange("p (u n) -> p u n", u=1), OP.add))
                    bres_s.note_read(tml)
                    bres_l.note_read(tml)
                    p.op("dve", lambda E, so=so: E.tensor_scalar(
                        stg[:, so + 513:so + 514], stg[:, so + 512:so + 513], 0.0, -1e9,
                        OP.mult, OP.add))
                    p.op("dve", lambda E, sm3=sm3, sc0=sc0: E.tensor_reduce(
                        small[:, sc0:sc0 + 2].rearrange("p (w u) -> p w u", w=2), sm3,
                        AX.X, OP.max))
                    tmx = p.opm("dve", lambda E, sc0=sc0: E.tensor_scalar(
                        small[:, sc0:sc0 + 2], small[:, sc0:sc0 + 2], -SCALE, 0.0,
                        OP.mult, OP.bypass))
                    p.wait("act", tmx)
                    p.op("act", lambda E, so=so, sc0=sc0: E.activation(
                        stg[:, so:so + 257], stg[:, so:so + 257], AF.Exp, scale=SCALE,
                        bias=small[:, sc0:sc0 + 1]))
                    tex = p.opm("act", lambda E, so=so, sc0=sc0: E.activation(
                        stg[:, so + 257:so + 514], stg[:, so + 257:so + 514], AF.Exp,
                        scale=SCALE, bias=small[:, sc0 + 1:sc0 + 2]))
                    p.wait("dve", tex)
                    p.op("dve", lambda E, sm3=sm3, sc1=sc1: E.tensor_reduce(
                        small[:, sc1:sc1 + 2].rearrange("p (w u) -> p w u", w=2), sm3,
                        AX.X, OP.add))
                    trc = p.opm("dve", lambda E, sc1=sc1: E.reciprocal(
                        small[:, sc1 + 2:sc1 + 4], small[:, sc1:sc1 + 2]))
                    p.wait("pool", trc)
                    p.wait_all("pool", sres[sl].r)
                    tat = p.opm("pool", lambda E, sl=sl, sm3=sm3, sc1=sc1: E.tensor_tensor(
                        atn[:, sl, 0:514].rearrange("p (w n) -> p w n", w=2), sm3,
                        small[:, sc1 + 2:sc1 + 4].rearrange(
                            "p (w u) -> p w u", w=2).broadcast_to((128, 2, 257)), OP.mult))
                    fe[h] = tat

                def backend(h, jt=jt, lvtm=lvtm):
                    sl = h % 2
                    pb = psum_bf[sl]
                    br = pbank_bf[sl]
                    tat = fe[h]
                    p.wait("pe", tat)
                    p.wait_all("pe", br.r)
                    p.op("pe", lambda E, sl=sl, pb=pb: E.transpose(
                        pb[:, 384:512], atn[:, sl, 0:128], ident_bf[:, :]))
                    p.op("pe", lambda E, sl=sl, pb=pb: E.transpose(
                        pb[:, 512:640], atn[:, sl, 128:256], ident_bf[:, :]))
                    p.op("pe", lambda E, sl=sl, pb=pb: E.transpose(
                        pb[:, 640:768], atn[:, sl, 257:385], ident_bf[:, :]))
                    p.op("pe", lambda E, sl=sl, pb=pb: E.transpose(
                        pb[:, 768:896], atn[:, sl, 385:513], ident_bf[:, :]))
                    ts5 = p.opm("pe", lambda E, sl=sl, pb=pb: E.transpose(
                        pb[0:1, 896:1024], atn[:, sl, 256:257], ident_bf[:, :]))
                    br.note_write(ts5)
                    p.wait("act", ts5)
                    p.op("act", lambda E, sl=sl, pb=pb: E.activation(
                        atT[:, sl, :], pb[:, 384:896], AF.Copy))
                    tev5 = p.opm("act", lambda E, sl=sl, pb=pb: E.activation(
                        ask[0:1, sl, :], pb[0:1, 896:1024], AF.Copy))
                    br.note_read(tev5)
                    sres[sl].note_write(tev5)
                    bank_c = psum[4 + sl]
                    bres_c = pbank[4 + sl]
                    p.wait("pe", tev5)
                    p.wait_all("pe", bres_c.r)
                    p.op("pe", lambda E, sl=sl, bank_c=bank_c: E.matmul(
                        bank_c[:, 0:128], atT[:, sl, 0:128], cvall_w[:, 0, :],
                        start=True, stop=False))
                    p.op("pe", lambda E, sl=sl, bank_c=bank_c: E.matmul(
                        bank_c[:, 0:128], atT[:, sl, 128:256], cvall_w[:, 1, :],
                        start=False, stop=False))
                    p.op("pe", lambda E, h=h, sl=sl, bank_c=bank_c: E.matmul(
                        bank_c[:, 0:128], ask[0:1, sl, :], sinkv_row[0:1, h * 128:(h + 1) * 128],
                        start=False, stop=False))
                    p.op("pe", lambda E, h=h, sl=sl, bank_c=bank_c, lvtm=lvtm, jt=jt: E.matmul(
                        bank_c[:, 0:128], atT[:, sl, 256:384], lvtm[:, jt, h * 128:(h + 1) * 128],
                        start=False, stop=False))
                    tcx = p.opm("pe", lambda E, h=h, sl=sl, bank_c=bank_c, lvtm=lvtm, jt=jt: E.matmul(
                        bank_c[:, 0:128], atT[:, sl, 384:512],
                        lvtm[:, jt + 1, h * 128:(h + 1) * 128], start=False, stop=True))
                    bres_c.note_write(tcx)
                    sres[sl].note_read(tcx)
                    p.wait("act", tcx)
                    tvm = p.opm("act", lambda E, h=h, bank_c=bank_c: E.activation(
                        mst[:, h * 128:(h + 1) * 128], bank_c[:, 0:128], AF.Copy, scale=0.5))
                    bres_c.note_read(tvm)

                for h in range(18):
                    if h >= 1 and h - 1 < 16:
                        frontend_s(h - 1)
                    if h < 16:
                        frontend_a(h)
                    if h >= 2:
                        backend(h - 2)
                p.wait("pe", p.cur("act"))
                for g in range(2):
                    pb = psum_bf[g]
                    br = pbank_bf[g]
                    p.wait_all("pe", br.r)
                    for j in range(8):
                        k = g * 8 + j
                        p.op("pe", lambda E, k=k, j=j, pb=pb: E.transpose(
                            pb[:, j * 128:(j + 1) * 128], mst[:, k * 128:(k + 1) * 128],
                            ident_bf[:, :]))
                    tq6 = p.milestone("pe")
                    br.note_write(tq6)
                    p.wait("dve", tq6)
                    tc6 = p.opm("dve", lambda E, b=b, jt=jt, g=g, pb=pb: E.tensor_copy(
                        xT[:, g * 8:(g + 1) * 8, b * 512 + jt * 128:b * 512 + (jt + 1) * 128],
                        pb[:, :].rearrange("p (j n) -> p j n", j=8)))
                    br.note_read(tc6)
                mst_guard[0] = p.cur("pe")
            p.barrier()
            ar.pop()

        acc_bf = ar.take_bf16(NT, D)
        tb_o = load_bias(B_l1[3], 1)
        p.wait("pe", tb_o)

        def evac_accbf(it, ns, bank, tmm):
            p.wait("act", tmm)
            return p.opm("act", lambda E, it=it, ns=ns, bank=bank: E.activation(
                acc_bf[:, it, ns * NSL:(ns + 1) * NSL], bank[:, 0:NSL], AF.Copy))

        emit_proj(W_l1["wo"], 1, evac_accbf)
        p.wait("dve", p.cur("act"))
        for it in range(NT):
            ln_stats(acc_bf[:, it, :], it, 4)
            mcol = stats[:, it, 4:5]
            rcol = stats[:, it, 5:6]
            p.opm("dve", lambda E, it=it, m=mcol, r=rcol: E.tensor_scalar(
                stg[:, :], acc_bf[:, it, :], m, r, OP.subtract, OP.mult))
            ta = p.opm("dve", lambda E, it=it: E.tensor_tensor(
                xres[:, it, :], xres[:, it, :], stg[:, :], OP.add))
            p.wait("act", ta)
        ar.pop()

    if DEBUG_TAP != "l0_attn":
        emit_moe_ffn(F0_w13, F0_wdn, gtile0, nhc=8, gate_af=AF.Silu,
                     scale_input=True, seed_xln=False, router_softmax=False)
    if DEBUG_TAP not in ("l0_attn", "l0_ffn"):
        emit_l1_attn()
    if DEBUG_TAP not in ("l0_attn", "l0_ffn", "l1_attn"):
        emit_moe_ffn(F1_w13, F1_wdn, gtile1, nhc=11, gate_af=AF.Sigmoid,
                     scale_input=False, seed_xln=True, router_softmax=True)

    t_done = p.cur("dve")
    p.wait("sp", t_done)
    t_outs = []
    tgt = dbg if DEBUG_TAP is not None else OUT
    for it in range(NT):
        t_outs.append(p.dma("sp", tgt[it], xres[:, it, :], stream="xio"))
    if DEBUG_TAP is not None:
        for it in range(NT):
            t_outs.append(p.dma("sp", OUT[it], xres[:, it, :], stream="xio"))
    p.wait_all("sp", t_outs)

    with nc.Block() as block:
        p.emit(nc, block, stack)
    stack.close()
    return nc, p



def _prep_inputs(inputs):
    x = np.asarray(inputs["L0_attn_x"])
    ident = np.eye(128, dtype=np.float32)
    pos = np.arange(S, dtype=np.float32)
    decay_full = np.exp(-pos / S).astype(np.float32)

    wq = tile_proj_weight(np.ascontiguousarray(np.asarray(inputs["L0_attn_Wq"]).T))
    wk = tile_proj_weight(np.ascontiguousarray(np.asarray(inputs["L0_attn_Wk"]).T))
    wv = tile_proj_weight(np.ascontiguousarray(np.asarray(inputs["L0_attn_Wv"]).T))
    wo = tile_proj_weight(np.ascontiguousarray(np.asarray(inputs["L0_attn_Wo"]).T))
    l0_bias = np.stack([np.asarray(inputs[f"L0_attn_{k}"])[None, :] for k in
                        ("bq", "bk", "bv", "bo")])

    def moe_blobs(w1s, w3s, w2s, sw1, sw3, sw2):
        m13, mdn = [], []
        for i in range(9):
            w1m = sw1 if i == 0 else w1s[i - 1]
            w3m = sw3 if i == 0 else w3s[i - 1]
            w2m = sw2 if i == 0 else w2s[i - 1]
            a = tile_proj_weight(np.ascontiguousarray(w1m.T), 128)
            b = tile_proj_weight(np.ascontiguousarray(w3m.T), 128)
            m13.append(np.concatenate([a, b], axis=-1))
            w2t = np.ascontiguousarray(w2m.T)
            nhc = w2t.shape[0] // 128
            halves = [np.ascontiguousarray(
                w2t[:, oh * 1024:(oh + 1) * 1024].reshape(nhc, 128, 1024).transpose(1, 0, 2))
                for oh in range(2)]
            mdn.append(np.stack(halves))
        return np.ascontiguousarray(np.stack(m13)), np.ascontiguousarray(np.stack(mdn))

    f0_w13, f0_wdn = moe_blobs(
        np.asarray(inputs["L0_ffn_w1"]), np.asarray(inputs["L0_ffn_w3"]),
        np.asarray(inputs["L0_ffn_w2"]), np.asarray(inputs["L0_ffn_shared_w1"]),
        np.asarray(inputs["L0_ffn_shared_w3"]), np.asarray(inputs["L0_ffn_shared_w2"]))
    f0_gate = np.ascontiguousarray(
        np.asarray(inputs["L0_ffn_gate"]).T.reshape(KC, 128, 8).transpose(1, 0, 2))
    f0_eb = np.asarray(inputs["L0_ffn_expert_bias"])[None, :]
    iota8 = np.arange(8, dtype=np.float32)[None, :]

    l1_wq = tile_proj_weight(np.ascontiguousarray(np.asarray(inputs["L1_attn_Wq"])))
    l1_wlk = tile_proj_weight(np.ascontiguousarray(np.asarray(inputs["L1_attn_Wlk"])))
    l1_wlv = tile_proj_weight(np.ascontiguousarray(np.asarray(inputs["L1_attn_Wlv"])))
    l1_wo = tile_proj_weight(np.ascontiguousarray(np.asarray(inputs["L1_attn_Wo"])))
    l1_bias = np.stack([np.asarray(inputs[f"L1_attn_{k}"])[None, :] for k in
                        ("bq", "blk", "blv", "bo")])
    l1_wc = np.ascontiguousarray(
        np.asarray(inputs["L1_attn_Wc"]).reshape(KC, 128, 1).transpose(1, 0, 2))
    l1_bc = np.asarray(inputs["L1_attn_bc"]).reshape(1, 1)
    l1_wkv = np.ascontiguousarray(np.concatenate(
        [np.asarray(inputs["L1_attn_Wk"]).reshape(KC, 128, 128),
         np.asarray(inputs["L1_attn_Wv"]).reshape(KC, 128, 128)], -1).transpose(1, 0, 2))
    l1_bkv = np.concatenate([np.asarray(inputs["L1_attn_bk"]),
                             np.asarray(inputs["L1_attn_bv"])])[None, :]
    l1_qn = np.asarray(inputs["L1_attn_qn_w"])[None, :]
    l1_kn = np.asarray(inputs["L1_attn_kn_w"])[None, :]
    l1_sinkT = np.ascontiguousarray(np.asarray(inputs["L1_attn_sink_k"]).T)
    l1_sinkv = np.ascontiguousarray(np.asarray(inputs["L1_attn_sink_v"]).reshape(1, D))
    f1_w13, f1_wdn = moe_blobs(
        np.asarray(inputs["L1_ffn_w_gate"]), np.asarray(inputs["L1_ffn_w_up"]),
        np.asarray(inputs["L1_ffn_w_down"]), np.asarray(inputs["L1_ffn_shared_gate"]),
        np.asarray(inputs["L1_ffn_shared_up"]), np.asarray(inputs["L1_ffn_shared_down"]))
    f1_gate = np.ascontiguousarray(
        np.asarray(inputs["L1_ffn_router_w"]).T.reshape(KC, 128, 8).transpose(1, 0, 2))
    pattern = (np.arange(128)[:, None] // 16 == np.arange(8)[None, :]).astype(np.float32)
    inv_freq = (1.0 / 10000.0 ** (np.arange(32, dtype=np.float32) * 2.0 / 64.0))

    def rope_tabs(pos):
        ang = pos.astype(np.float32)[:, None] * inv_freq[None, :]
        return np.cos(ang).astype(np.float32), np.sin(ang).astype(np.float32)

    maps = []
    for c in range(NCORES):
        sl = slice(c * SPC, (c + 1) * SPC)
        xc = np.ascontiguousarray(
            np.concatenate([x[0, sl], x[1, sl]], axis=0)).reshape(NT, 128, D)
        dec = np.ascontiguousarray(np.tile(decay_full[sl], 2).reshape(NT, 128).T)
        pos_c = np.arange(c * SPC, (c + 1) * SPC)
        cq, sq = rope_tabs(pos_c)
        cosq_c = np.ascontiguousarray(cq.reshape(NTB, 128, 32).transpose(1, 0, 2))
        sinq_c = np.ascontiguousarray(sq.reshape(NTB, 128, 32).transpose(1, 0, 2))
        hpos = np.maximum(np.arange(c * SPC - 128, c * SPC), 0)
        cosh_c, sinh_c = rope_tabs(hpos)
        bends = (c * 32 + np.arange(32)) * 16 + 15
        cosck_c, sinck_c = rope_tabs(bends)
        cm = np.full((128, NTB, 257), -1e9, np.float32)
        cm[:, :, 256] = 0.0
        allends = np.arange(256) * 16 + 15
        for j in range(NTB):
            posj = c * SPC + j * 128 + np.arange(128)
            cm[:, j, 0:256] = np.where(allends[None, :] <= posj[:, None], 0.0, -1e9)
        cmask_c = np.ascontiguousarray(cm)
        lm = np.full((128, 2, 256), -1e9, np.float32)
        ii = np.arange(128)[:, None]
        kk = np.arange(256)[None, :]
        valid = (kk >= ii) & (kk <= ii + 128)
        lm[:, 1, :] = np.where(valid, 0.0, -1e9)
        v0 = valid & ((kk >= 128) if c == 0 else np.ones_like(valid))
        lm[:, 0, :] = np.where(v0, 0.0, -1e9)
        lmask_c = np.ascontiguousarray(lm)
        haloidx_c = np.ascontiguousarray(
            (np.arange(128)[:, None] + np.array([0, 128])[None, :] +
             (((c - 1) % NCORES) * 256)).astype(np.uint32))
        maps.append({
            "x": xc, "ident": ident, "decay": dec,
            "l0_wq": wq, "l0_wk": wk, "l0_wv": wv, "l0_wo": wo,
            "l0_bias": l0_bias, "iota8": iota8,
            "f0_w13": f0_w13, "f0_wdn": f0_wdn, "f0_gate": f0_gate, "f0_eb": f0_eb,
            "l1_wq": l1_wq, "l1_wlk": l1_wlk, "l1_wlv": l1_wlv, "l1_wo": l1_wo,
            "l1_bias": l1_bias, "l1_wc": l1_wc, "l1_bc": l1_bc, "l1_wkv": l1_wkv,
            "l1_bkv": l1_bkv, "l1_qn": l1_qn, "l1_kn": l1_kn,
            "l1_sinkT": l1_sinkT, "l1_sinkv": l1_sinkv,
            "f1_w13": f1_w13, "f1_wdn": f1_wdn, "f1_gate": f1_gate,
            "cosq": cosq_c, "sinq": sinq_c, "cosh": cosh_c, "sinh": sinh_c,
            "cosck": cosck_c, "sinck": sinck_c,
            "cmask": cmask_c, "lmask": lmask_c, "pattern": pattern,
            "haloidx": haloidx_c,
        })
    return maps


def kernel(**inputs):
    maps = _prep_inputs(inputs)
    nc, _ = build_program()
    res = run_bass_kernel_spmd(nc, maps, core_ids=list(range(NCORES)))
    out = np.empty((B, S, D), dtype=np.float32)
    key = "dbg" if DEBUG_TAP is not None else "out"
    for c in range(NCORES):
        oc = res.results[c][key].reshape(TPC, D)
        out[0, c * SPC:(c + 1) * SPC] = oc[:SPC]
        out[1, c * SPC:(c + 1) * SPC] = oc[SPC:]
    return out


# version 09 (final)
# speedup vs baseline: 1.0869x; 1.0869x over previous
import numpy as np
import contextlib
import concourse.bass as bass
import concourse.mybir as mybir
from concourse.bass_utils import run_bass_kernel_spmd
from concourse.alu_op_type import AluOpType as OP

F32 = mybir.dt.float32
BF16 = mybir.dt.bfloat16
AF = mybir.ActivationFunctionType
AX = mybir.AxisListType

NCORES = 8
TPC = 1024
NT = 8
NTB = 4
D = 2048
KC = 16
S = 4096
B = 2
SPC = 512
EPS0 = 1e-6
NSL = 256

ENGS = ("pe", "act", "dve", "pool", "sp")
_BLOCK_ATTR = {"pe": "tensor", "act": "scalar", "dve": "vector", "pool": "gpsimd", "sp": "sync"}

DEBUG_TAP = None
_NO_CC = False


class Prog:
    def __init__(self):
        self.items = {e: [] for e in ENGS}
        self.semcount = {}
        self.last_wait = {e: {} for e in ENGS}
        self.own = {}

    def _bump(self, sem, inc):
        self.semcount[sem] = self.semcount.get(sem, 0) + inc
        return self.semcount[sem]

    SELF_CHAIN = ("dve", "act", "pool")

    def op(self, eng, fn, is_dma=False):
        if eng in self.SELF_CHAIN and not is_dma:
            sem = eng + "_c"
            cnt = self.semcount.get(sem, 0)
            if cnt and self.last_wait[eng].get(sem, 0) < cnt:
                self.last_wait[eng][sem] = cnt
                self.items[eng].append(["w", sem, cnt])
            self.items[eng].append(["op", fn, [], is_dma])
            self.milestone(eng)
            return
        self.items[eng].append(["op", fn, [], is_dma])

    def opm(self, eng, fn, kind="cmp", sem=None, inc=None):
        self.op(eng, fn)
        return self.milestone(eng, kind=kind, sem=sem, inc=inc)

    def dma(self, eng, out_ap, in_ap, stream=None, **kw):
        self.op(eng, lambda E, o=out_ap, i=in_ap, kw=kw: E.dma_start(out=o, in_=i, **kw),
                is_dma=True)
        return self.milestone(eng, kind="dma", sem=(stream or eng) + "_d", inc=16)

    def milestone(self, eng, kind="cmp", sem=None, inc=None):
        sem = sem or (eng + "_c")
        inc = inc if inc is not None else 1
        if kind == "cmp":
            self.own.setdefault(sem, eng)
        for it in reversed(self.items[eng]):
            if it[0] == "op":
                if it[3] and kind != "cc" and inc % 16 != 0:
                    inc = 16
                    if kind == "cmp":
                        kind = "dma"
                        sem = sem if sem.endswith("_d") else sem[:-2] + "_d"
                v = self._bump(sem, inc)
                for j, (s0, i0) in enumerate(it[2]):
                    if s0 == sem:
                        it[2][j] = (sem, i0 + inc)
                        return (sem, v, kind)
                it[2].append((sem, inc))
                return (sem, v, kind)
        raise RuntimeError(f"milestone on empty engine {eng}")

    def cur(self, eng):
        for it in reversed(self.items[eng]):
            if it[0] == "op":
                if it[3] and it[2]:
                    s0 = it[2][0][0]
                    return (s0, self.semcount[s0], "dma")
                if it[2]:
                    sem = eng + "_c"
                    for s, _ in it[2]:
                        if s == sem:
                            return (sem, self.semcount[sem], "cmp")
                return self.milestone(eng)
        return None

    def barrier(self, dst_engs=("pe", "act", "dve", "pool"), src_engs=("pe", "act", "dve", "pool")):
        toks = [self.cur(e) for e in src_engs]
        toks += [(s, v, "dma") for s, v in self.semcount.items() if s.endswith("_d")]
        for d in dst_engs:
            self.wait_all(d, toks)

    def wait(self, eng, tok):
        if tok is None:
            return
        sem, val, kind = tok
        if kind == "cmp" and self.own.get(sem) == eng:
            return
        if kind in ("dma", "cc"):
            val = self.semcount[sem]
        if self.last_wait[eng].get(sem, 0) >= val:
            return
        self.last_wait[eng][sem] = val
        self.items[eng].append(["w", sem, val])

    def wait_all(self, eng, toks):
        for t in list(toks):
            self.wait(eng, t)

    def cc(self, fn, sem):
        self.op("pool", fn, is_dma=True)
        return self.milestone("pool", kind="cc", sem=sem, inc=1)

    def raw_wait(self, eng, sem, val):
        self.items[eng].append(["w", sem, val])

    def check_deadlock(self):
        ptr = {e: 0 for e in ENGS}
        sem = {}
        progress = True
        while progress:
            progress = False
            for e in ENGS:
                while ptr[e] < len(self.items[e]):
                    it = self.items[e][ptr[e]]
                    if it[0] == "op":
                        for s, inc in it[2]:
                            sem[s] = sem.get(s, 0) + inc
                        ptr[e] += 1
                        progress = True
                    else:
                        if sem.get(it[1], 0) >= it[2]:
                            ptr[e] += 1
                            progress = True
                        else:
                            break
        stuck = {e for e in ENGS if ptr[e] < len(self.items[e])}
        if stuck:
            det = []
            for e in stuck:
                it = self.items[e][ptr[e]]
                det.append(f"{e}@{ptr[e]}/{len(self.items[e])} waits {it[1]}>={it[2]}"
                           f" (has {sem.get(it[1], 0)})")
            raise RuntimeError("deadlock in built program: " + "; ".join(det))

    def emit(self, nc, block, stack):
        self.check_deadlock()
        sems = {s: stack.enter_context(nc.semaphore(s)) for s in self.semcount}

        def mk(e):
            def body(eng):
                for it in self.items[e]:
                    if it[0] == "op":
                        inst = it[1](eng)
                        for s, inc in it[2]:
                            inst.then_inc(sems[s], inc)
                    else:
                        eng.wait_ge(sems[it[1]], it[2])
            return body

        for e in ENGS:
            getattr(block, _BLOCK_ATTR[e])(mk(e))

    def stats(self):
        return {e: len(self.items[e]) for e in ENGS}


class Res:
    __slots__ = ("w", "r")

    def __init__(self):
        self.w = None
        self.r = []

    def note_write(self, tok):
        self.w = tok
        self.r = []

    def note_read(self, tok):
        if tok is not None:
            self.r.append(tok)
            if len(self.r) > 8:
                self.r[:] = self.r[-8:]


class Arena:

    def __init__(self, handle, n_f32):
        self.h = handle
        self.n = n_f32
        self.marks = []
        self.off = 0

    def push(self):
        self.marks.append(self.off)

    def pop(self):
        self.off = self.marks.pop()

    def take_f32(self, *shape):
        n = int(np.prod(shape))
        assert self.off + n <= self.n, f"arena overflow: {self.off}+{n} > {self.n}"
        v = self.h.ap()[:, self.off:self.off + n]
        self.off += n
        if len(shape) > 1:
            pat = " ".join(f"d{i}" for i in range(len(shape)))
            v = v.rearrange(f"p ({pat}) -> p {pat}",
                            **{f"d{i}": shape[i] for i in range(len(shape) - 1)})
        return v

    def take_bf16(self, *shape):
        n = int(np.prod(shape))
        assert n % 2 == 0
        assert self.off + n // 2 <= self.n, f"arena overflow bf16: {self.off}+{n // 2} > {self.n}"
        v = self.h.ap()[:, self.off:self.off + n // 2].bitcast(BF16)
        self.off += n // 2
        if len(shape) > 1:
            pat = " ".join(f"d{i}" for i in range(len(shape)))
            v = v.rearrange(f"p ({pat}) -> p {pat}",
                            **{f"d{i}": shape[i] for i in range(len(shape) - 1)})
        return v



def tile_proj_weight(w_kn, nsl=NSL):
    K, N = w_kn.shape
    kc, ns = K // 128, N // nsl
    return np.ascontiguousarray(w_kn.reshape(kc, 128, ns, nsl).transpose(2, 1, 0, 3))



def build_program():
    nc = bass.Bass("TRN2", target_bir_lowering=False)
    stack = contextlib.ExitStack()

    def din(name, shape, dtype=F32):
        return nc.dram_tensor(name, list(shape), dtype, kind="ExternalInput").ap()

    X = din("x", [NT, 128, D])
    OUT = nc.dram_tensor("out", [NT, 128, D], F32, kind="ExternalOutput").ap()
    IDENT = din("ident", [128, 128])
    DECAY = din("decay", [128, NT])

    W_l0 = {k: din(f"l0_{k}", [8, 128, KC, NSL]) for k in ("wq", "wk", "wv", "wo")}
    B_l0 = din("l0_bias", [4, 1, D])
    IOTA8 = din("iota8", [1, 8])
    F0_w13 = din("f0_w13", [9, 8, 128, KC, 256])
    F0_wdn = din("f0_wdn", [9, 2, 128, 8, 1024])
    F0_gate = din("f0_gate", [128, KC, 8])
    F0_eb = din("f0_eb", [1, 8])
    W_l1 = {k: din(f"l1_{k}", [8, 128, KC, NSL]) for k in ("wq", "wlk", "wlv", "wo")}
    B_l1 = din("l1_bias", [4, 1, D])
    L1_wc = din("l1_wc", [128, KC, 1])
    L1_bc = din("l1_bc", [1, 1])
    L1_wkv = din("l1_wkv", [128, KC, 256])
    B_l1_kv = din("l1_bkv", [1, 256])
    L1_qn = din("l1_qn", [1, 128])
    L1_kn = din("l1_kn", [1, 128])
    L1_sinkT = din("l1_sinkT", [128, 16])
    L1_sinkv = din("l1_sinkv", [1, D])
    COSQ = din("cosq", [128, NTB, 32])
    SINQ = din("sinq", [128, NTB, 32])
    COSH = din("cosh", [128, 32])
    SINH = din("sinh", [128, 32])
    COSCK = din("cosck", [32, 32])
    SINCK = din("sinck", [32, 32])
    CMASK = din("cmask", [128, NTB, 257])
    LMASK = din("lmask", [128, 2, 256])
    PATTERN = din("pattern", [128, 8])
    HALOIDX = din("haloidx", [128, 2], mybir.dt.uint32)
    F1_w13 = din("f1_w13", [9, 11, 128, KC, 256])
    F1_wdn = din("f1_wdn", [9, 2, 128, 11, 1024])
    F1_gate = din("f1_gate", [128, KC, 8])
    hx_in = nc.dram_tensor("hx_in", [256, D], BF16)
    hx_out = nc.dram_tensor("hx_out", [2048, D], BF16)
    ccin = nc.dram_tensor("ccin", [64, 256], F32)
    ccout = nc.dram_tensor("ccout", [512, 256], F32)

    dbg = None
    if DEBUG_TAP is not None:
        dbg = nc.dram_tensor("dbg", [NT, 128, D], F32, kind="ExternalOutput").ap()

    used = [0]

    def salloc(name, shape, dtype):
        t = stack.enter_context(nc.sbuf_tensor(name, list(shape), dtype))
        used[0] += int(np.prod(shape[1:])) * mybir.dt.size(dtype)
        return t

    xres_t = salloc("xres", [128, NT * D], F32)
    xres = xres_t.ap().rearrange("p (t d) -> p t d", t=NT)
    xT = salloc("xT", [128, KC, TPC], BF16).ap()
    wbuf = salloc("wbuf", [128, 2, KC, NSL], BF16).ap()
    stg_t = salloc("stg", [128, D], F32)
    stg = stg_t.ap()
    bias_bf = salloc("bias_bf", [1, 2, D], BF16).ap()
    stats = salloc("stats", [128, NT, 8], F32).ap()
    ssel = salloc("ssel", [128, NT, 8], F32).ap()
    small = salloc("small", [128, 64], F32).ap()
    ident_bf = salloc("ident_bf", [128, 128], BF16).ap()
    ones_bf = salloc("ones_bf", [128, 128], BF16).ap()
    ones_f = salloc("ones_f", [1, 128], F32).ap()
    ident_f = salloc("ident_f", [128, 128], F32).ap()
    gtile0 = salloc("gtile0", [128, KC, 8], F32).ap()
    gtile1 = salloc("gtile1", [128, KC, 8], F32).ap()
    ebias_t = salloc("ebias_t", [128, 8], F32).ap()
    iota_t = salloc("iota_t", [128, 8], F32).ap()
    rev_t = salloc("rev_t", [128, 8], F32).ap()
    decay_sb = salloc("decay_sb", [128, NT], F32).ap()

    arena_f32 = (212700 - used[0]) // 4
    arena_t = salloc("arena", [128, arena_f32], F32)
    ar = Arena(arena_t, arena_f32)

    psum = [stack.enter_context(nc.psum_tensor(f"ps{i}", [128, 512], F32)) for i in range(6)]
    psum_bf = [stack.enter_context(nc.psum_tensor(f"psbf{i}", [128, 1024], BF16)) for i in range(2)]
    pbank = [Res() for _ in range(6)]
    pbank_bf = [Res() for _ in range(2)]
    wpar = [Res(), Res()]
    bias_res = [Res(), Res()]
    stgreg = [Res() for _ in range(4)]

    kv_in = nc.dram_tensor("kv_in", [128, 32 * 129], BF16)
    kv_out = nc.dram_tensor("kv_out", [NCORES * 128, 32 * 129], BF16)
    xspill = nc.dram_tensor("xspill", [NT, 128, D], F32)

    p = Prog()

    def ln_stats(src_ap, it, col, sq_scratch=None):
        sq = stg if sq_scratch is None else sq_scratch
        m_raw = stats[:, it, col + 2:col + 3]
        s2_raw = stats[:, it, col + 3:col + 4]
        p.opm("dve", lambda E, a=src_ap, o=m_raw: E.tensor_reduce(o, a, AX.X, OP.add))
        t2 = p.opm("act", lambda E, a=src_ap, o=s2_raw, sq=sq: E.activation(
            sq[:, :], a, AF.Square, accum_out=o))
        mcol = stats[:, it, col:col + 1]
        rcol = stats[:, it, col + 1:col + 2]
        p.wait("dve", t2)
        p.op("dve", lambda E, i=m_raw, o=mcol: E.tensor_scalar(o, i, 1.0 / D, 0.0, OP.mult, OP.bypass))
        p.op("dve", lambda E, m=mcol: E.tensor_tensor(small[:, 0:1], m, m, OP.mult))
        p.op("dve", lambda E, i=s2_raw: E.tensor_scalar(small[:, 1:2], i, 1.0 / D, small[:, 0:1],
                                                        OP.mult, OP.subtract))
        tv = p.opm("dve", lambda E: E.tensor_scalar(small[:, 1:2], small[:, 1:2], 1e-5, 0.0,
                                                    OP.add, OP.bypass))
        p.wait("act", tv)
        ts = p.opm("act", lambda E: E.activation(small[:, 2:3], small[:, 1:2], AF.Sqrt))
        p.wait("dve", ts)
        return p.opm("dve", lambda E, o=rcol: E.reciprocal(o, small[:, 2:3]))

    def transpose_tokmajor_to(get_src_tile, dstT, ntiles=NT, dst_tok0=0):
        p.wait("pe", p.cur("dve"))
        p.wait("pe", p.cur("act"))
        for it in range(ntiles):
            srct, tok = get_src_tile(it)
            p.wait("pe", tok)
            for g in range(2):
                pb = psum_bf[g]
                br = pbank_bf[g]
                p.wait_all("pe", br.r)
                for j in range(8):
                    k = g * 8 + j
                    p.op("pe", lambda E, srct=srct, k=k, j=j, pb=pb: E.transpose(
                        pb[:, j * 128:(j + 1) * 128], srct[:, k * 128:(k + 1) * 128],
                        ident_bf[:, :]))
                tq = p.milestone("pe")
                br.note_write(tq)
                p.wait("dve", tq)
                tc = p.opm("dve", lambda E, it=it, g=g, pb=pb, dstT=dstT, dst_tok0=dst_tok0: E.tensor_copy(
                    dstT[:, g * 8:(g + 1) * 8, dst_tok0 + it * 128:dst_tok0 + (it + 1) * 128],
                    pb[:, :].rearrange("p (j n) -> p j n", j=8)))
                br.note_read(tc)

    def emit_ln_to_xT(src_f32, scol, norm_buf, single_tile=False):
        p.wait("act", p.cur("dve"))
        p.wait("dve", p.cur("act"))
        for it in range(NT):
            ln_stats(src_f32[:, it, :], it, scol)

        def srctile(it):
            mcol = stats[:, it, scol:scol + 1]
            rcol = stats[:, it, scol + 1:scol + 2]
            dst = norm_buf if single_tile else norm_buf[:, it, :]
            if single_tile:
                p.wait("dve", p.cur("pe"))
            tn = p.opm("dve", lambda E, it=it, m=mcol, r=rcol, dst=dst: E.tensor_scalar(
                dst, src_f32[:, it, :], m, r, OP.subtract, OP.mult))
            return dst, tn
        transpose_tokmajor_to(srctile, xT)

    def load_bias(bias_dram_row, slot, width=D):
        p.wait_all("pool", bias_res[slot].r)
        t = p.dma("pool", bias_bf[:, slot, 0:width], bias_dram_row, stream=f"bias{slot}")
        bias_res[slot].note_write(t)
        return t

    def emit_proj(wblob, bias_slot, evac, dout=D, src=None, tok0=0, ntiles=NT):
        if src is None:
            src = xT
        p.wait("pe", p.cur("dve"))
        nsl_count = dout // NSL
        for ns in range(nsl_count):
            par = ns % 2
            wb = wbuf[:, par]
            p.wait_all("pool", wpar[par].r)
            tw = p.dma("pool", wb, wblob[ns, :, :, :], stream=f"wg{par}")
            wpar[par].note_write(tw)
            p.wait("pe", tw)
            if bias_slot is not None:
                p.wait("pe", bias_res[bias_slot].w)
            for it in range(ntiles):
                bank = psum[it % 4]
                bres = pbank[it % 4]
                p.wait_all("pe", bres.r)
                for k in range(KC):
                    p.op("pe", lambda E, it=it, k=k, bank=bank, wb=wb, src=src, tok0=tok0: E.matmul(
                        bank[:, 0:NSL], src[:, k, tok0 + it * 128:tok0 + (it + 1) * 128],
                        wb[:, k, :], start=(k == 0), stop=(bias_slot is None and k == KC - 1)))
                if bias_slot is not None:
                    tmm = p.opm("pe", lambda E, bank=bank, ns=ns, bs=bias_slot: E.matmul(
                        bank[:, 0:NSL], ones_bf[0:1, :],
                        bias_bf[0:1, bs, ns * NSL:(ns + 1) * NSL], start=False, stop=True))
                    bias_res[bias_slot].note_read(tmm)
                else:
                    tmm = p.milestone("pe")
                bres.note_write(tmm)
                wpar[par].note_read(tmm)
                tev = evac(it, ns, bank, tmm)
                bres.note_read(tev)

    t_x = [p.dma("sp", xres[:, it, :], X[it], stream="xio") for it in range(NT)]
    t_id = p.dma("sp", ident_f, IDENT, stream="misc")
    t_dec = p.dma("sp", decay_sb[:, :], DECAY, stream="misc")
    t_ones = p.opm("pool", lambda E: E.memset(ones_bf[:, :], 1.0))
    t_onesf = p.opm("pool", lambda E: E.memset(ones_f[:, :], 1.0))
    t_g0 = p.dma("sp", gtile0, F0_gate, stream="misc")
    t_g1 = p.dma("sp", gtile1, F1_gate, stream="misc")
    t_eb = p.dma("sp", small[0:1, 32:40], F0_eb, stream="misc")
    t_io = p.dma("sp", small[0:1, 40:48], IOTA8, stream="misc")
    p.wait("dve", t_id)
    tid_bf = p.opm("dve", lambda E: E.tensor_copy(ident_bf[:, :], ident_f))
    p.wait("pe", t_eb)
    p.wait("pe", t_io)
    p.wait("pe", t_onesf)
    tmb = p.opm("pe", lambda E: E.matmul(psum[4][:, 0:8], ones_f[0:1, :],
                                         small[0:1, 32:40], start=True, stop=True))
    pbank[4].note_write(tmb)
    p.wait("dve", tmb)
    te1 = p.opm("dve", lambda E: E.tensor_copy(ebias_t, psum[4][:, 0:8]))
    pbank[4].note_read(te1)
    tmb2 = p.opm("pe", lambda E: E.matmul(psum[5][:, 0:8], ones_f[0:1, :],
                                          small[0:1, 40:48], start=True, stop=True))
    pbank[5].note_write(tmb2)
    p.wait("dve", tmb2)
    te2 = p.opm("dve", lambda E: E.tensor_copy(iota_t, psum[5][:, 0:8]))
    pbank[5].note_read(te2)
    p.op("dve", lambda E: E.tensor_scalar(rev_t, iota_t, -1.0, 8.0, OP.mult, OP.add))

    ar.push()
    bufA = ar.take_bf16(NT, D)
    bufB = ar.take_bf16(NT, D)

    tb0 = load_bias(B_l0[1], 0)
    tb1 = load_bias(B_l0[2], 1)

    p.wait_all("dve", t_x)
    p.wait_all("act", t_x)
    p.wait("act", tid_bf)
    p.wait("pe", tid_bf)
    emit_ln_to_xT(xres, 0, bufA)

    def evac_relu_decay(dstbuf):
        def f(it, ns, bank, tmm):
            p.wait("dve", tmm)
            p.wait("dve", t_dec)
            return p.opm("dve", lambda E, it=it, ns=ns, bank=bank: E.tensor_scalar(
                dstbuf[:, it, ns * NSL:(ns + 1) * NSL], bank[:, 0:NSL],
                0.0, decay_sb[:, it:it + 1], OP.max, OP.mult))
        return f

    def evac_plain_bf(dstbuf):
        def f(it, ns, bank, tmm):
            p.wait("act", tmm)
            return p.opm("act", lambda E, it=it, ns=ns, bank=bank: E.activation(
                dstbuf[:, it, ns * NSL:(ns + 1) * NSL], bank[:, 0:NSL], AF.Copy))
        return f

    p.wait("pe", t_ones)
    emit_proj(W_l0["wk"], 0, evac_relu_decay(bufB))
    tb0 = load_bias(B_l0[0], 0)
    emit_proj(W_l0["wv"], 1, evac_plain_bf(bufA))

    p.wait("pe", p.cur("dve"))
    p.wait("pe", p.cur("act"))
    for b in range(B):
        for h in range(16):
            pair = b * 16 + h
            bank = psum[4 + (pair % 2)]
            bres = pbank[4 + (pair % 2)]
            p.wait_all("pe", bres.r)
            for j in range(NTB):
                it = b * NTB + j
                p.op("pe", lambda E, it=it, h=h, bank=bank, j=j: E.matmul(
                    bank[:, 0:128], bufB[:, it, h * 128:(h + 1) * 128],
                    bufA[:, it, h * 128:(h + 1) * 128], start=(j == 0), stop=False))
                p.op("pe", lambda E, it=it, h=h, bank=bank, j=j: E.matmul(
                    bank[:, 128:129], bufB[:, it, h * 128:(h + 1) * 128],
                    ones_bf[:, 0:1], start=False, stop=(j == NTB - 1)))
            tm = p.milestone("pe")
            bres.note_write(tm)
            p.wait("dve", tm)
            par2 = pair % 2
            stg_res2 = stgreg[par2]
            p.wait_all("dve", stg_res2.r)
            stg_kv = stg[:, 1100 + par2 * 80:1100 + par2 * 80 + 65].bitcast(BF16)[:, 0:129]
            tc = p.opm("dve", lambda E, bank=bank, o=stg_kv: E.tensor_copy(o, bank[:, 0:129]))
            bres.note_read(tc)
            stg_res2.note_write(tc)
            p.wait("pool", tc)
            td = p.dma("pool", kv_in.ap()[:, pair * 129:(pair + 1) * 129],
                       stg_kv, stream="kvb")
            stg_res2.note_read(td)
    p.wait("pool", td)
    if _NO_CC:
        for r in range(NCORES):
            p.op("pool", lambda E, r=r: E.dma_start(
                out=kv_out.ap()[r * 128:(r + 1) * 128, :], in_=kv_in.ap()), is_dma=True)
            p.milestone("pool", kind="cc", sem="cc0", inc=16)
        n_cc = 128
    else:
        p.op("pool", lambda E: E.collective_compute(
            "AllGather", OP.bypass, replica_groups=[list(range(NCORES))],
            ins=[kv_in.ap().opt()], outs=[kv_out.ap().opt()]), is_dma=True)
        p.milestone("pool", kind="cc", sem="cc0", inc=1)
        n_cc = 1

    emit_proj(W_l0["wq"], 0, evac_relu_decay(bufB))
    qT3 = bufA.rearrange("p t d -> p (t d)").rearrange("p (t h n) -> p t h n", t=NT, h=16)

    kvall = wbuf.rearrange("p a k n -> p (a k n)")[:, 0:32 * 129].rearrange(
        "p (q n) -> p q n", q=32)
    p.raw_wait("pool", "cc0", n_cc)
    p.wait_all("pool", [wpar[0].r[-1] if wpar[0].r else None,
                        wpar[1].r[-1] if wpar[1].r else None])
    p.wait("pool", p.cur("dve"))
    p.wait("pool", p.cur("act"))
    for qtr in range(4):
        cw = 1032
        dst = stg[:, 0:cw]
        for r in range(NCORES):
            t_acc = p.dma("pool", dst,
                          kv_out.ap()[r * 128:(r + 1) * 128,
                                      qtr * cw:(qtr + 1) * cw],
                          stream="kvb",
                          **({} if r == 0 else {"accum_op": OP.add}))
            p.wait("pool", t_acc)
        p.wait("dve", t_acc)
        tkc = p.opm("dve", lambda E, qtr=qtr, dst=dst, cw=cw: E.tensor_copy(
            kvall.rearrange("p q n -> p (q n)")[:, qtr * cw:(qtr + 1) * cw], dst))
        p.wait("pool", tkc)
    t_kv = p.cur("dve")
    p.wait("pe", t_kv)

    def qsrc(it):
        return bufB[:, it, :], None
    p.wait("pe", p.cur("dve"))
    for it in range(NT):
        for g in range(2):
            pb = psum_bf[g]
            br = pbank_bf[g]
            p.wait_all("pe", br.r)
            for j in range(8):
                h = g * 8 + j
                p.op("pe", lambda E, it=it, h=h, j=j, pb=pb: E.transpose(
                    pb[:, j * 128:(j + 1) * 128], bufB[:, it, h * 128:(h + 1) * 128],
                    ident_bf[:, :]))
            tq = p.milestone("pe")
            br.note_write(tq)
            p.wait("dve", tq)
            tc = p.opm("dve", lambda E, it=it, g=g, pb=pb: E.tensor_copy(
                qT3[:, it, g * 8:(g + 1) * 8, :], pb[:, :].rearrange("p (j n) -> p j n", j=8)))
            br.note_read(tc)


    p.wait("pe", p.cur("dve"))
    kv_read = []
    for b in range(B):
        for h in range(16):
            pair = b * 16 + h
            for j in range(NTB):
                it = b * NTB + j
                bi = (pair * NTB + j) % 4
                bank, bres = psum[bi], pbank[bi]
                p.wait_all("pe", bres.r)
                tm = p.opm("pe", lambda E, it=it, h=h, pair=pair, bank=bank: E.matmul(
                    bank[:, 0:129], qT3[:, it, h, :], kvall[:, pair, :],
                    start=True, stop=True))
                bres.note_write(tm)
                p.wait("dve", tm)
                c0, c1 = bi * 2, bi * 2 + 1
                p.op("dve", lambda E, bank=bank, c0=c0: E.tensor_scalar(
                    small[:, 8 + c0:9 + c0], bank[:, 128:129], EPS0, 0.0, OP.add, OP.bypass))
                p.op("dve", lambda E, c0=c0, c1=c1: E.reciprocal(
                    small[:, 8 + c1:9 + c1], small[:, 8 + c0:9 + c0]))
                tz = p.opm("dve", lambda E, it=it, h=h, bank=bank, c1=c1: E.tensor_scalar(
                    bufB[:, it, h * 128:(h + 1) * 128], bank[:, 0:128],
                    small[:, 8 + c1:9 + c1], 0.0, OP.mult, OP.bypass))
                bres.note_read(tz)
                kv_read.append(tz)

    t_pe_kv = p.cur("pe")
    wpar[0].note_read(t_pe_kv)
    wpar[1].note_read(t_pe_kv)
    p.wait_all("pool", kv_read[-4:])
    transpose_tokmajor_to(lambda it: (bufB[:, it, :], None), xT)
    tb1 = load_bias(B_l0[3], 1)

    p.barrier()
    ar.pop()
    ar.push()
    acc = ar.take_f32(NT, D)

    def evac_to_acc(it, ns, bank, tmm):
        p.wait("act", tmm)
        return p.opm("act", lambda E, it=it, ns=ns, bank=bank: E.activation(
            acc[:, it, ns * NSL:(ns + 1) * NSL], bank[:, 0:NSL], AF.Copy))

    emit_proj(W_l0["wo"], 1, evac_to_acc)

    p.wait("dve", p.cur("act"))
    for it in range(NT):
        ln_stats(acc[:, it, :], it, 4)
        mcol = stats[:, it, 4:5]
        rcol = stats[:, it, 5:6]
        tn = p.opm("dve", lambda E, it=it, m=mcol, r=rcol: E.tensor_scalar(
            stg[:, :], acc[:, it, :], m, r, OP.subtract, OP.mult))
        ta = p.opm("dve", lambda E, it=it: E.tensor_tensor(
            xres[:, it, :], xres[:, it, :], stg[:, :], OP.add))
        p.wait("act", ta)
    ar.pop()


    def emit_moe_ffn(w13, wdn, gate_sb, nhc, gate_af, scale_input,
                     seed_xln, router_softmax):
        HID = nhc * 128
        p.barrier()
        p.barrier(dst_engs=("sp",), src_engs=("pe", "act", "dve"))
        ar.push()
        lnstage = ar.take_bf16(D)
        emit_ln_to_xT(xres, 0, lnstage, single_tile=True)
        gtile = gate_sb
        rsc = ar.take_f32(4, 8)
        sq2 = ar.take_f32(D)
        xT32 = sq2.rearrange("p (k n) -> p k n", k=KC)
        acc = xres

        for it in range(NT):
            mcol = stats[:, it, 0:1]
            rcol = stats[:, it, 1:2]
            txn = p.opm("dve", lambda E, it=it, m=mcol, r=rcol: E.tensor_scalar(
                stg[:, :], xres[:, it, :], m, r, OP.subtract, OP.mult))
            p.wait("pe", txn)
            for rd in range(4):
                bank = psum[4 + rd % 2]
                bres = pbank[4 + rd % 2]
                p.wait_all("pe", bres.r)
                for j in range(4):
                    k = rd * 4 + j
                    p.op("pe", lambda E, k=k, j=j, bank=bank: E.transpose(
                        bank[:, j * 128:(j + 1) * 128], stg[:, k * 128:(k + 1) * 128],
                        ident_f))
                tq = p.milestone("pe")
                bres.note_write(tq)
                p.wait("dve", tq)
                tc = p.opm("dve", lambda E, rd=rd, bank=bank: E.tensor_copy(
                    xT32[:, rd * 4:(rd + 1) * 4, :],
                    bank[:, :].rearrange("p (j n) -> p j n", j=4)))
                bres.note_read(tc)
            p.wait("pe", p.cur("dve"))
            p.wait_all("pe", pbank[4].r)
            for k in range(KC):
                p.op("pe", lambda E, k=k: E.matmul(
                    psum[4][:, 0:8], xT32[:, k, :], gtile[:, k, :],
                    start=(k == 0), stop=(k == KC - 1)))
            tsc = p.milestone("pe")
            pbank[4].note_write(tsc)
            scores = rsc[:, 0, :]
            key = rsc[:, 1, :]
            tmp = rsc[:, 2, :]
            msel = rsc[:, 3, :]
            if router_softmax:
                p.wait("dve", tsc)
                p.op("dve", lambda E: E.tensor_reduce(small[:, 0:1], psum[4][:, 0:8],
                                                      AX.X, OP.max))
                p.op("dve", lambda E: E.tensor_scalar(small[:, 1:2], small[:, 0:1],
                                                      -1.0, 0.0, OP.mult, OP.bypass))
                tmx = p.cur("dve")
                p.wait("act", tmx)
                tex = p.opm("act", lambda E: E.activation(
                    key, psum[4][:, 0:8], AF.Exp, bias=small[:, 1:2],
                    accum_out=small[:, 2:3]))
                p.wait("dve", tex)
                p.op("dve", lambda E: E.reciprocal(small[:, 3:4], small[:, 2:3]))
                tsf = p.opm("dve", lambda E: E.tensor_scalar(
                    scores, key, small[:, 3:4], 0.0, OP.mult, OP.bypass))
                pbank[4].note_read(tex)
                p.op("dve", lambda E: E.tensor_copy(key, scores))
            else:
                p.wait("act", tsc)
                tsg = p.opm("act", lambda E: E.activation(scores, psum[4][:, 0:8], AF.Sigmoid))
                pbank[4].note_read(tsg)
                p.wait("dve", tsg)
                p.op("dve", lambda E: E.tensor_tensor(key, scores, ebias_t, OP.add))
            p.op("dve", lambda E: E.tensor_reduce(small[:, 0:1], key, AX.X, OP.max))
            p.op("dve", lambda E: E.tensor_scalar(tmp, key, small[:, 0:1], 0.0,
                                                  OP.is_ge, OP.bypass))
            p.op("dve", lambda E: E.tensor_tensor(tmp, tmp, rev_t, OP.mult))
            p.op("dve", lambda E: E.tensor_reduce(small[:, 1:2], tmp, AX.X, OP.max))
            p.op("dve", lambda E: E.tensor_scalar(small[:, 1:2], small[:, 1:2], -1.0, 8.0,
                                                  OP.mult, OP.add))
            p.op("dve", lambda E: E.tensor_scalar(msel, iota_t, small[:, 1:2], 0.0,
                                                  OP.is_equal, OP.bypass))
            p.op("dve", lambda E: E.tensor_scalar(tmp, msel, -1e9, 0.0, OP.mult, OP.bypass))
            p.op("dve", lambda E: E.tensor_tensor(tmp, key, tmp, OP.add))
            p.op("dve", lambda E: E.tensor_reduce(small[:, 0:1], tmp, AX.X, OP.max))
            p.op("dve", lambda E: E.tensor_scalar(tmp, tmp, small[:, 0:1], 0.0,
                                                  OP.is_ge, OP.bypass))
            p.op("dve", lambda E: E.tensor_tensor(tmp, tmp, rev_t, OP.mult))
            p.op("dve", lambda E: E.tensor_reduce(small[:, 1:2], tmp, AX.X, OP.max))
            p.op("dve", lambda E: E.tensor_scalar(small[:, 1:2], small[:, 1:2], -1.0, 8.0,
                                                  OP.mult, OP.add))
            p.op("dve", lambda E: E.tensor_scalar(tmp, iota_t, small[:, 1:2], 0.0,
                                                  OP.is_equal, OP.bypass))
            p.op("dve", lambda E: E.tensor_tensor(msel, msel, tmp, OP.add))
            p.op("dve", lambda E, it=it: E.tensor_tensor(ssel[:, it, :], scores, msel,
                                                         OP.mult))

        t_sp = []
        p.wait("sp", p.cur("dve"))
        for it in range(NT):
            t_sp.append(p.dma("sp", xspill[it], xres[:, it, :], stream="spill"))
        if seed_xln:
            for it in range(NT):
                p.wait("dve", t_sp[it])
                mcol = stats[:, it, 0:1]
                rcol = stats[:, it, 1:2]
                p.opm("dve", lambda E, it=it, m=mcol, r=rcol: E.tensor_scalar(
                    acc[:, it, :], xres[:, it, :], m, r, OP.subtract, OP.mult))

        h_fm = ar.take_bf16(nhc, 1024)
        wdn_sb = ar.take_bf16(nhc, 1024)
        sbc = ar.take_f32(1024)
        stgA = ar.take_f32(1024)
        stgB = ar.take_f32(1024)
        srow = ar.take_f32(1024)
        h_res = Res()
        wdn_res = Res()
        for m in range(9):
            e = m - 1
            scaled = m > 0
            if scaled:
                p.wait("pe", p.cur("dve"))
                for g in range(2):
                    bank = psum[4 + g]
                    bres = pbank[4 + g]
                    p.wait_all("pe", bres.r)
                    for j in range(4):
                        it = g * 4 + j
                        sap = ssel[:, it, e:e + 1]
                        p.op("pe", lambda E, j=j, bank=bank, sap=sap: E.transpose(
                            bank[0:1, j * 128:(j + 1) * 128], sap, ident_f))
                    tq = p.milestone("pe")
                    bres.note_write(tq)
                    p.wait("dve", tq)
                    tc = p.opm("dve", lambda E, g=g, bank=bank: E.tensor_copy(
                        srow[0:1, g * 512:(g + 1) * 512], bank[0:1, 0:512]))
                    bres.note_read(tc)
                p.wait("pe", p.cur("dve"))
                for g in range(2):
                    bank = psum[4 + g]
                    bres = pbank[4 + g]
                    p.wait_all("pe", bres.r)
                    tb = p.opm("pe", lambda E, g=g, bank=bank: E.matmul(
                        bank[:, :], ones_f[0:1, :], srow[0:1, g * 512:(g + 1) * 512],
                        start=True, stop=True))
                    bres.note_write(tb)
                    p.wait("dve", tb)
                    tc = p.opm("dve", lambda E, g=g, bank=bank: E.tensor_copy(
                        sbc[:, g * 512:(g + 1) * 512], bank[:, :]))
                    bres.note_read(tc)
            p.wait("pe", p.cur("dve"))
            for hc in range(nhc):
                par = hc % 2
                wb = wbuf[:, par]
                p.wait_all("pool", wpar[par].r)
                tw = p.dma("pool", wb[:, :, :], w13[m, hc], stream=f"wg{par}")
                wpar[par].note_write(tw)
                p.wait("pe", tw)
                for half in range(2):
                    bankG, brG = psum[half], pbank[half]
                    bankU, brU = psum[2 + half], pbank[2 + half]
                    p.wait_all("pe", brG.r)
                    p.wait_all("pe", brU.r)
                    for k in range(KC):
                        p.op("pe", lambda E, k=k, half=half, bankG=bankG, wb=wb: E.matmul(
                            bankG[:, :], wb[:, k, 0:128], xT[:, k, half * 512:(half + 1) * 512],
                            start=(k == 0), stop=(k == KC - 1)))
                    for k in range(KC):
                        p.op("pe", lambda E, k=k, half=half, bankU=bankU, wb=wb: E.matmul(
                            bankU[:, :], wb[:, k, 128:256], xT[:, k, half * 512:(half + 1) * 512],
                            start=(k == 0), stop=(k == KC - 1)))
                    tgu = p.milestone("pe")
                    brG.note_write(tgu)
                    brU.note_write(tgu)
                    wpar[par].note_read(tgu)
                    hs = half * 512
                    hdst = h_fm[:, hc, hs:hs + 512]
                    p.wait("dve", tgu)
                    p.wait("act", tgu)
                    p.wait_all("dve", h_res.r)
                    if scale_input and scaled:
                        p.op("dve", lambda E, bankG=bankG, hs=hs: E.tensor_tensor(
                            stgA[:, hs:hs + 512], bankG[:, :], sbc[:, hs:hs + 512], OP.mult))
                        tga = p.cur("dve")
                        p.wait("act", tga)
                        tact = p.opm("act", lambda E, hs=hs: E.activation(
                            stgA[:, hs:hs + 512], stgA[:, hs:hs + 512], gate_af))
                        p.wait("dve", tact)
                        p.op("dve", lambda E, bankU=bankU, hs=hs: E.tensor_tensor(
                            stgB[:, hs:hs + 512], bankU[:, :], sbc[:, hs:hs + 512], OP.mult))
                        th = p.opm("dve", lambda E, hdst=hdst, hs=hs: E.tensor_tensor(
                            hdst, stgA[:, hs:hs + 512], stgB[:, hs:hs + 512], OP.mult))
                    else:
                        tact = p.opm("act", lambda E, bankG=bankG, hs=hs: E.activation(
                            stgA[:, hs:hs + 512], bankG[:, :], gate_af))
                        p.wait("dve", tact)
                        if scaled:
                            p.op("dve", lambda E, bankU=bankU, hs=hs: E.tensor_tensor(
                                stgB[:, hs:hs + 512], stgA[:, hs:hs + 512], bankU[:, :],
                                OP.mult))
                            th = p.opm("dve", lambda E, hdst=hdst, hs=hs: E.tensor_tensor(
                                hdst, stgB[:, hs:hs + 512], sbc[:, hs:hs + 512], OP.mult))
                        else:
                            th = p.opm("dve", lambda E, hdst=hdst, bankU=bankU, hs=hs: E.tensor_tensor(
                                hdst, stgA[:, hs:hs + 512], bankU[:, :], OP.mult))
                    brG.note_read(th)
                    brU.note_read(th)
                    h_res.note_write(th)
            p.wait("pe", p.cur("dve"))
            for oh in range(2):
                p.wait_all("pool", wdn_res.r)
                twd = p.dma("pool", wdn_sb, wdn[m, oh], stream="w2")
                wdn_res.note_write(twd)
                p.wait("pe", twd)
                for it in range(NT):
                    for ns2 in range(2):
                        bi = (it % 2) * 2 + ns2
                        bank, bres = psum[bi], pbank[bi]
                        p.wait_all("pe", bres.r)
                        for hc in range(nhc):
                            p.op("pe", lambda E, hc=hc, it=it, ns2=ns2, bank=bank: E.matmul(
                                bank[:, :], h_fm[:, hc, it * 128:(it + 1) * 128],
                                wdn_sb[:, hc, ns2 * 512:(ns2 + 1) * 512],
                                start=(hc == 0), stop=(hc == nhc - 1)))
                        tdn = p.milestone("pe")
                        bres.note_write(tdn)
                        wdn_res.note_read(tdn)
                        h_res.note_read(tdn)
                        off = oh * 1024 + ns2 * 512
                        if m == 0 and not seed_xln:
                            p.wait("act", tdn)
                            if not seed_xln:
                                p.wait_all("act", [t_sp[it]])
                            tev = p.opm("act", lambda E, it=it, off=off, bank=bank: E.activation(
                                acc[:, it, off:off + 512], bank[:, :], AF.Copy))
                            bres.note_read(tev)
                            p.wait("dve", tev)
                        else:
                            p.wait("dve", tdn)
                            p.wait_all("dve", [t_sp[it]])
                            tev = p.opm("dve", lambda E, it=it, off=off, bank=bank: E.tensor_tensor(
                                acc[:, it, off:off + 512], acc[:, it, off:off + 512],
                                bank[:, :], OP.add))
                            bres.note_read(tev)

        p.barrier()
        t_rl_prev = None
        for it in range(NT):
            if t_rl_prev is not None:
                p.wait("sp", t_rl_prev)
                p.wait("act", t_rl_prev)
            p.wait("sp", p.cur("dve"))
            t_rl = p.dma("sp", stg[:, :], xspill[it], stream="spill")
            ln_stats(acc[:, it, :], it, 4, sq_scratch=sq2)
            mcol = stats[:, it, 4:5]
            rcol = stats[:, it, 5:6]
            p.opm("dve", lambda E, it=it, m=mcol, r=rcol: E.tensor_scalar(
                sq2[:, :], acc[:, it, :], m, r, OP.subtract, OP.mult))
            p.wait("dve", t_rl)
            tfin = p.opm("dve", lambda E, it=it: E.tensor_tensor(
                acc[:, it, :], sq2[:, :], stg[:, :], OP.add))
            t_rl_prev = tfin
        ar.pop()


    def emit_l1_attn():
        from concourse.bass import IndirectOffsetOnAxis
        SCALE = 1.0 / np.sqrt(128.0)
        p.barrier()
        p.barrier(dst_engs=("sp",), src_engs=("pe", "act", "dve"))
        ar.push()
        cosq = ar.take_f32(NTB, 32)
        sinq = ar.take_f32(NTB, 32)
        cosh = ar.take_f32(32)
        sinh = ar.take_f32(32)
        cosck = ar.take_f32(32)
        sinck = ar.take_f32(32)
        qnb = ar.take_f32(128)
        knb = ar.take_f32(128)
        cmask = ar.take_bf16(NTB, 258)
        lmask = ar.take_bf16(2, 256)
        pattern = ar.take_f32(8)
        haloidx = ar.take_f32(2).bitcast(mybir.dt.uint32)
        wc_bf = ar.take_bf16(KC)
        sinkT_bf = ar.take_bf16(16)
        sinkv_row = ar.take_bf16(D)
        onesrow = ar.take_bf16(256)

        tl = []
        tl.append(p.dma("sp", cosq, COSQ, stream="l1c"))
        tl.append(p.dma("sp", sinq, SINQ, stream="l1c"))
        tl.append(p.dma("sp", cosh, COSH, stream="l1c"))
        tl.append(p.dma("sp", sinh, SINH, stream="l1c"))
        tl.append(p.dma("sp", cosck[0:32, 0:32], COSCK, stream="l1c"))
        tl.append(p.dma("sp", sinck[0:32, 0:32], SINCK, stream="l1c"))
        tl.append(p.dma("sp", pattern, PATTERN, stream="l1c"))
        tl.append(p.dma("sp", haloidx, HALOIDX, stream="l1c"))
        tqn = p.dma("sp", stg[0:1, 0:128], L1_qn, stream="l1c")
        tkn = p.dma("sp", stg[0:1, 128:256], L1_kn, stream="l1c")
        bc_bf = small[0:1, 48:49].bitcast(BF16)[0:1, 0:1]
        tw = []
        tw.append(p.dma("pool", small[0:1, 48:49].bitcast(BF16)[0:1, 0:1], L1_bc, stream="l1w"))
        tw.append(p.dma("pool", cmask[:, :, 0:257], CMASK, stream="l1w"))
        tw.append(p.dma("pool", lmask, LMASK, stream="l1w"))
        tw.append(p.dma("pool", wc_bf, L1_wc.rearrange("p k u -> p (k u)"), stream="l1w"))
        tw.append(p.dma("pool", sinkT_bf, L1_sinkT, stream="l1w"))
        tw.append(p.dma("pool", sinkv_row[0:1, :], L1_sinkv, stream="l1w"))
        tor = p.opm("pool", lambda E: E.memset(onesrow[0:1, :], 1.0))
        p.wait_all("pe", tl)
        p.wait_all("pe", tw)
        p.wait("pe", tor)
        p.wait("pe", tqn)
        p.wait("pe", tkn)
        p.wait_all("pe", pbank[4].r)
        tb1 = p.opm("pe", lambda E: E.matmul(psum[4][:, 0:128], ones_f[0:1, :],
                                             stg[0:1, 0:128], start=True, stop=True))
        pbank[4].note_write(tb1)
        p.wait("dve", tb1)
        tc1 = p.opm("dve", lambda E: E.tensor_copy(qnb, psum[4][:, 0:128]))
        pbank[4].note_read(tc1)
        p.wait_all("pe", pbank[5].r)
        tb2 = p.opm("pe", lambda E: E.matmul(psum[5][:, 0:128], ones_f[0:1, :],
                                             stg[0:1, 128:256], start=True, stop=True))
        pbank[5].note_write(tb2)
        p.wait("dve", tb2)
        tc2 = p.opm("dve", lambda E: E.tensor_copy(knb, psum[5][:, 0:128]))
        pbank[5].note_read(tc2)

        xT_halo = ar.take_bf16(B, KC, 128)

        ar.push()
        lnst = ar.take_bf16(D)
        halo_tm = ar.take_bf16(B, D)
        emit_ln_to_xT(xres, 0, lnst, single_tile=True)
        tdd = None
        for b in range(B):
            it = b * NTB + (NTB - 1)
            mcol = stats[:, it, 0:1]
            rcol = stats[:, it, 1:2]
            if tdd is not None:
                p.wait("dve", tdd)
            ttl = p.opm("dve", lambda E, it=it, m=mcol, r=rcol: E.tensor_scalar(
                lnst, xres[:, it, :], m, r, OP.subtract, OP.mult))
            p.wait("pool", ttl)
            tdd = p.dma("pool", hx_in.ap()[b * 128:(b + 1) * 128, :], lnst, stream="hx")
        p.wait("pool", tdd)
        p.op("pool", lambda E: E.collective_compute(
            "AllGather", OP.bypass, replica_groups=[list(range(NCORES))],
            ins=[hx_in.ap().opt()], outs=[hx_out.ap().opt()]), is_dma=True)
        p.milestone("pool", kind="cc", sem="cc1", inc=1)

        ccpack = ar.take_f32(256)
        tokstage = ar.take_bf16(D)
        entT = ar.take_bf16(KC, 32)
        wcol = ar.take_f32(4)
        wmat_bf = small[:, 52:56].bitcast(BF16)
        p.wait_all("pool", wpar[0].r)
        twkv = p.dma("pool", wbuf[:, 0], L1_wkv, stream="wg0")
        wpar[0].note_write(twkv)
        tkv_b = load_bias(B_l1_kv, 1, width=256)
        for b in range(B):
            p.wait("pe", p.cur("dve"))
            p.wait_all("pe", pbank[4].r)
            for k in range(KC):
                p.op("pe", lambda E, k=k, b=b: E.matmul(
                    psum[4][0:1, 0:512], wc_bf[:, k:k + 1], xT[:, k, b * 512:(b + 1) * 512],
                    start=(k == 0), stop=False))
            p.op("pe", lambda E: E.matmul(
                psum[4][0:1, 0:256], bc_bf, onesrow[0:1, :], start=False, stop=False))
            tsc = p.opm("pe", lambda E: E.matmul(
                psum[4][0:1, 256:512], bc_bf, onesrow[0:1, :], start=False, stop=True))
            pbank[4].note_write(tsc)
            p.wait("dve", tsc)
            r3 = stg[0:1, 0:512].rearrange("o (c r) -> o c r", c=32)
            tcp = p.opm("dve", lambda E: E.tensor_copy(stg[0:1, 0:512], psum[4][0:1, 0:512]))
            pbank[4].note_read(tcp)
            p.op("dve", lambda E, r3=r3: E.tensor_reduce(
                stg[0:1, 512:544].rearrange("o (c u) -> o c u", c=32), r3, AX.X, OP.max))
            p.op("dve", lambda E, r3=r3: E.tensor_tensor(
                r3, r3, stg[0:1, 512:544].rearrange("o (c u) -> o c u", c=32).broadcast_to(
                    (1, 32, 16)), OP.subtract))
            tex = p.cur("dve")
            p.wait("act", tex)
            tex2 = p.opm("act", lambda E: E.activation(stg[0:1, 0:512], stg[0:1, 0:512], AF.Exp))
            p.wait("dve", tex2)
            p.op("dve", lambda E, r3=r3: E.tensor_reduce(
                stg[0:1, 512:544].rearrange("o (c u) -> o c u", c=32), r3, AX.X, OP.add))
            p.op("dve", lambda E: E.reciprocal(stg[0:1, 544:576], stg[0:1, 512:544]))
            tnw = p.opm("dve", lambda E, r3=r3: E.tensor_tensor(
                r3, r3, stg[0:1, 544:576].rearrange("o (c u) -> o c u", c=32).broadcast_to(
                    (1, 32, 16)), OP.mult))
            p.wait("pe", tnw)
            p.wait_all("pe", pbank[5].r)
            for jt in range(NTB):
                p.op("pe", lambda E, jt=jt: E.transpose(
                    psum[5][:, jt:jt + 1], stg[0:1, jt * 128:(jt + 1) * 128], ident_f[0:1, 0:1]))
            twc = p.milestone("pe")
            pbank[5].note_write(twc)
            p.wait("dve", twc)
            twc2 = p.opm("dve", lambda E: E.tensor_copy(wcol, psum[5][:, 0:4]))
            pbank[5].note_read(twc2)
            for jt in range(NTB):
                it = b * NTB + jt
                twm = p.opm("dve", lambda E, jt=jt: E.tensor_scalar(
                    wmat_bf, pattern, wcol[:, jt:jt + 1], 0.0, OP.mult, OP.bypass))
                mcol = stats[:, it, 0:1]
                rcol = stats[:, it, 1:2]
                ttk = p.opm("dve", lambda E, it=it, m=mcol, r=rcol: E.tensor_scalar(
                    tokstage, xres[:, it, :], m, r, OP.subtract, OP.mult))
                p.wait("pe", twm)
                p.wait("pe", ttk)
                for ns in range(4):
                    bank = psum[ns]
                    bres = pbank[ns]
                    p.wait_all("pe", bres.r)
                    tme = p.opm("pe", lambda E, ns=ns, bank=bank: E.matmul(
                        bank[0:8, 0:512], wmat_bf,
                        tokstage[:, ns * 512:(ns + 1) * 512], start=True, stop=True))
                    bres.note_write(tme)
                    p.wait("dve", tme)
                    tev = p.opm("dve", lambda E, ns=ns, bank=bank: E.tensor_copy(
                        stg[0:8, ns * 512:(ns + 1) * 512], bank[0:8, 0:512]))
                    bres.note_read(tev)
                p.wait("pe", p.cur("dve"))
                p.wait_all("pe", pbank[4].r)
                for k in range(KC):
                    p.op("pe", lambda E, k=k: E.transpose(
                        psum[4][:, k * 8:(k + 1) * 8], stg[0:8, k * 128:(k + 1) * 128],
                        ident_f[0:8, 0:8]))
                tet = p.milestone("pe")
                pbank[4].note_write(tet)
                p.wait("dve", tet)
                tet2 = p.opm("dve", lambda E, jt=jt: E.tensor_copy(
                    entT[:, :, jt * 8:(jt + 1) * 8],
                    psum[4][:, 0:128].rearrange("p (k e) -> p k e", k=KC)))
                pbank[4].note_read(tet2)
            p.wait("pe", p.cur("dve"))
            p.wait("pe", twkv)
            p.wait("pe", tkv_b)
            for half, boff in ((0, 0), (1, 128)):
                bank = psum[4]
                bres = pbank[4]
                p.wait_all("pe", bres.r)
                for k in range(KC):
                    p.op("pe", lambda E, k=k, boff=boff, bank=bank: E.matmul(
                        bank[0:32, 0:128], entT[:, k, :], wbuf[:, 0, k, boff:boff + 128],
                        start=(k == 0), stop=False))
                tbb = p.opm("pe", lambda E, boff=boff, bank=bank: E.matmul(
                    bank[0:32, 0:128], ones_bf[0:1, 0:32], bias_bf[0:1, 1, boff:boff + 128],
                    start=False, stop=True))
                bres.note_write(tbb)
                wpar[0].note_read(tbb)
                bias_res[1].note_read(tbb)
                p.wait("dve", tbb)
                if half == 0:
                    ckv = stg[0:32, 1024:1152]
                    tck = p.opm("dve", lambda E, bank=bank, ckv=ckv: E.tensor_copy(
                        ckv, bank[0:32, 0:128]))
                    bres.note_read(tck)
                    p.wait("act", tck)
                    tsq = p.opm("act", lambda E, ckv=ckv: E.activation(
                        stg[0:32, 1152:1280], ckv, AF.Square, accum_out=small[0:32, 60:61]))
                    p.wait("dve", tsq)
                    p.op("dve", lambda E: E.tensor_scalar(
                        small[0:32, 61:62], small[0:32, 60:61], 1.0 / 128, 1e-6, OP.mult, OP.add))
                    tvv = p.cur("dve")
                    p.wait("act", tvv)
                    tsr = p.opm("act", lambda E: E.activation(small[0:32, 62:63],
                                                              small[0:32, 61:62], AF.Sqrt))
                    p.wait("dve", tsr)
                    p.op("dve", lambda E: E.reciprocal(small[0:32, 63:64], small[0:32, 62:63]))
                    p.op("dve", lambda E, ckv=ckv: E.tensor_scalar(
                        ckv, ckv, small[0:32, 63:64], 0.0, OP.mult, OP.bypass))
                    p.op("dve", lambda E, ckv=ckv: E.tensor_tensor(ckv, ckv, knb[0:32, :], OP.mult))
                    x1 = ckv[:, 0:32]
                    x2 = ckv[:, 32:64]
                    p.op("dve", lambda E, x1=x1: E.tensor_tensor(
                        stg[0:32, 1280:1312], x1, cosck[0:32, 0:32], OP.mult))
                    p.op("dve", lambda E, x2=x2: E.tensor_tensor(
                        stg[0:32, 1312:1344], x2, sinck[0:32, 0:32], OP.mult))
                    p.op("dve", lambda E, x1=x1: E.tensor_tensor(
                        stg[0:32, 1344:1376], x1, sinck[0:32, 0:32], OP.mult))
                    p.op("dve", lambda E, x2=x2: E.tensor_tensor(
                        stg[0:32, 1376:1408], x2, cosck[0:32, 0:32], OP.mult))
                    p.op("dve", lambda E, x1=x1: E.tensor_tensor(
                        x1, stg[0:32, 1280:1312], stg[0:32, 1312:1344], OP.subtract))
                    p.op("dve", lambda E, x2=x2: E.tensor_tensor(
                        x2, stg[0:32, 1344:1376], stg[0:32, 1376:1408], OP.add))
                    p.opm("dve", lambda E, b=b, ckv=ckv: E.tensor_copy(
                        ccpack[b * 32:(b + 1) * 32, 0:128], ckv))
                else:
                    tpk = p.opm("dve", lambda E, b=b, bank=bank: E.tensor_copy(
                        ccpack[b * 32:(b + 1) * 32, 128:256], bank[0:32, 0:128]))
                    bres.note_read(tpk)
        p.wait("pool", p.cur("dve"))
        tcc = p.dma("pool", ccin.ap(), ccpack[0:64, :], stream="hx")
        p.wait("pool", tcc)
        p.op("pool", lambda E: E.collective_compute(
            "AllGather", OP.bypass, replica_groups=[list(range(NCORES))],
            ins=[ccin.ap().opt()], outs=[ccout.ap().opt()]), is_dma=True)
        p.milestone("pool", kind="cc", sem="cc2", inc=1)

        p.raw_wait("pool", "cc1", 1)
        tg_last = None
        for b in range(B):
            p.op("pool", lambda E, b=b: E.indirect_dma_start(
                out=halo_tm[:, b, :], out_offset=None, in_=hx_out.ap(),
                in_offset=IndirectOffsetOnAxis(ap=haloidx[:, b:b + 1], axis=0)),
                is_dma=True)
            tg_last = p.milestone("pool", kind="dma", sem="hx_d", inc=16)
        for b in range(B):
            p.wait("pe", tg_last)
            for g in range(2):
                pb = psum_bf[g]
                br = pbank_bf[g]
                p.wait_all("pe", br.r)
                for j in range(8):
                    k = g * 8 + j
                    p.op("pe", lambda E, b=b, k=k, j=j, pb=pb: E.transpose(
                        pb[:, j * 128:(j + 1) * 128], halo_tm[:, b, k * 128:(k + 1) * 128],
                        ident_bf[:, :]))
                tq2 = p.milestone("pe")
                br.note_write(tq2)
                p.wait("dve", tq2)
                tc3 = p.opm("dve", lambda E, b=b, g=g, pb=pb: E.tensor_copy(
                    xT_halo[:, b, g * 8:(g + 1) * 8, :],
                    pb[:, :].rearrange("p (j n) -> p j n", j=8)))
                br.note_read(tc3)
        p.barrier()
        ar.pop()

        wflat = wbuf.rearrange("p a k n -> p (a k n)")
        qTt = wflat[:, 0:2048].rearrange("p (h n) -> p h n", h=16)
        lkTab = wflat[:, 2048:6144].rearrange("p (a h n) -> p a h n", a=2, h=16)
        mst = wflat[:, 6144:8192]
        atn = stg[:, 1040:1556].bitcast(BF16).rearrange("p (s n) -> p s n", s=2)
        ckT_w = wflat[:, 0:0]
        mst_guard = [None]

        def load_ckcv(b, ckT_w, cvall_w):
            p.raw_wait("pool", "cc2", 1)
            p.wait("pool", p.cur("dve"))
            p.wait("pool", p.cur("pe"))
            v4 = ccout.ap().rearrange("(r b q) n -> r b q n", r=NCORES, b=B)
            for cch in range(2):
                for r2 in range(4):
                    r = cch * 4 + r2
                    tr1 = p.dma("pool", stg[r2 * 32:(r2 + 1) * 32, cch * 128:(cch + 1) * 128],
                                v4[r, b, :, 0:128], stream="hx")
                    tr2 = p.dma("pool", stg[r2 * 32:(r2 + 1) * 32,
                                            256 + cch * 128:256 + (cch + 1) * 128],
                                v4[r, b, :, 128:256], stream="hx")
            p.wait("dve", tr1)
            p.wait("dve", tr2)
            p.opm("dve", lambda E, cvall_w=cvall_w: E.tensor_copy(
                cvall_w, stg[:, 256:512].rearrange("p (c n) -> p c n", c=2)))
            tkb = p.opm("dve", lambda E: E.tensor_copy(
                stg[:, 512:640].bitcast(BF16), stg[:, 0:256]))
            p.wait("pe", tkb)
            for c in range(2):
                pb = psum_bf[c]
                br = pbank_bf[c]
                p.wait_all("pe", br.r)
                tt = p.opm("pe", lambda E, c=c, pb=pb: E.transpose(
                    pb[:, 0:128], stg[:, 512:640].bitcast(BF16)[:, c * 128:(c + 1) * 128],
                    ident_bf[:, :]))
                br.note_write(tt)
                p.wait("dve", tt)
                te = p.opm("dve", lambda E, c=c, pb=pb, ckT_w=ckT_w: E.tensor_copy(
                    ckT_w[:, c * 128:(c + 1) * 128], pb[:, 0:128]))
                br.note_read(te)
            p.wait("pe", te)

        def transpose_tile16(src_tile, dst16):
            for g in range(2):
                pb = psum_bf[g]
                br = pbank_bf[g]
                p.wait_all("pe", br.r)
                for j in range(8):
                    k = g * 8 + j
                    p.op("pe", lambda E, k=k, j=j, pb=pb, src_tile=src_tile: E.transpose(
                        pb[:, j * 128:(j + 1) * 128], src_tile[:, k * 128:(k + 1) * 128],
                        ident_bf[:, :]))
                tq = p.milestone("pe")
                br.note_write(tq)
                p.wait("dve", tq)
                tc = p.opm("dve", lambda E, g=g, pb=pb, dst16=dst16: E.tensor_copy(
                    dst16[:, g * 8:(g + 1) * 8, :],
                    pb[:, :].rearrange("p (j n) -> p j n", j=8)))
                br.note_read(tc)
            return tc

        def evac_lv(dstbuf, slot_of):
            def f(jt, ns, bank, tmm):
                slot = slot_of(jt)
                p.wait("act", tmm)
                return p.opm("act", lambda E, slot=slot, ns=ns, bank=bank: E.activation(
                    dstbuf[:, slot, ns * NSL:(ns + 1) * NSL], bank[:, 0:NSL], AF.Copy))
            return f

        def bulk_norm_rope(tile_bf, nb, cosT, sinT, rtmp0):
            rtmp = rtmp0
            t3 = tile_bf.rearrange("p (h d) -> p h d", h=16)
            p.wait("act", p.cur("dve"))
            p.wait("act", p.cur("pe"))
            p.wait("act", p.cur("pool"))
            ta1 = p.opm("act", lambda E, tile_bf=tile_bf: E.activation(
                stg[:, :], tile_bf, AF.Square))
            p.wait("dve", ta1)
            p.op("dve", lambda E: E.tensor_reduce(
                small[:, 0:16].rearrange("p (h u) -> p h u", h=16),
                stg[:, :].rearrange("p (h d) -> p h d", h=16), AX.X, OP.add))
            tsv = p.opm("dve", lambda E: E.tensor_scalar(
                small[:, 0:16], small[:, 0:16], 1.0 / 128, 1e-6, OP.mult, OP.add))
            p.wait("act", tsv)
            ta2 = p.opm("act", lambda E: E.activation(small[:, 16:32], small[:, 0:16], AF.Sqrt))
            p.wait("dve", ta2)
            p.op("dve", lambda E: E.reciprocal(small[:, 16:32], small[:, 16:32]))
            s3 = stg[:, :].rearrange("p (h d) -> p h d", h=16)
            p.op("dve", lambda E, t3=t3, s3=s3: E.tensor_tensor(
                s3, t3, small[:, 16:32].rearrange("p (h u) -> p h u", h=16).broadcast_to(
                    (128, 16, 128)), OP.mult))
            p.op("dve", lambda E, s3=s3, nb=nb: E.tensor_tensor(
                s3, s3, nb.rearrange("p (u d) -> p u d", u=1).broadcast_to((128, 16, 128)),
                OP.mult))
            x1 = s3[:, :, 0:32]
            x2 = s3[:, :, 32:64]
            cB = cosT.rearrange("p (u d) -> p u d", u=1).broadcast_to((128, 16, 32))
            sB = sinT.rearrange("p (u d) -> p u d", u=1).broadcast_to((128, 16, 32))
            p.op("dve", lambda E, x2=x2, sB=sB, rtmp=rtmp: E.tensor_tensor(rtmp, x2, sB, OP.mult))
            p.op("dve", lambda E, t3=t3, x1=x1, cB=cB: E.tensor_tensor(
                t3[:, :, 0:32], x1, cB, OP.mult))
            p.op("dve", lambda E, t3=t3, rtmp=rtmp: E.tensor_tensor(
                t3[:, :, 0:32], t3[:, :, 0:32], rtmp, OP.subtract))
            p.op("dve", lambda E, x2=x2, cB=cB, rtmp=rtmp: E.tensor_tensor(rtmp, x2, cB, OP.mult))
            p.op("dve", lambda E, t3=t3, x1=x1, sB=sB: E.tensor_tensor(
                t3[:, :, 32:64], x1, sB, OP.mult))
            p.op("dve", lambda E, t3=t3, rtmp=rtmp: E.tensor_tensor(
                t3[:, :, 32:64], t3[:, :, 32:64], rtmp, OP.add))
            return p.opm("dve", lambda E, t3=t3, s3=s3: E.tensor_copy(
                t3[:, :, 64:128], s3[:, :, 64:128]))

        for b in range(B):
            ar.push()
            qtm = ar.take_bf16(NTB, D)
            lktm = ar.take_bf16(5, D)
            lvtm = ar.take_bf16(5, D)
            scr = ar.take_bf16(1024)
            rtmp = scr[:, 0:512].rearrange("p (h n) -> p h n", h=16)
            atT = scr[:, 0:1024].rearrange("p (s n) -> p s n", s=2)
            ask = stg[:, 1812:1940].bitcast(BF16).rearrange("p (s n) -> p s n", s=2)
            ckcv = stg[:, 1556:1812].bitcast(BF16)
            ckT_w = ckcv[:, 0:256]
            cvall_w = ckcv[:, 256:512].rearrange("p (c n) -> p c n", c=2)
            tb_q = load_bias(B_l1[0], 0)
            p.wait("pe", tb_q)
            emit_proj(W_l1["wq"], 0, evac_lv(qtm, lambda j: j),
                      src=xT, tok0=b * 512, ntiles=NTB)
            tb_lk = load_bias(B_l1[1], 1)
            p.wait("pe", tb_lk)
            for jt in range(NTB):
                bulk_norm_rope(qtm[:, jt, :], qnb, cosq[:, jt, :], sinq[:, jt, :], rtmp)
            emit_proj(W_l1["wlk"], 1, evac_lv(lktm, lambda j: j + 1),
                      src=xT, tok0=b * 512, ntiles=NTB)
            emit_proj(W_l1["wlk"], 1, evac_lv(lktm, lambda j: 0),
                      src=xT_halo[:, b], tok0=0, ntiles=1)
            tb_lv = load_bias(B_l1[2], 0)
            p.wait("pe", tb_lv)
            for sl5 in range(5):
                bulk_norm_rope(lktm[:, sl5, :], knb,
                               cosh if sl5 == 0 else cosq[:, sl5 - 1, :],
                               sinh if sl5 == 0 else sinq[:, sl5 - 1, :], rtmp)
            emit_proj(W_l1["wlv"], 0, evac_lv(lvtm, lambda j: j + 1),
                      src=xT, tok0=b * 512, ntiles=NTB)
            emit_proj(W_l1["wlv"], 0, evac_lv(lvtm, lambda j: 0),
                      src=xT_halo[:, b], tok0=0, ntiles=1)
            p.barrier()
            load_ckcv(b, ckT_w, cvall_w)
            sres = [Res(), Res()]
            for jt in range(NTB):
                it = b * NTB + jt
                if mst_guard[0] is not None:
                    p.wait("act", mst_guard[0])
                p.wait("pe", p.cur("pool"))
                transpose_tile16(qtm[:, jt, :], qTt)
                if jt == 0:
                    transpose_tile16(lktm[:, 0, :], lkTab[:, 1])
                tlk = transpose_tile16(lktm[:, jt + 1, :], lkTab[:, jt % 2])
                p.wait("pe", tlk)
                lkT_prev = lkTab[:, (jt + 1) % 2]
                lkT_cur = lkTab[:, jt % 2]
                fe = {}

                def frontend_a(h, jt=jt, lkT_prev=lkT_prev, lkT_cur=lkT_cur):
                    sl = h % 2
                    bank_s = psum[sl]
                    bres_s = pbank[sl]
                    p.wait_all("pe", bres_s.r)
                    p.op("pe", lambda E, h=h, bank_s=bank_s: E.matmul(
                        bank_s[:, 0:256], qTt[:, h, :], ckT_w, start=True, stop=False))
                    tcs = p.opm("pe", lambda E, h=h, bank_s=bank_s: E.matmul(
                        bank_s[:, 256:257], qTt[:, h, :], sinkT_bf[:, h:h + 1],
                        start=False, stop=True))
                    bres_s.note_write(tcs)
                    bank_l = psum[2 + sl]
                    bres_l = pbank[2 + sl]
                    p.wait_all("pe", bres_l.r)
                    p.op("pe", lambda E, h=h, bank_l=bank_l, lkT_prev=lkT_prev: E.matmul(
                        bank_l[:, 0:128], qTt[:, h, :], lkT_prev[:, h, :],
                        start=True, stop=False))
                    tls = p.opm("pe", lambda E, h=h, bank_l=bank_l, lkT_cur=lkT_cur: E.matmul(
                        bank_l[:, 128:256], qTt[:, h, :], lkT_cur[:, h, :],
                        start=False, stop=True))
                    bres_l.note_write(tls)
                    fe[(h, "mm")] = (tcs, tls)

                def frontend_s(h, jt=jt):
                    sl = h % 2
                    bank_s = psum[sl]
                    bres_s = pbank[sl]
                    bank_l = psum[2 + sl]
                    bres_l = pbank[2 + sl]
                    tcs, tls = fe[(h, "mm")]
                    if h >= 2:
                        p.wait("dve", fe[h - 2])
                    so = sl * 520
                    sc0, sc1 = 40 + sl * 6, 42 + sl * 6
                    sm3 = stg[:, so:so + 514].rearrange("p (w n) -> p w n", w=2)
                    p.wait("dve", tcs)
                    p.wait("dve", tls)
                    p.op("dve", lambda E, bank_s=bank_s, sm3=sm3, jt=jt: E.tensor_tensor(
                        sm3[:, 0:1, :], bank_s[:, 0:257].rearrange("p (u n) -> p u n", u=1),
                        cmask[:, jt, 0:257].rearrange("p (u n) -> p u n", u=1), OP.add))
                    tml = p.opm("dve", lambda E, bank_l=bank_l, sm3=sm3, jt=jt: E.tensor_tensor(
                        sm3[:, 1:2, 0:256], bank_l[:, 0:256].rearrange("p (u n) -> p u n", u=1),
                        lmask[:, min(jt, 1), :].rearrange("p (u n) -> p u n", u=1), OP.add))
                    bres_s.note_read(tml)
                    bres_l.note_read(tml)
                    p.op("dve", lambda E, so=so: E.tensor_scalar(
                        stg[:, so + 513:so + 514], stg[:, so + 512:so + 513], 0.0, -1e9,
                        OP.mult, OP.add))
                    p.op("dve", lambda E, sm3=sm3, sc0=sc0: E.tensor_reduce(
                        small[:, sc0:sc0 + 2].rearrange("p (w u) -> p w u", w=2), sm3,
                        AX.X, OP.max))
                    tmx = p.opm("dve", lambda E, sc0=sc0: E.tensor_scalar(
                        small[:, sc0:sc0 + 2], small[:, sc0:sc0 + 2], -SCALE, 0.0,
                        OP.mult, OP.bypass))
                    p.wait("act", tmx)
                    p.op("act", lambda E, so=so, sc0=sc0: E.activation(
                        stg[:, so:so + 257], stg[:, so:so + 257], AF.Exp, scale=SCALE,
                        bias=small[:, sc0:sc0 + 1]))
                    tex = p.opm("act", lambda E, so=so, sc0=sc0: E.activation(
                        stg[:, so + 257:so + 514], stg[:, so + 257:so + 514], AF.Exp,
                        scale=SCALE, bias=small[:, sc0 + 1:sc0 + 2]))
                    p.wait("dve", tex)
                    p.op("dve", lambda E, sm3=sm3, sc1=sc1: E.tensor_reduce(
                        small[:, sc1:sc1 + 2].rearrange("p (w u) -> p w u", w=2), sm3,
                        AX.X, OP.add))
                    trc = p.opm("dve", lambda E, sc1=sc1: E.reciprocal(
                        small[:, sc1 + 2:sc1 + 4], small[:, sc1:sc1 + 2]))
                    p.wait("pool", trc)
                    p.wait_all("pool", sres[sl].r)
                    tat = p.opm("pool", lambda E, sl=sl, sm3=sm3, sc1=sc1: E.tensor_tensor(
                        atn[:, sl, 0:514].rearrange("p (w n) -> p w n", w=2), sm3,
                        small[:, sc1 + 2:sc1 + 4].rearrange(
                            "p (w u) -> p w u", w=2).broadcast_to((128, 2, 257)), OP.mult))
                    fe[h] = tat

                def backend(h, jt=jt, lvtm=lvtm, atT=atT, ask=ask):
                    sl = h % 2
                    pb = psum_bf[sl]
                    br = pbank_bf[sl]
                    tat = fe[h]
                    p.wait("pe", tat)
                    p.wait_all("pe", br.r)
                    p.op("pe", lambda E, sl=sl, pb=pb: E.transpose(
                        pb[:, 0:128], atn[:, sl, 0:128], ident_bf[:, :]))
                    p.op("pe", lambda E, sl=sl, pb=pb: E.transpose(
                        pb[:, 128:256], atn[:, sl, 128:256], ident_bf[:, :]))
                    p.op("pe", lambda E, sl=sl, pb=pb: E.transpose(
                        pb[:, 256:384], atn[:, sl, 257:385], ident_bf[:, :]))
                    p.op("pe", lambda E, sl=sl, pb=pb: E.transpose(
                        pb[:, 384:512], atn[:, sl, 385:513], ident_bf[:, :]))
                    ts5 = p.opm("pe", lambda E, sl=sl, pb=pb: E.transpose(
                        pb[0:1, 512:640], atn[:, sl, 256:257], ident_bf[:, :]))
                    br.note_write(ts5)
                    p.wait("act", ts5)
                    p.op("act", lambda E, sl=sl, pb=pb, atT=atT: E.activation(
                        atT[:, sl, :], pb[:, 0:512], AF.Copy))
                    tev5 = p.opm("act", lambda E, sl=sl, pb=pb, ask=ask: E.activation(
                        ask[0:1, sl, :], pb[0:1, 512:640], AF.Copy))
                    br.note_read(tev5)
                    sres[sl].note_read(ts5)
                    bank_c = psum[4 + sl]
                    bres_c = pbank[4 + sl]
                    p.wait("pe", tev5)
                    p.wait_all("pe", bres_c.r)
                    p.op("pe", lambda E, sl=sl, bank_c=bank_c, atT=atT: E.matmul(
                        bank_c[:, 0:128], atT[:, sl, 0:128], cvall_w[:, 0, :],
                        start=True, stop=False))
                    p.op("pe", lambda E, sl=sl, bank_c=bank_c, atT=atT: E.matmul(
                        bank_c[:, 0:128], atT[:, sl, 128:256], cvall_w[:, 1, :],
                        start=False, stop=False))
                    p.op("pe", lambda E, h=h, sl=sl, bank_c=bank_c, ask=ask: E.matmul(
                        bank_c[:, 0:128], ask[0:1, sl, :], sinkv_row[0:1, h * 128:(h + 1) * 128],
                        start=False, stop=False))
                    p.op("pe", lambda E, h=h, sl=sl, bank_c=bank_c, lvtm=lvtm, jt=jt: E.matmul(
                        bank_c[:, 0:128], atT[:, sl, 256:384], lvtm[:, jt, h * 128:(h + 1) * 128],
                        start=False, stop=False))
                    tcx = p.opm("pe", lambda E, h=h, sl=sl, bank_c=bank_c, lvtm=lvtm, jt=jt: E.matmul(
                        bank_c[:, 0:128], atT[:, sl, 384:512],
                        lvtm[:, jt + 1, h * 128:(h + 1) * 128], start=False, stop=True))
                    bres_c.note_write(tcx)
                    p.wait("act", tcx)
                    tvm = p.opm("act", lambda E, h=h, bank_c=bank_c: E.activation(
                        mst[:, h * 128:(h + 1) * 128], bank_c[:, 0:128], AF.Copy, scale=0.5))
                    bres_c.note_read(tvm)

                for h in range(18):
                    if h >= 1 and h - 1 < 16:
                        frontend_s(h - 1)
                    if h < 16:
                        frontend_a(h)
                    if h >= 2:
                        backend(h - 2)
                p.wait("pe", p.cur("act"))
                for g in range(2):
                    pb = psum_bf[g]
                    br = pbank_bf[g]
                    p.wait_all("pe", br.r)
                    for j in range(8):
                        k = g * 8 + j
                        p.op("pe", lambda E, k=k, j=j, pb=pb: E.transpose(
                            pb[:, j * 128:(j + 1) * 128], mst[:, k * 128:(k + 1) * 128],
                            ident_bf[:, :]))
                    tq6 = p.milestone("pe")
                    br.note_write(tq6)
                    p.wait("dve", tq6)
                    tc6 = p.opm("dve", lambda E, b=b, jt=jt, g=g, pb=pb: E.tensor_copy(
                        xT[:, g * 8:(g + 1) * 8, b * 512 + jt * 128:b * 512 + (jt + 1) * 128],
                        pb[:, :].rearrange("p (j n) -> p j n", j=8)))
                    br.note_read(tc6)
                mst_guard[0] = p.cur("pe")
            p.barrier()
            ar.pop()

        acc_bf = ar.take_bf16(NT, D)
        tb_o = load_bias(B_l1[3], 1)
        p.wait("pe", tb_o)

        def evac_accbf(it, ns, bank, tmm):
            p.wait("act", tmm)
            return p.opm("act", lambda E, it=it, ns=ns, bank=bank: E.activation(
                acc_bf[:, it, ns * NSL:(ns + 1) * NSL], bank[:, 0:NSL], AF.Copy))

        emit_proj(W_l1["wo"], 1, evac_accbf)
        p.wait("dve", p.cur("act"))
        for it in range(NT):
            ln_stats(acc_bf[:, it, :], it, 4)
            mcol = stats[:, it, 4:5]
            rcol = stats[:, it, 5:6]
            p.opm("dve", lambda E, it=it, m=mcol, r=rcol: E.tensor_scalar(
                stg[:, :], acc_bf[:, it, :], m, r, OP.subtract, OP.mult))
            ta = p.opm("dve", lambda E, it=it: E.tensor_tensor(
                xres[:, it, :], xres[:, it, :], stg[:, :], OP.add))
            p.wait("act", ta)
        ar.pop()

    if DEBUG_TAP != "l0_attn":
        emit_moe_ffn(F0_w13, F0_wdn, gtile0, nhc=8, gate_af=AF.Silu,
                     scale_input=True, seed_xln=False, router_softmax=False)
    if DEBUG_TAP not in ("l0_attn", "l0_ffn"):
        emit_l1_attn()
    if DEBUG_TAP not in ("l0_attn", "l0_ffn", "l1_attn"):
        emit_moe_ffn(F1_w13, F1_wdn, gtile1, nhc=11, gate_af=AF.Sigmoid,
                     scale_input=False, seed_xln=True, router_softmax=True)

    t_done = p.cur("dve")
    p.wait("sp", t_done)
    t_outs = []
    tgt = dbg if DEBUG_TAP is not None else OUT
    for it in range(NT):
        t_outs.append(p.dma("sp", tgt[it], xres[:, it, :], stream="xio"))
    if DEBUG_TAP is not None:
        for it in range(NT):
            t_outs.append(p.dma("sp", OUT[it], xres[:, it, :], stream="xio"))
    p.wait_all("sp", t_outs)

    with nc.Block() as block:
        p.emit(nc, block, stack)
    stack.close()
    return nc, p



def _prep_inputs(inputs):
    x = np.asarray(inputs["L0_attn_x"])
    ident = np.eye(128, dtype=np.float32)
    pos = np.arange(S, dtype=np.float32)
    decay_full = np.exp(-pos / S).astype(np.float32)

    wq = tile_proj_weight(np.ascontiguousarray(np.asarray(inputs["L0_attn_Wq"]).T))
    wk = tile_proj_weight(np.ascontiguousarray(np.asarray(inputs["L0_attn_Wk"]).T))
    wv = tile_proj_weight(np.ascontiguousarray(np.asarray(inputs["L0_attn_Wv"]).T))
    wo = tile_proj_weight(np.ascontiguousarray(np.asarray(inputs["L0_attn_Wo"]).T))
    l0_bias = np.stack([np.asarray(inputs[f"L0_attn_{k}"])[None, :] for k in
                        ("bq", "bk", "bv", "bo")])

    def moe_blobs(w1s, w3s, w2s, sw1, sw3, sw2):
        m13, mdn = [], []
        for i in range(9):
            w1m = sw1 if i == 0 else w1s[i - 1]
            w3m = sw3 if i == 0 else w3s[i - 1]
            w2m = sw2 if i == 0 else w2s[i - 1]
            a = tile_proj_weight(np.ascontiguousarray(w1m.T), 128)
            b = tile_proj_weight(np.ascontiguousarray(w3m.T), 128)
            m13.append(np.concatenate([a, b], axis=-1))
            w2t = np.ascontiguousarray(w2m.T)
            nhc = w2t.shape[0] // 128
            halves = [np.ascontiguousarray(
                w2t[:, oh * 1024:(oh + 1) * 1024].reshape(nhc, 128, 1024).transpose(1, 0, 2))
                for oh in range(2)]
            mdn.append(np.stack(halves))
        return np.ascontiguousarray(np.stack(m13)), np.ascontiguousarray(np.stack(mdn))

    f0_w13, f0_wdn = moe_blobs(
        np.asarray(inputs["L0_ffn_w1"]), np.asarray(inputs["L0_ffn_w3"]),
        np.asarray(inputs["L0_ffn_w2"]), np.asarray(inputs["L0_ffn_shared_w1"]),
        np.asarray(inputs["L0_ffn_shared_w3"]), np.asarray(inputs["L0_ffn_shared_w2"]))
    f0_gate = np.ascontiguousarray(
        np.asarray(inputs["L0_ffn_gate"]).T.reshape(KC, 128, 8).transpose(1, 0, 2))
    f0_eb = np.asarray(inputs["L0_ffn_expert_bias"])[None, :]
    iota8 = np.arange(8, dtype=np.float32)[None, :]

    l1_wq = tile_proj_weight(np.ascontiguousarray(np.asarray(inputs["L1_attn_Wq"])))
    l1_wlk = tile_proj_weight(np.ascontiguousarray(np.asarray(inputs["L1_attn_Wlk"])))
    l1_wlv = tile_proj_weight(np.ascontiguousarray(np.asarray(inputs["L1_attn_Wlv"])))
    l1_wo = tile_proj_weight(np.ascontiguousarray(np.asarray(inputs["L1_attn_Wo"])))
    l1_bias = np.stack([np.asarray(inputs[f"L1_attn_{k}"])[None, :] for k in
                        ("bq", "blk", "blv", "bo")])
    l1_wc = np.ascontiguousarray(
        np.asarray(inputs["L1_attn_Wc"]).reshape(KC, 128, 1).transpose(1, 0, 2))
    l1_bc = np.asarray(inputs["L1_attn_bc"]).reshape(1, 1)
    l1_wkv = np.ascontiguousarray(np.concatenate(
        [np.asarray(inputs["L1_attn_Wk"]).reshape(KC, 128, 128),
         np.asarray(inputs["L1_attn_Wv"]).reshape(KC, 128, 128)], -1).transpose(1, 0, 2))
    l1_bkv = np.concatenate([np.asarray(inputs["L1_attn_bk"]),
                             np.asarray(inputs["L1_attn_bv"])])[None, :]
    l1_qn = np.asarray(inputs["L1_attn_qn_w"])[None, :]
    l1_kn = np.asarray(inputs["L1_attn_kn_w"])[None, :]
    l1_sinkT = np.ascontiguousarray(np.asarray(inputs["L1_attn_sink_k"]).T)
    l1_sinkv = np.ascontiguousarray(np.asarray(inputs["L1_attn_sink_v"]).reshape(1, D))
    f1_w13, f1_wdn = moe_blobs(
        np.asarray(inputs["L1_ffn_w_gate"]), np.asarray(inputs["L1_ffn_w_up"]),
        np.asarray(inputs["L1_ffn_w_down"]), np.asarray(inputs["L1_ffn_shared_gate"]),
        np.asarray(inputs["L1_ffn_shared_up"]), np.asarray(inputs["L1_ffn_shared_down"]))
    f1_gate = np.ascontiguousarray(
        np.asarray(inputs["L1_ffn_router_w"]).T.reshape(KC, 128, 8).transpose(1, 0, 2))
    pattern = (np.arange(128)[:, None] // 16 == np.arange(8)[None, :]).astype(np.float32)
    inv_freq = (1.0 / 10000.0 ** (np.arange(32, dtype=np.float32) * 2.0 / 64.0))

    def rope_tabs(pos):
        ang = pos.astype(np.float32)[:, None] * inv_freq[None, :]
        return np.cos(ang).astype(np.float32), np.sin(ang).astype(np.float32)

    maps = []
    for c in range(NCORES):
        sl = slice(c * SPC, (c + 1) * SPC)
        xc = np.ascontiguousarray(
            np.concatenate([x[0, sl], x[1, sl]], axis=0)).reshape(NT, 128, D)
        dec = np.ascontiguousarray(np.tile(decay_full[sl], 2).reshape(NT, 128).T)
        pos_c = np.arange(c * SPC, (c + 1) * SPC)
        cq, sq = rope_tabs(pos_c)
        cosq_c = np.ascontiguousarray(cq.reshape(NTB, 128, 32).transpose(1, 0, 2))
        sinq_c = np.ascontiguousarray(sq.reshape(NTB, 128, 32).transpose(1, 0, 2))
        hpos = np.maximum(np.arange(c * SPC - 128, c * SPC), 0)
        cosh_c, sinh_c = rope_tabs(hpos)
        bends = (c * 32 + np.arange(32)) * 16 + 15
        cosck_c, sinck_c = rope_tabs(bends)
        cm = np.full((128, NTB, 257), -1e9, np.float32)
        cm[:, :, 256] = 0.0
        allends = np.arange(256) * 16 + 15
        for j in range(NTB):
            posj = c * SPC + j * 128 + np.arange(128)
            cm[:, j, 0:256] = np.where(allends[None, :] <= posj[:, None], 0.0, -1e9)
        cmask_c = np.ascontiguousarray(cm)
        lm = np.full((128, 2, 256), -1e9, np.float32)
        ii = np.arange(128)[:, None]
        kk = np.arange(256)[None, :]
        valid = (kk >= ii) & (kk <= ii + 128)
        lm[:, 1, :] = np.where(valid, 0.0, -1e9)
        v0 = valid & ((kk >= 128) if c == 0 else np.ones_like(valid))
        lm[:, 0, :] = np.where(v0, 0.0, -1e9)
        lmask_c = np.ascontiguousarray(lm)
        haloidx_c = np.ascontiguousarray(
            (np.arange(128)[:, None] + np.array([0, 128])[None, :] +
             (((c - 1) % NCORES) * 256)).astype(np.uint32))
        maps.append({
            "x": xc, "ident": ident, "decay": dec,
            "l0_wq": wq, "l0_wk": wk, "l0_wv": wv, "l0_wo": wo,
            "l0_bias": l0_bias, "iota8": iota8,
            "f0_w13": f0_w13, "f0_wdn": f0_wdn, "f0_gate": f0_gate, "f0_eb": f0_eb,
            "l1_wq": l1_wq, "l1_wlk": l1_wlk, "l1_wlv": l1_wlv, "l1_wo": l1_wo,
            "l1_bias": l1_bias, "l1_wc": l1_wc, "l1_bc": l1_bc, "l1_wkv": l1_wkv,
            "l1_bkv": l1_bkv, "l1_qn": l1_qn, "l1_kn": l1_kn,
            "l1_sinkT": l1_sinkT, "l1_sinkv": l1_sinkv,
            "f1_w13": f1_w13, "f1_wdn": f1_wdn, "f1_gate": f1_gate,
            "cosq": cosq_c, "sinq": sinq_c, "cosh": cosh_c, "sinh": sinh_c,
            "cosck": cosck_c, "sinck": sinck_c,
            "cmask": cmask_c, "lmask": lmask_c, "pattern": pattern,
            "haloidx": haloidx_c,
        })
    return maps


def kernel(**inputs):
    maps = _prep_inputs(inputs)
    nc, _ = build_program()
    res = run_bass_kernel_spmd(nc, maps, core_ids=list(range(NCORES)))
    out = np.empty((B, S, D), dtype=np.float32)
    key = "dbg" if DEBUG_TAP is not None else "out"
    for c in range(NCORES):
        oc = res.results[c][key].reshape(TPC, D)
        out[0, c * SPC:(c + 1) * SPC] = oc[:SPC]
        out[1, c * SPC:(c + 1) * SPC] = oc[SPC:]
    return out


# version 10
# speedup vs baseline: 1.1064x; 1.0180x over previous
import numpy as np
import contextlib
import concourse.bass as bass
import concourse.mybir as mybir
from concourse.bass_utils import run_bass_kernel_spmd
from concourse.alu_op_type import AluOpType as OP

F32 = mybir.dt.float32
BF16 = mybir.dt.bfloat16
AF = mybir.ActivationFunctionType
AX = mybir.AxisListType

NCORES = 8
TPC = 1024
NT = 8
NTB = 4
D = 2048
KC = 16
S = 4096
B = 2
SPC = 512
EPS0 = 1e-6
NSL = 256

ENGS = ("pe", "act", "dve", "pool", "sp")
_BLOCK_ATTR = {"pe": "tensor", "act": "scalar", "dve": "vector", "pool": "gpsimd", "sp": "sync"}

DEBUG_TAP = None
_NO_CC = False


class Prog:
    def __init__(self):
        self.items = {e: [] for e in ENGS}
        self.semcount = {}
        self.last_wait = {e: {} for e in ENGS}
        self.own = {}

    def _bump(self, sem, inc):
        self.semcount[sem] = self.semcount.get(sem, 0) + inc
        return self.semcount[sem]

    SELF_CHAIN = ("dve", "act", "pool")

    def op(self, eng, fn, is_dma=False):
        if eng in self.SELF_CHAIN and not is_dma:
            sem = eng + "_c"
            cnt = self.semcount.get(sem, 0)
            if cnt and self.last_wait[eng].get(sem, 0) < cnt:
                self.last_wait[eng][sem] = cnt
                self.items[eng].append(["w", sem, cnt])
            self.items[eng].append(["op", fn, [], is_dma])
            self.milestone(eng)
            return
        self.items[eng].append(["op", fn, [], is_dma])

    def opm(self, eng, fn, kind="cmp", sem=None, inc=None):
        self.op(eng, fn)
        return self.milestone(eng, kind=kind, sem=sem, inc=inc)

    def dma(self, eng, out_ap, in_ap, stream=None, **kw):
        self.op(eng, lambda E, o=out_ap, i=in_ap, kw=kw: E.dma_start(out=o, in_=i, **kw),
                is_dma=True)
        return self.milestone(eng, kind="dma", sem=(stream or eng) + "_d", inc=16)

    def milestone(self, eng, kind="cmp", sem=None, inc=None):
        sem = sem or (eng + "_c")
        inc = inc if inc is not None else 1
        if kind == "cmp":
            self.own.setdefault(sem, eng)
        for it in reversed(self.items[eng]):
            if it[0] == "op":
                if it[3] and kind != "cc" and inc % 16 != 0:
                    inc = 16
                    if kind == "cmp":
                        kind = "dma"
                        sem = sem if sem.endswith("_d") else sem[:-2] + "_d"
                v = self._bump(sem, inc)
                for j, (s0, i0) in enumerate(it[2]):
                    if s0 == sem:
                        it[2][j] = (sem, i0 + inc)
                        return (sem, v, kind)
                it[2].append((sem, inc))
                return (sem, v, kind)
        raise RuntimeError(f"milestone on empty engine {eng}")

    def cur(self, eng):
        for it in reversed(self.items[eng]):
            if it[0] == "op":
                if it[3] and it[2]:
                    s0 = it[2][0][0]
                    return (s0, self.semcount[s0], "dma")
                if it[2]:
                    sem = eng + "_c"
                    for s, _ in it[2]:
                        if s == sem:
                            return (sem, self.semcount[sem], "cmp")
                return self.milestone(eng)
        return None

    def barrier(self, dst_engs=("pe", "act", "dve", "pool"), src_engs=("pe", "act", "dve", "pool")):
        toks = [self.cur(e) for e in src_engs]
        toks += [(s, v, "dma") for s, v in self.semcount.items() if s.endswith("_d")]
        for d in dst_engs:
            self.wait_all(d, toks)

    def wait(self, eng, tok):
        if tok is None:
            return
        sem, val, kind = tok
        if kind == "cmp" and self.own.get(sem) == eng:
            return
        if kind in ("dma", "cc"):
            val = self.semcount[sem]
        if self.last_wait[eng].get(sem, 0) >= val:
            return
        self.last_wait[eng][sem] = val
        self.items[eng].append(["w", sem, val])

    def wait_all(self, eng, toks):
        for t in list(toks):
            self.wait(eng, t)

    def cc(self, fn, sem):
        self.op("pool", fn, is_dma=True)
        return self.milestone("pool", kind="cc", sem=sem, inc=1)

    def raw_wait(self, eng, sem, val):
        self.items[eng].append(["w", sem, val])

    def check_deadlock(self):
        ptr = {e: 0 for e in ENGS}
        sem = {}
        progress = True
        while progress:
            progress = False
            for e in ENGS:
                while ptr[e] < len(self.items[e]):
                    it = self.items[e][ptr[e]]
                    if it[0] == "op":
                        for s, inc in it[2]:
                            sem[s] = sem.get(s, 0) + inc
                        ptr[e] += 1
                        progress = True
                    else:
                        if sem.get(it[1], 0) >= it[2]:
                            ptr[e] += 1
                            progress = True
                        else:
                            break
        stuck = {e for e in ENGS if ptr[e] < len(self.items[e])}
        if stuck:
            det = []
            for e in stuck:
                it = self.items[e][ptr[e]]
                det.append(f"{e}@{ptr[e]}/{len(self.items[e])} waits {it[1]}>={it[2]}"
                           f" (has {sem.get(it[1], 0)})")
            raise RuntimeError("deadlock in built program: " + "; ".join(det))

    def emit(self, nc, block, stack):
        self.check_deadlock()
        sems = {s: stack.enter_context(nc.semaphore(s)) for s in self.semcount}

        def mk(e):
            def body(eng):
                for it in self.items[e]:
                    if it[0] == "op":
                        inst = it[1](eng)
                        for s, inc in it[2]:
                            inst.then_inc(sems[s], inc)
                    else:
                        eng.wait_ge(sems[it[1]], it[2])
            return body

        for e in ENGS:
            getattr(block, _BLOCK_ATTR[e])(mk(e))

    def stats(self):
        return {e: len(self.items[e]) for e in ENGS}


class Res:
    __slots__ = ("w", "r")

    def __init__(self):
        self.w = None
        self.r = []

    def note_write(self, tok):
        self.w = tok
        self.r = []

    def note_read(self, tok):
        if tok is not None:
            self.r.append(tok)
            if len(self.r) > 8:
                self.r[:] = self.r[-8:]


class Arena:

    def __init__(self, handle, n_f32):
        self.h = handle
        self.n = n_f32
        self.marks = []
        self.off = 0

    def push(self):
        self.marks.append(self.off)

    def pop(self):
        self.off = self.marks.pop()

    def take_f32(self, *shape):
        n = int(np.prod(shape))
        assert self.off + n <= self.n, f"arena overflow: {self.off}+{n} > {self.n}"
        v = self.h.ap()[:, self.off:self.off + n]
        self.off += n
        if len(shape) > 1:
            pat = " ".join(f"d{i}" for i in range(len(shape)))
            v = v.rearrange(f"p ({pat}) -> p {pat}",
                            **{f"d{i}": shape[i] for i in range(len(shape) - 1)})
        return v

    def take_bf16(self, *shape):
        n = int(np.prod(shape))
        assert n % 2 == 0
        assert self.off + n // 2 <= self.n, f"arena overflow bf16: {self.off}+{n // 2} > {self.n}"
        v = self.h.ap()[:, self.off:self.off + n // 2].bitcast(BF16)
        self.off += n // 2
        if len(shape) > 1:
            pat = " ".join(f"d{i}" for i in range(len(shape)))
            v = v.rearrange(f"p ({pat}) -> p {pat}",
                            **{f"d{i}": shape[i] for i in range(len(shape) - 1)})
        return v



def tile_proj_weight(w_kn, nsl=NSL):
    K, N = w_kn.shape
    kc, ns = K // 128, N // nsl
    return np.ascontiguousarray(w_kn.reshape(kc, 128, ns, nsl).transpose(2, 1, 0, 3))



def build_program():
    nc = bass.Bass("TRN2", target_bir_lowering=False)
    stack = contextlib.ExitStack()

    def din(name, shape, dtype=F32):
        return nc.dram_tensor(name, list(shape), dtype, kind="ExternalInput").ap()

    X = din("x", [NT, 128, D])
    OUT = nc.dram_tensor("out", [NT, 128, D], F32, kind="ExternalOutput").ap()
    IDENT = din("ident", [128, 128])
    DECAY = din("decay", [128, NT])

    W_l0 = {k: din(f"l0_{k}", [8, 128, KC, NSL]) for k in ("wq", "wk", "wv", "wo")}
    B_l0 = din("l0_bias", [4, 1, D])
    IOTA8 = din("iota8", [1, 8])
    F0_w13 = din("f0_w13", [9, 8, 128, KC, 256])
    F0_wdn = din("f0_wdn", [9, 2, 128, 8, 1024])
    F0_gate = din("f0_gate", [128, KC, 8])
    F0_eb = din("f0_eb", [1, 8])
    W_l1 = {k: din(f"l1_{k}", [8, 128, KC, NSL]) for k in ("wq", "wlk", "wlv", "wo")}
    B_l1 = din("l1_bias", [4, 1, D])
    L1_wc = din("l1_wc", [128, KC, 1])
    L1_bc = din("l1_bc", [1, 1])
    L1_wkv = din("l1_wkv", [128, KC, 256])
    B_l1_kv = din("l1_bkv", [1, 256])
    L1_qn = din("l1_qn", [1, 128])
    L1_kn = din("l1_kn", [1, 128])
    L1_sinkT = din("l1_sinkT", [128, 16])
    L1_sinkv = din("l1_sinkv", [1, D])
    COSQ = din("cosq", [128, NTB, 32])
    SINQ = din("sinq", [128, NTB, 32])
    COSH = din("cosh", [128, 32])
    SINH = din("sinh", [128, 32])
    COSCK = din("cosck", [32, 32])
    SINCK = din("sinck", [32, 32])
    CMASK = din("cmask", [128, NTB, 257])
    LMASK = din("lmask", [128, 2, 256])
    PATTERN = din("pattern", [128, 8])
    HALOIDX = din("haloidx", [128, 2], mybir.dt.uint32)
    F1_w13 = din("f1_w13", [9, 11, 128, KC, 256])
    F1_wdn = din("f1_wdn", [9, 2, 128, 11, 1024])
    F1_gate = din("f1_gate", [128, KC, 8])
    hx_in = nc.dram_tensor("hx_in", [256, D], BF16)
    hx_out = nc.dram_tensor("hx_out", [2048, D], BF16)
    ccin = nc.dram_tensor("ccin", [64, 256], F32)
    ccout = nc.dram_tensor("ccout", [512, 256], F32)

    dbg = None
    if DEBUG_TAP is not None:
        dbg = nc.dram_tensor("dbg", [NT, 128, D], F32, kind="ExternalOutput").ap()

    used = [0]

    def salloc(name, shape, dtype):
        t = stack.enter_context(nc.sbuf_tensor(name, list(shape), dtype))
        used[0] += int(np.prod(shape[1:])) * mybir.dt.size(dtype)
        return t

    xres_t = salloc("xres", [128, NT * D], F32)
    xres = xres_t.ap().rearrange("p (t d) -> p t d", t=NT)
    xT = salloc("xT", [128, KC, TPC], BF16).ap()
    wbuf = salloc("wbuf", [128, 2, KC, NSL], BF16).ap()
    stg_t = salloc("stg", [128, D], F32)
    stg = stg_t.ap()
    bias_bf = salloc("bias_bf", [1, 2, D], BF16).ap()
    stats = salloc("stats", [128, NT, 8], F32).ap()
    ssel = salloc("ssel", [128, NT, 8], F32).ap()
    small = salloc("small", [128, 64], F32).ap()
    ident_bf = salloc("ident_bf", [128, 128], BF16).ap()
    ones_bf = salloc("ones_bf", [128, 128], BF16).ap()
    ones_f = salloc("ones_f", [1, 128], F32).ap()
    ident_f = salloc("ident_f", [128, 128], F32).ap()
    gtile0 = salloc("gtile0", [128, KC, 8], F32).ap()
    gtile1 = salloc("gtile1", [128, KC, 8], F32).ap()
    ebias_t = salloc("ebias_t", [128, 8], F32).ap()
    iota_t = salloc("iota_t", [128, 8], F32).ap()
    rev_t = salloc("rev_t", [128, 8], F32).ap()
    decay_sb = salloc("decay_sb", [128, NT], F32).ap()

    arena_f32 = (212700 - used[0]) // 4
    arena_t = salloc("arena", [128, arena_f32], F32)
    ar = Arena(arena_t, arena_f32)

    psum = [stack.enter_context(nc.psum_tensor(f"ps{i}", [128, 512], F32)) for i in range(6)]
    psum_bf = [stack.enter_context(nc.psum_tensor(f"psbf{i}", [128, 1024], BF16)) for i in range(2)]
    pbank = [Res() for _ in range(6)]
    pbank_bf = [Res() for _ in range(2)]
    wpar = [Res(), Res()]
    bias_res = [Res(), Res()]
    stgreg = [Res() for _ in range(4)]

    kv_in = nc.dram_tensor("kv_in", [128, 32 * 129], BF16)
    kv_out = nc.dram_tensor("kv_out", [NCORES * 128, 32 * 129], BF16)
    xspill = nc.dram_tensor("xspill", [NT, 128, D], F32)

    p = Prog()

    def ln_stats(src_ap, it, col, sq_scratch=None):
        sq = stg if sq_scratch is None else sq_scratch
        m_raw = stats[:, it, col + 2:col + 3]
        s2_raw = stats[:, it, col + 3:col + 4]
        p.opm("dve", lambda E, a=src_ap, o=m_raw: E.tensor_reduce(o, a, AX.X, OP.add))
        t2 = p.opm("act", lambda E, a=src_ap, o=s2_raw, sq=sq: E.activation(
            sq[:, :], a, AF.Square, accum_out=o))
        mcol = stats[:, it, col:col + 1]
        rcol = stats[:, it, col + 1:col + 2]
        p.wait("dve", t2)
        p.op("dve", lambda E, i=m_raw, o=mcol: E.tensor_scalar(o, i, 1.0 / D, 0.0, OP.mult, OP.bypass))
        p.op("dve", lambda E, m=mcol: E.tensor_tensor(small[:, 0:1], m, m, OP.mult))
        p.op("dve", lambda E, i=s2_raw: E.tensor_scalar(small[:, 1:2], i, 1.0 / D, small[:, 0:1],
                                                        OP.mult, OP.subtract))
        tv = p.opm("dve", lambda E: E.tensor_scalar(small[:, 1:2], small[:, 1:2], 1e-5, 0.0,
                                                    OP.add, OP.bypass))
        p.wait("act", tv)
        ts = p.opm("act", lambda E: E.activation(small[:, 2:3], small[:, 1:2], AF.Sqrt))
        p.wait("dve", ts)
        return p.opm("dve", lambda E, o=rcol: E.reciprocal(o, small[:, 2:3]))

    def transpose_tokmajor_to(get_src_tile, dstT, ntiles=NT, dst_tok0=0):
        p.wait("pe", p.cur("dve"))
        p.wait("pe", p.cur("act"))
        for it in range(ntiles):
            srct, tok = get_src_tile(it)
            p.wait("pe", tok)
            for g in range(2):
                pb = psum_bf[g]
                br = pbank_bf[g]
                p.wait_all("pe", br.r)
                for j in range(8):
                    k = g * 8 + j
                    p.op("pe", lambda E, srct=srct, k=k, j=j, pb=pb: E.transpose(
                        pb[:, j * 128:(j + 1) * 128], srct[:, k * 128:(k + 1) * 128],
                        ident_bf[:, :]))
                tq = p.milestone("pe")
                br.note_write(tq)
                p.wait("dve", tq)
                tc = p.opm("dve", lambda E, it=it, g=g, pb=pb, dstT=dstT, dst_tok0=dst_tok0: E.tensor_copy(
                    dstT[:, g * 8:(g + 1) * 8, dst_tok0 + it * 128:dst_tok0 + (it + 1) * 128],
                    pb[:, :].rearrange("p (j n) -> p j n", j=8)))
                br.note_read(tc)

    def emit_ln_to_xT(src_f32, scol, norm_buf, single_tile=False):
        p.wait("act", p.cur("dve"))
        p.wait("dve", p.cur("act"))
        for it in range(NT):
            ln_stats(src_f32[:, it, :], it, scol)

        def srctile(it):
            mcol = stats[:, it, scol:scol + 1]
            rcol = stats[:, it, scol + 1:scol + 2]
            dst = norm_buf if single_tile else norm_buf[:, it, :]
            if single_tile:
                p.wait("dve", p.cur("pe"))
            tn = p.opm("dve", lambda E, it=it, m=mcol, r=rcol, dst=dst: E.tensor_scalar(
                dst, src_f32[:, it, :], m, r, OP.subtract, OP.mult))
            return dst, tn
        transpose_tokmajor_to(srctile, xT)

    def load_bias(bias_dram_row, slot, width=D):
        p.wait_all("pool", bias_res[slot].r)
        t = p.dma("pool", bias_bf[:, slot, 0:width], bias_dram_row, stream=f"bias{slot}")
        bias_res[slot].note_write(t)
        return t

    def emit_proj(wblob, bias_slot, evac, dout=D, src=None, tok0=0, ntiles=NT):
        if src is None:
            src = xT
        p.wait("pe", p.cur("dve"))
        nsl_count = dout // NSL
        for ns in range(nsl_count):
            par = ns % 2
            wb = wbuf[:, par]
            p.wait_all("pool", wpar[par].r)
            tw = p.dma("pool", wb, wblob[ns, :, :, :], stream=f"wg{par}")
            wpar[par].note_write(tw)
            p.wait("pe", tw)
            if bias_slot is not None:
                p.wait("pe", bias_res[bias_slot].w)
            for it in range(ntiles):
                bank = psum[it % 4]
                bres = pbank[it % 4]
                p.wait_all("pe", bres.r)
                for k in range(KC):
                    p.op("pe", lambda E, it=it, k=k, bank=bank, wb=wb, src=src, tok0=tok0: E.matmul(
                        bank[:, 0:NSL], src[:, k, tok0 + it * 128:tok0 + (it + 1) * 128],
                        wb[:, k, :], start=(k == 0), stop=(bias_slot is None and k == KC - 1)))
                if bias_slot is not None:
                    tmm = p.opm("pe", lambda E, bank=bank, ns=ns, bs=bias_slot: E.matmul(
                        bank[:, 0:NSL], ones_bf[0:1, :],
                        bias_bf[0:1, bs, ns * NSL:(ns + 1) * NSL], start=False, stop=True))
                    bias_res[bias_slot].note_read(tmm)
                else:
                    tmm = p.milestone("pe")
                bres.note_write(tmm)
                wpar[par].note_read(tmm)
                tev = evac(it, ns, bank, tmm)
                bres.note_read(tev)

    t_x = [p.dma("sp", xres[:, it, :], X[it], stream="xio") for it in range(NT)]
    t_id = p.dma("sp", ident_f, IDENT, stream="misc")
    t_dec = p.dma("sp", decay_sb[:, :], DECAY, stream="misc")
    t_ones = p.opm("pool", lambda E: E.memset(ones_bf[:, :], 1.0))
    t_onesf = p.opm("pool", lambda E: E.memset(ones_f[:, :], 1.0))
    t_g0 = p.dma("sp", gtile0, F0_gate, stream="misc")
    t_g1 = p.dma("sp", gtile1, F1_gate, stream="misc")
    t_eb = p.dma("sp", small[0:1, 32:40], F0_eb, stream="misc")
    t_io = p.dma("sp", small[0:1, 40:48], IOTA8, stream="misc")
    p.wait("dve", t_id)
    tid_bf = p.opm("dve", lambda E: E.tensor_copy(ident_bf[:, :], ident_f))
    p.wait("pe", t_eb)
    p.wait("pe", t_io)
    p.wait("pe", t_onesf)
    tmb = p.opm("pe", lambda E: E.matmul(psum[4][:, 0:8], ones_f[0:1, :],
                                         small[0:1, 32:40], start=True, stop=True))
    pbank[4].note_write(tmb)
    p.wait("dve", tmb)
    te1 = p.opm("dve", lambda E: E.tensor_copy(ebias_t, psum[4][:, 0:8]))
    pbank[4].note_read(te1)
    tmb2 = p.opm("pe", lambda E: E.matmul(psum[5][:, 0:8], ones_f[0:1, :],
                                          small[0:1, 40:48], start=True, stop=True))
    pbank[5].note_write(tmb2)
    p.wait("dve", tmb2)
    te2 = p.opm("dve", lambda E: E.tensor_copy(iota_t, psum[5][:, 0:8]))
    pbank[5].note_read(te2)
    p.op("dve", lambda E: E.tensor_scalar(rev_t, iota_t, -1.0, 8.0, OP.mult, OP.add))

    ar.push()
    bufA = ar.take_bf16(NT, D)
    bufB = ar.take_bf16(NT, D)

    tb0 = load_bias(B_l0[1], 0)
    tb1 = load_bias(B_l0[2], 1)

    p.wait_all("dve", t_x)
    p.wait_all("act", t_x)
    p.wait("act", tid_bf)
    p.wait("pe", tid_bf)
    emit_ln_to_xT(xres, 0, bufA)

    def evac_relu_decay(dstbuf):
        def f(it, ns, bank, tmm):
            p.wait("dve", tmm)
            p.wait("dve", t_dec)
            return p.opm("dve", lambda E, it=it, ns=ns, bank=bank: E.tensor_scalar(
                dstbuf[:, it, ns * NSL:(ns + 1) * NSL], bank[:, 0:NSL],
                0.0, decay_sb[:, it:it + 1], OP.max, OP.mult))
        return f

    def evac_plain_bf(dstbuf):
        def f(it, ns, bank, tmm):
            p.wait("act", tmm)
            return p.opm("act", lambda E, it=it, ns=ns, bank=bank: E.activation(
                dstbuf[:, it, ns * NSL:(ns + 1) * NSL], bank[:, 0:NSL], AF.Copy))
        return f

    p.wait("pe", t_ones)
    emit_proj(W_l0["wk"], 0, evac_relu_decay(bufB))
    tb0 = load_bias(B_l0[0], 0)
    emit_proj(W_l0["wv"], 1, evac_plain_bf(bufA))

    p.wait("pe", p.cur("dve"))
    p.wait("pe", p.cur("act"))
    for b in range(B):
        for h in range(16):
            pair = b * 16 + h
            bank = psum[4 + (pair % 2)]
            bres = pbank[4 + (pair % 2)]
            p.wait_all("pe", bres.r)
            for j in range(NTB):
                it = b * NTB + j
                p.op("pe", lambda E, it=it, h=h, bank=bank, j=j: E.matmul(
                    bank[:, 0:128], bufB[:, it, h * 128:(h + 1) * 128],
                    bufA[:, it, h * 128:(h + 1) * 128], start=(j == 0), stop=False))
                p.op("pe", lambda E, it=it, h=h, bank=bank, j=j: E.matmul(
                    bank[:, 128:129], bufB[:, it, h * 128:(h + 1) * 128],
                    ones_bf[:, 0:1], start=False, stop=(j == NTB - 1)))
            tm = p.milestone("pe")
            bres.note_write(tm)
            p.wait("dve", tm)
            par2 = pair % 2
            stg_res2 = stgreg[par2]
            p.wait_all("dve", stg_res2.r)
            stg_kv = stg[:, 1100 + par2 * 80:1100 + par2 * 80 + 65].bitcast(BF16)[:, 0:129]
            tc = p.opm("dve", lambda E, bank=bank, o=stg_kv: E.tensor_copy(o, bank[:, 0:129]))
            bres.note_read(tc)
            stg_res2.note_write(tc)
            p.wait("pool", tc)
            td = p.dma("pool", kv_in.ap()[:, pair * 129:(pair + 1) * 129],
                       stg_kv, stream="kvb")
            stg_res2.note_read(td)
    p.wait("pool", td)
    if _NO_CC:
        for r in range(NCORES):
            p.op("pool", lambda E, r=r: E.dma_start(
                out=kv_out.ap()[r * 128:(r + 1) * 128, :], in_=kv_in.ap()), is_dma=True)
            p.milestone("pool", kind="cc", sem="cc0", inc=16)
        n_cc = 128
    else:
        p.op("pool", lambda E: E.collective_compute(
            "AllGather", OP.bypass, replica_groups=[list(range(NCORES))],
            ins=[kv_in.ap().opt()], outs=[kv_out.ap().opt()]), is_dma=True)
        p.milestone("pool", kind="cc", sem="cc0", inc=1)
        n_cc = 1

    emit_proj(W_l0["wq"], 0, evac_relu_decay(bufB))
    qT3 = bufA.rearrange("p t d -> p (t d)").rearrange("p (t h n) -> p t h n", t=NT, h=16)

    kvall = wbuf.rearrange("p a k n -> p (a k n)")[:, 0:32 * 129].rearrange(
        "p (q n) -> p q n", q=32)
    p.raw_wait("pool", "cc0", n_cc)
    p.wait_all("pool", [wpar[0].r[-1] if wpar[0].r else None,
                        wpar[1].r[-1] if wpar[1].r else None])
    p.wait("pool", p.cur("dve"))
    p.wait("pool", p.cur("act"))
    for qtr in range(4):
        cw = 1032
        dst = stg[:, 0:cw]
        for r in range(NCORES):
            t_acc = p.dma("pool", dst,
                          kv_out.ap()[r * 128:(r + 1) * 128,
                                      qtr * cw:(qtr + 1) * cw],
                          stream="kvb",
                          **({} if r == 0 else {"accum_op": OP.add}))
            p.wait("pool", t_acc)
        p.wait("dve", t_acc)
        tkc = p.opm("dve", lambda E, qtr=qtr, dst=dst, cw=cw: E.tensor_copy(
            kvall.rearrange("p q n -> p (q n)")[:, qtr * cw:(qtr + 1) * cw], dst))
        p.wait("pool", tkc)
    t_kv = p.cur("dve")
    p.wait("pe", t_kv)

    def qsrc(it):
        return bufB[:, it, :], None
    p.wait("pe", p.cur("dve"))
    for it in range(NT):
        for g in range(2):
            pb = psum_bf[g]
            br = pbank_bf[g]
            p.wait_all("pe", br.r)
            for j in range(8):
                h = g * 8 + j
                p.op("pe", lambda E, it=it, h=h, j=j, pb=pb: E.transpose(
                    pb[:, j * 128:(j + 1) * 128], bufB[:, it, h * 128:(h + 1) * 128],
                    ident_bf[:, :]))
            tq = p.milestone("pe")
            br.note_write(tq)
            p.wait("dve", tq)
            tc = p.opm("dve", lambda E, it=it, g=g, pb=pb: E.tensor_copy(
                qT3[:, it, g * 8:(g + 1) * 8, :], pb[:, :].rearrange("p (j n) -> p j n", j=8)))
            br.note_read(tc)


    p.wait("pe", p.cur("dve"))
    kv_read = []
    for b in range(B):
        for h in range(16):
            pair = b * 16 + h
            for j in range(NTB):
                it = b * NTB + j
                bi = (pair * NTB + j) % 4
                bank, bres = psum[bi], pbank[bi]
                p.wait_all("pe", bres.r)
                tm = p.opm("pe", lambda E, it=it, h=h, pair=pair, bank=bank: E.matmul(
                    bank[:, 0:129], qT3[:, it, h, :], kvall[:, pair, :],
                    start=True, stop=True))
                bres.note_write(tm)
                p.wait("dve", tm)
                c0, c1 = bi * 2, bi * 2 + 1
                p.op("dve", lambda E, bank=bank, c0=c0: E.tensor_scalar(
                    small[:, 8 + c0:9 + c0], bank[:, 128:129], EPS0, 0.0, OP.add, OP.bypass))
                p.op("dve", lambda E, c0=c0, c1=c1: E.reciprocal(
                    small[:, 8 + c1:9 + c1], small[:, 8 + c0:9 + c0]))
                tz = p.opm("dve", lambda E, it=it, h=h, bank=bank, c1=c1: E.tensor_scalar(
                    bufB[:, it, h * 128:(h + 1) * 128], bank[:, 0:128],
                    small[:, 8 + c1:9 + c1], 0.0, OP.mult, OP.bypass))
                bres.note_read(tz)
                kv_read.append(tz)

    t_pe_kv = p.cur("pe")
    wpar[0].note_read(t_pe_kv)
    wpar[1].note_read(t_pe_kv)
    p.wait_all("pool", kv_read[-4:])
    transpose_tokmajor_to(lambda it: (bufB[:, it, :], None), xT)
    tb1 = load_bias(B_l0[3], 1)

    p.barrier()
    ar.pop()
    ar.push()
    acc = ar.take_f32(NT, D)

    def evac_to_acc(it, ns, bank, tmm):
        p.wait("act", tmm)
        return p.opm("act", lambda E, it=it, ns=ns, bank=bank: E.activation(
            acc[:, it, ns * NSL:(ns + 1) * NSL], bank[:, 0:NSL], AF.Copy))

    emit_proj(W_l0["wo"], 1, evac_to_acc)

    p.wait("dve", p.cur("act"))
    for it in range(NT):
        ln_stats(acc[:, it, :], it, 4)
        mcol = stats[:, it, 4:5]
        rcol = stats[:, it, 5:6]
        tn = p.opm("dve", lambda E, it=it, m=mcol, r=rcol: E.tensor_scalar(
            stg[:, :], acc[:, it, :], m, r, OP.subtract, OP.mult))
        ta = p.opm("dve", lambda E, it=it: E.tensor_tensor(
            xres[:, it, :], xres[:, it, :], stg[:, :], OP.add))
        p.wait("act", ta)
    ar.pop()


    def emit_moe_ffn(w13, wdn, gate_sb, nhc, gate_af, scale_input,
                     seed_xln, router_softmax):
        HID = nhc * 128
        p.barrier()
        p.barrier(dst_engs=("sp",), src_engs=("pe", "act", "dve"))
        ar.push()
        lnstage = ar.take_bf16(D)
        emit_ln_to_xT(xres, 0, lnstage, single_tile=True)
        gtile = gate_sb
        rsc = ar.take_f32(4, 8)
        sq2 = ar.take_f32(D)
        xT32 = sq2.rearrange("p (k n) -> p k n", k=KC)
        acc = xres

        for it in range(NT):
            mcol = stats[:, it, 0:1]
            rcol = stats[:, it, 1:2]
            txn = p.opm("dve", lambda E, it=it, m=mcol, r=rcol: E.tensor_scalar(
                stg[:, :], xres[:, it, :], m, r, OP.subtract, OP.mult))
            p.wait("pe", txn)
            for rd in range(4):
                bank = psum[4 + rd % 2]
                bres = pbank[4 + rd % 2]
                p.wait_all("pe", bres.r)
                for j in range(4):
                    k = rd * 4 + j
                    p.op("pe", lambda E, k=k, j=j, bank=bank: E.transpose(
                        bank[:, j * 128:(j + 1) * 128], stg[:, k * 128:(k + 1) * 128],
                        ident_f))
                tq = p.milestone("pe")
                bres.note_write(tq)
                p.wait("dve", tq)
                tc = p.opm("dve", lambda E, rd=rd, bank=bank: E.tensor_copy(
                    xT32[:, rd * 4:(rd + 1) * 4, :],
                    bank[:, :].rearrange("p (j n) -> p j n", j=4)))
                bres.note_read(tc)
            p.wait("pe", p.cur("dve"))
            p.wait_all("pe", pbank[4].r)
            for k in range(KC):
                p.op("pe", lambda E, k=k: E.matmul(
                    psum[4][:, 0:8], xT32[:, k, :], gtile[:, k, :],
                    start=(k == 0), stop=(k == KC - 1)))
            tsc = p.milestone("pe")
            pbank[4].note_write(tsc)
            scores = rsc[:, 0, :]
            key = rsc[:, 1, :]
            tmp = rsc[:, 2, :]
            msel = rsc[:, 3, :]
            if router_softmax:
                p.wait("dve", tsc)
                p.op("dve", lambda E: E.tensor_reduce(small[:, 0:1], psum[4][:, 0:8],
                                                      AX.X, OP.max))
                p.op("dve", lambda E: E.tensor_scalar(small[:, 1:2], small[:, 0:1],
                                                      -1.0, 0.0, OP.mult, OP.bypass))
                tmx = p.cur("dve")
                p.wait("act", tmx)
                tex = p.opm("act", lambda E: E.activation(
                    key, psum[4][:, 0:8], AF.Exp, bias=small[:, 1:2],
                    accum_out=small[:, 2:3]))
                p.wait("dve", tex)
                p.op("dve", lambda E: E.reciprocal(small[:, 3:4], small[:, 2:3]))
                tsf = p.opm("dve", lambda E: E.tensor_scalar(
                    scores, key, small[:, 3:4], 0.0, OP.mult, OP.bypass))
                pbank[4].note_read(tex)
                p.op("dve", lambda E: E.tensor_copy(key, scores))
            else:
                p.wait("act", tsc)
                tsg = p.opm("act", lambda E: E.activation(scores, psum[4][:, 0:8], AF.Sigmoid))
                pbank[4].note_read(tsg)
                p.wait("dve", tsg)
                p.op("dve", lambda E: E.tensor_tensor(key, scores, ebias_t, OP.add))
            p.op("dve", lambda E: E.tensor_reduce(small[:, 0:1], key, AX.X, OP.max))
            p.op("dve", lambda E: E.tensor_scalar(tmp, key, small[:, 0:1], 0.0,
                                                  OP.is_ge, OP.bypass))
            p.op("dve", lambda E: E.tensor_tensor(tmp, tmp, rev_t, OP.mult))
            p.op("dve", lambda E: E.tensor_reduce(small[:, 1:2], tmp, AX.X, OP.max))
            p.op("dve", lambda E: E.tensor_scalar(small[:, 1:2], small[:, 1:2], -1.0, 8.0,
                                                  OP.mult, OP.add))
            p.op("dve", lambda E: E.tensor_scalar(msel, iota_t, small[:, 1:2], 0.0,
                                                  OP.is_equal, OP.bypass))
            p.op("dve", lambda E: E.tensor_scalar(tmp, msel, -1e9, 0.0, OP.mult, OP.bypass))
            p.op("dve", lambda E: E.tensor_tensor(tmp, key, tmp, OP.add))
            p.op("dve", lambda E: E.tensor_reduce(small[:, 0:1], tmp, AX.X, OP.max))
            p.op("dve", lambda E: E.tensor_scalar(tmp, tmp, small[:, 0:1], 0.0,
                                                  OP.is_ge, OP.bypass))
            p.op("dve", lambda E: E.tensor_tensor(tmp, tmp, rev_t, OP.mult))
            p.op("dve", lambda E: E.tensor_reduce(small[:, 1:2], tmp, AX.X, OP.max))
            p.op("dve", lambda E: E.tensor_scalar(small[:, 1:2], small[:, 1:2], -1.0, 8.0,
                                                  OP.mult, OP.add))
            p.op("dve", lambda E: E.tensor_scalar(tmp, iota_t, small[:, 1:2], 0.0,
                                                  OP.is_equal, OP.bypass))
            p.op("dve", lambda E: E.tensor_tensor(msel, msel, tmp, OP.add))
            p.op("dve", lambda E, it=it: E.tensor_tensor(ssel[:, it, :], scores, msel,
                                                         OP.mult))

        t_sp = []
        p.wait("sp", p.cur("dve"))
        for it in range(NT):
            t_sp.append(p.dma("sp", xspill[it], xres[:, it, :], stream="spill"))
        if seed_xln:
            for it in range(NT):
                p.wait("dve", t_sp[it])
                mcol = stats[:, it, 0:1]
                rcol = stats[:, it, 1:2]
                p.opm("dve", lambda E, it=it, m=mcol, r=rcol: E.tensor_scalar(
                    acc[:, it, :], xres[:, it, :], m, r, OP.subtract, OP.mult))

        h_fm = ar.take_bf16(nhc, 1024)
        wdn_sb = ar.take_bf16(nhc, 1024)
        sbc = ar.take_f32(1024)
        stgA = ar.take_f32(1024)
        stgB = ar.take_f32(1024)
        srow = ar.take_f32(1024)
        h_res = Res()
        wdn_res = Res()
        for m in range(9):
            e = m - 1
            scaled = m > 0
            if scaled:
                p.wait("pe", p.cur("dve"))
                for g in range(2):
                    bank = psum[4 + g]
                    bres = pbank[4 + g]
                    p.wait_all("pe", bres.r)
                    for j in range(4):
                        it = g * 4 + j
                        sap = ssel[:, it, e:e + 1]
                        p.op("pe", lambda E, j=j, bank=bank, sap=sap: E.transpose(
                            bank[0:1, j * 128:(j + 1) * 128], sap, ident_f))
                    tq = p.milestone("pe")
                    bres.note_write(tq)
                    p.wait("dve", tq)
                    tc = p.opm("dve", lambda E, g=g, bank=bank: E.tensor_copy(
                        srow[0:1, g * 512:(g + 1) * 512], bank[0:1, 0:512]))
                    bres.note_read(tc)
                p.wait("pe", p.cur("dve"))
                for g in range(2):
                    bank = psum[4 + g]
                    bres = pbank[4 + g]
                    p.wait_all("pe", bres.r)
                    tb = p.opm("pe", lambda E, g=g, bank=bank: E.matmul(
                        bank[:, :], ones_f[0:1, :], srow[0:1, g * 512:(g + 1) * 512],
                        start=True, stop=True))
                    bres.note_write(tb)
                    p.wait("dve", tb)
                    tc = p.opm("dve", lambda E, g=g, bank=bank: E.tensor_copy(
                        sbc[:, g * 512:(g + 1) * 512], bank[:, :]))
                    bres.note_read(tc)
            p.wait("pe", p.cur("dve"))
            for hc in range(nhc):
                par = hc % 2
                wb = wbuf[:, par]
                p.wait_all("pool", wpar[par].r)
                tw = p.dma("pool", wb[:, :, :], w13[m, hc], stream=f"wg{par}")
                wpar[par].note_write(tw)
                p.wait("pe", tw)
                for half in range(2):
                    bankG, brG = psum[half], pbank[half]
                    bankU, brU = psum[2 + half], pbank[2 + half]
                    p.wait_all("pe", brG.r)
                    p.wait_all("pe", brU.r)
                    for k in range(KC):
                        p.op("pe", lambda E, k=k, half=half, bankG=bankG, wb=wb: E.matmul(
                            bankG[:, :], wb[:, k, 0:128], xT[:, k, half * 512:(half + 1) * 512],
                            start=(k == 0), stop=(k == KC - 1)))
                    for k in range(KC):
                        p.op("pe", lambda E, k=k, half=half, bankU=bankU, wb=wb: E.matmul(
                            bankU[:, :], wb[:, k, 128:256], xT[:, k, half * 512:(half + 1) * 512],
                            start=(k == 0), stop=(k == KC - 1)))
                    tgu = p.milestone("pe")
                    brG.note_write(tgu)
                    brU.note_write(tgu)
                    wpar[par].note_read(tgu)
                    hs = half * 512
                    hdst = h_fm[:, hc, hs:hs + 512]
                    p.wait("dve", tgu)
                    p.wait("act", tgu)
                    p.wait_all("dve", h_res.r)
                    if scale_input and scaled:
                        p.op("dve", lambda E, bankG=bankG, hs=hs: E.tensor_tensor(
                            stgA[:, hs:hs + 512], bankG[:, :], sbc[:, hs:hs + 512], OP.mult))
                        tga = p.cur("dve")
                        p.wait("act", tga)
                        tact = p.opm("act", lambda E, hs=hs: E.activation(
                            stgA[:, hs:hs + 512], stgA[:, hs:hs + 512], gate_af))
                        p.wait("dve", tact)
                        p.op("dve", lambda E, bankU=bankU, hs=hs: E.tensor_tensor(
                            stgB[:, hs:hs + 512], bankU[:, :], sbc[:, hs:hs + 512], OP.mult))
                        th = p.opm("dve", lambda E, hdst=hdst, hs=hs: E.tensor_tensor(
                            hdst, stgA[:, hs:hs + 512], stgB[:, hs:hs + 512], OP.mult))
                    else:
                        tact = p.opm("act", lambda E, bankG=bankG, hs=hs: E.activation(
                            stgA[:, hs:hs + 512], bankG[:, :], gate_af))
                        p.wait("dve", tact)
                        if scaled:
                            p.op("dve", lambda E, bankU=bankU, hs=hs: E.tensor_tensor(
                                stgB[:, hs:hs + 512], stgA[:, hs:hs + 512], bankU[:, :],
                                OP.mult))
                            th = p.opm("dve", lambda E, hdst=hdst, hs=hs: E.tensor_tensor(
                                hdst, stgB[:, hs:hs + 512], sbc[:, hs:hs + 512], OP.mult))
                        else:
                            th = p.opm("dve", lambda E, hdst=hdst, bankU=bankU, hs=hs: E.tensor_tensor(
                                hdst, stgA[:, hs:hs + 512], bankU[:, :], OP.mult))
                    brG.note_read(th)
                    brU.note_read(th)
                    h_res.note_write(th)
            p.wait("pe", p.cur("dve"))
            for oh in range(2):
                p.wait_all("pool", wdn_res.r)
                twd = p.dma("pool", wdn_sb, wdn[m, oh], stream="w2")
                wdn_res.note_write(twd)
                p.wait("pe", twd)
                for it in range(NT):
                    for ns2 in range(2):
                        bi = (it % 2) * 2 + ns2
                        bank, bres = psum[bi], pbank[bi]
                        p.wait_all("pe", bres.r)
                        for hc in range(nhc):
                            p.op("pe", lambda E, hc=hc, it=it, ns2=ns2, bank=bank: E.matmul(
                                bank[:, :], h_fm[:, hc, it * 128:(it + 1) * 128],
                                wdn_sb[:, hc, ns2 * 512:(ns2 + 1) * 512],
                                start=(hc == 0), stop=(hc == nhc - 1)))
                        tdn = p.milestone("pe")
                        bres.note_write(tdn)
                        wdn_res.note_read(tdn)
                        h_res.note_read(tdn)
                        off = oh * 1024 + ns2 * 512
                        if m == 0 and not seed_xln:
                            p.wait("act", tdn)
                            if not seed_xln:
                                p.wait_all("act", [t_sp[it]])
                            tev = p.opm("act", lambda E, it=it, off=off, bank=bank: E.activation(
                                acc[:, it, off:off + 512], bank[:, :], AF.Copy))
                            bres.note_read(tev)
                            p.wait("dve", tev)
                        else:
                            p.wait("dve", tdn)
                            p.wait_all("dve", [t_sp[it]])
                            tev = p.opm("dve", lambda E, it=it, off=off, bank=bank: E.tensor_tensor(
                                acc[:, it, off:off + 512], acc[:, it, off:off + 512],
                                bank[:, :], OP.add))
                            bres.note_read(tev)

        p.barrier()
        t_rl_prev = None
        for it in range(NT):
            if t_rl_prev is not None:
                p.wait("sp", t_rl_prev)
                p.wait("act", t_rl_prev)
            p.wait("sp", p.cur("dve"))
            t_rl = p.dma("sp", stg[:, :], xspill[it], stream="spill")
            ln_stats(acc[:, it, :], it, 4, sq_scratch=sq2)
            mcol = stats[:, it, 4:5]
            rcol = stats[:, it, 5:6]
            p.opm("dve", lambda E, it=it, m=mcol, r=rcol: E.tensor_scalar(
                sq2[:, :], acc[:, it, :], m, r, OP.subtract, OP.mult))
            p.wait("dve", t_rl)
            tfin = p.opm("dve", lambda E, it=it: E.tensor_tensor(
                acc[:, it, :], sq2[:, :], stg[:, :], OP.add))
            t_rl_prev = tfin
        ar.pop()


    def emit_l1_attn():
        from concourse.bass import IndirectOffsetOnAxis
        SCALE = 1.0 / np.sqrt(128.0)
        p.barrier()
        p.barrier(dst_engs=("sp",), src_engs=("pe", "act", "dve"))
        ar.push()
        cosq = ar.take_f32(NTB, 32)
        sinq = ar.take_f32(NTB, 32)
        cosh = ar.take_f32(32)
        sinh = ar.take_f32(32)
        cosck = ar.take_f32(32)
        sinck = ar.take_f32(32)
        qnb = ar.take_f32(128)
        knb = ar.take_f32(128)
        cmask = ar.take_bf16(NTB, 258)
        lmask = ar.take_bf16(2, 256)
        pattern = ar.take_f32(8)
        haloidx = ar.take_f32(2).bitcast(mybir.dt.uint32)
        wc_bf = ar.take_bf16(KC)
        sinkT_bf = ar.take_bf16(16)
        sinkv_row = ar.take_bf16(D)
        onesrow = ar.take_bf16(256)

        tl = []
        tl.append(p.dma("sp", cosq, COSQ, stream="l1c"))
        tl.append(p.dma("sp", sinq, SINQ, stream="l1c"))
        tl.append(p.dma("sp", cosh, COSH, stream="l1c"))
        tl.append(p.dma("sp", sinh, SINH, stream="l1c"))
        tl.append(p.dma("sp", cosck[0:32, 0:32], COSCK, stream="l1c"))
        tl.append(p.dma("sp", sinck[0:32, 0:32], SINCK, stream="l1c"))
        tl.append(p.dma("sp", pattern, PATTERN, stream="l1c"))
        tl.append(p.dma("sp", haloidx, HALOIDX, stream="l1c"))
        tqn = p.dma("sp", stg[0:1, 0:128], L1_qn, stream="l1c")
        tkn = p.dma("sp", stg[0:1, 128:256], L1_kn, stream="l1c")
        bc_bf = small[0:1, 48:49].bitcast(BF16)[0:1, 0:1]
        tw = []
        tw.append(p.dma("pool", small[0:1, 48:49].bitcast(BF16)[0:1, 0:1], L1_bc, stream="l1w"))
        tw.append(p.dma("pool", cmask[:, :, 0:257], CMASK, stream="l1w"))
        tw.append(p.dma("pool", lmask, LMASK, stream="l1w"))
        tw.append(p.dma("pool", wc_bf, L1_wc.rearrange("p k u -> p (k u)"), stream="l1w"))
        tw.append(p.dma("pool", sinkT_bf, L1_sinkT, stream="l1w"))
        tw.append(p.dma("pool", sinkv_row[0:1, :], L1_sinkv, stream="l1w"))
        tor = p.opm("pool", lambda E: E.memset(onesrow[0:1, :], 1.0))
        p.wait_all("pe", tl)
        p.wait_all("pe", tw)
        p.wait("pe", tor)
        p.wait("pe", tqn)
        p.wait("pe", tkn)
        p.wait_all("pe", pbank[4].r)
        tb1 = p.opm("pe", lambda E: E.matmul(psum[4][:, 0:128], ones_f[0:1, :],
                                             stg[0:1, 0:128], start=True, stop=True))
        pbank[4].note_write(tb1)
        p.wait("dve", tb1)
        tc1 = p.opm("dve", lambda E: E.tensor_copy(qnb, psum[4][:, 0:128]))
        pbank[4].note_read(tc1)
        p.wait_all("pe", pbank[5].r)
        tb2 = p.opm("pe", lambda E: E.matmul(psum[5][:, 0:128], ones_f[0:1, :],
                                             stg[0:1, 128:256], start=True, stop=True))
        pbank[5].note_write(tb2)
        p.wait("dve", tb2)
        tc2 = p.opm("dve", lambda E: E.tensor_copy(knb, psum[5][:, 0:128]))
        pbank[5].note_read(tc2)

        xT_halo = ar.take_bf16(B, KC, 128)

        ar.push()
        lnst = ar.take_bf16(D)
        halo_tm = ar.take_bf16(B, D)
        emit_ln_to_xT(xres, 0, lnst, single_tile=True)
        tdd = None
        for b in range(B):
            it = b * NTB + (NTB - 1)
            mcol = stats[:, it, 0:1]
            rcol = stats[:, it, 1:2]
            if tdd is not None:
                p.wait("dve", tdd)
            ttl = p.opm("dve", lambda E, it=it, m=mcol, r=rcol: E.tensor_scalar(
                lnst, xres[:, it, :], m, r, OP.subtract, OP.mult))
            p.wait("pool", ttl)
            tdd = p.dma("pool", hx_in.ap()[b * 128:(b + 1) * 128, :], lnst, stream="hx")
        p.wait("pool", tdd)
        p.op("pool", lambda E: E.collective_compute(
            "AllGather", OP.bypass, replica_groups=[list(range(NCORES))],
            ins=[hx_in.ap().opt()], outs=[hx_out.ap().opt()]), is_dma=True)
        p.milestone("pool", kind="cc", sem="cc1", inc=1)

        ccpack = ar.take_f32(256)
        tokstage = ar.take_bf16(D)
        entT = ar.take_bf16(KC, 32)
        wcol = ar.take_f32(4)
        wmat_bf = small[:, 52:56].bitcast(BF16)
        p.wait_all("pool", wpar[0].r)
        twkv = p.dma("pool", wbuf[:, 0], L1_wkv, stream="wg0")
        wpar[0].note_write(twkv)
        tkv_b = load_bias(B_l1_kv, 1, width=256)
        for b in range(B):
            p.wait("pe", p.cur("dve"))
            p.wait_all("pe", pbank[4].r)
            for k in range(KC):
                p.op("pe", lambda E, k=k, b=b: E.matmul(
                    psum[4][0:1, 0:512], wc_bf[:, k:k + 1], xT[:, k, b * 512:(b + 1) * 512],
                    start=(k == 0), stop=False))
            p.op("pe", lambda E: E.matmul(
                psum[4][0:1, 0:256], bc_bf, onesrow[0:1, :], start=False, stop=False))
            tsc = p.opm("pe", lambda E: E.matmul(
                psum[4][0:1, 256:512], bc_bf, onesrow[0:1, :], start=False, stop=True))
            pbank[4].note_write(tsc)
            p.wait("dve", tsc)
            r3 = stg[0:1, 0:512].rearrange("o (c r) -> o c r", c=32)
            tcp = p.opm("dve", lambda E: E.tensor_copy(stg[0:1, 0:512], psum[4][0:1, 0:512]))
            pbank[4].note_read(tcp)
            p.op("dve", lambda E, r3=r3: E.tensor_reduce(
                stg[0:1, 512:544].rearrange("o (c u) -> o c u", c=32), r3, AX.X, OP.max))
            p.op("dve", lambda E, r3=r3: E.tensor_tensor(
                r3, r3, stg[0:1, 512:544].rearrange("o (c u) -> o c u", c=32).broadcast_to(
                    (1, 32, 16)), OP.subtract))
            tex = p.cur("dve")
            p.wait("act", tex)
            tex2 = p.opm("act", lambda E: E.activation(stg[0:1, 0:512], stg[0:1, 0:512], AF.Exp))
            p.wait("dve", tex2)
            p.op("dve", lambda E, r3=r3: E.tensor_reduce(
                stg[0:1, 512:544].rearrange("o (c u) -> o c u", c=32), r3, AX.X, OP.add))
            p.op("dve", lambda E: E.reciprocal(stg[0:1, 544:576], stg[0:1, 512:544]))
            tnw = p.opm("dve", lambda E, r3=r3: E.tensor_tensor(
                r3, r3, stg[0:1, 544:576].rearrange("o (c u) -> o c u", c=32).broadcast_to(
                    (1, 32, 16)), OP.mult))
            p.wait("pe", tnw)
            p.wait_all("pe", pbank[5].r)
            for jt in range(NTB):
                p.op("pe", lambda E, jt=jt: E.transpose(
                    psum[5][:, jt:jt + 1], stg[0:1, jt * 128:(jt + 1) * 128], ident_f[0:1, 0:1]))
            twc = p.milestone("pe")
            pbank[5].note_write(twc)
            p.wait("dve", twc)
            twc2 = p.opm("dve", lambda E: E.tensor_copy(wcol, psum[5][:, 0:4]))
            pbank[5].note_read(twc2)
            for jt in range(NTB):
                it = b * NTB + jt
                twm = p.opm("dve", lambda E, jt=jt: E.tensor_scalar(
                    wmat_bf, pattern, wcol[:, jt:jt + 1], 0.0, OP.mult, OP.bypass))
                mcol = stats[:, it, 0:1]
                rcol = stats[:, it, 1:2]
                ttk = p.opm("dve", lambda E, it=it, m=mcol, r=rcol: E.tensor_scalar(
                    tokstage, xres[:, it, :], m, r, OP.subtract, OP.mult))
                p.wait("pe", twm)
                p.wait("pe", ttk)
                for ns in range(4):
                    bank = psum[ns]
                    bres = pbank[ns]
                    p.wait_all("pe", bres.r)
                    tme = p.opm("pe", lambda E, ns=ns, bank=bank: E.matmul(
                        bank[0:8, 0:512], wmat_bf,
                        tokstage[:, ns * 512:(ns + 1) * 512], start=True, stop=True))
                    bres.note_write(tme)
                    p.wait("dve", tme)
                    tev = p.opm("dve", lambda E, ns=ns, bank=bank: E.tensor_copy(
                        stg[0:8, ns * 512:(ns + 1) * 512], bank[0:8, 0:512]))
                    bres.note_read(tev)
                p.wait("pe", p.cur("dve"))
                p.wait_all("pe", pbank[4].r)
                for k in range(KC):
                    p.op("pe", lambda E, k=k: E.transpose(
                        psum[4][:, k * 8:(k + 1) * 8], stg[0:8, k * 128:(k + 1) * 128],
                        ident_f[0:8, 0:8]))
                tet = p.milestone("pe")
                pbank[4].note_write(tet)
                p.wait("dve", tet)
                tet2 = p.opm("dve", lambda E, jt=jt: E.tensor_copy(
                    entT[:, :, jt * 8:(jt + 1) * 8],
                    psum[4][:, 0:128].rearrange("p (k e) -> p k e", k=KC)))
                pbank[4].note_read(tet2)
            p.wait("pe", p.cur("dve"))
            p.wait("pe", twkv)
            p.wait("pe", tkv_b)
            for half, boff in ((0, 0), (1, 128)):
                bank = psum[4]
                bres = pbank[4]
                p.wait_all("pe", bres.r)
                for k in range(KC):
                    p.op("pe", lambda E, k=k, boff=boff, bank=bank: E.matmul(
                        bank[0:32, 0:128], entT[:, k, :], wbuf[:, 0, k, boff:boff + 128],
                        start=(k == 0), stop=False))
                tbb = p.opm("pe", lambda E, boff=boff, bank=bank: E.matmul(
                    bank[0:32, 0:128], ones_bf[0:1, 0:32], bias_bf[0:1, 1, boff:boff + 128],
                    start=False, stop=True))
                bres.note_write(tbb)
                wpar[0].note_read(tbb)
                bias_res[1].note_read(tbb)
                p.wait("dve", tbb)
                if half == 0:
                    ckv = stg[0:32, 1024:1152]
                    tck = p.opm("dve", lambda E, bank=bank, ckv=ckv: E.tensor_copy(
                        ckv, bank[0:32, 0:128]))
                    bres.note_read(tck)
                    p.wait("act", tck)
                    tsq = p.opm("act", lambda E, ckv=ckv: E.activation(
                        stg[0:32, 1152:1280], ckv, AF.Square, accum_out=small[0:32, 60:61]))
                    p.wait("dve", tsq)
                    p.op("dve", lambda E: E.tensor_scalar(
                        small[0:32, 61:62], small[0:32, 60:61], 1.0 / 128, 1e-6, OP.mult, OP.add))
                    tvv = p.cur("dve")
                    p.wait("act", tvv)
                    tsr = p.opm("act", lambda E: E.activation(small[0:32, 62:63],
                                                              small[0:32, 61:62], AF.Sqrt))
                    p.wait("dve", tsr)
                    p.op("dve", lambda E: E.reciprocal(small[0:32, 63:64], small[0:32, 62:63]))
                    p.op("dve", lambda E, ckv=ckv: E.tensor_scalar(
                        ckv, ckv, small[0:32, 63:64], 0.0, OP.mult, OP.bypass))
                    p.op("dve", lambda E, ckv=ckv: E.tensor_tensor(ckv, ckv, knb[0:32, :], OP.mult))
                    x1 = ckv[:, 0:32]
                    x2 = ckv[:, 32:64]
                    p.op("dve", lambda E, x1=x1: E.tensor_tensor(
                        stg[0:32, 1280:1312], x1, cosck[0:32, 0:32], OP.mult))
                    p.op("dve", lambda E, x2=x2: E.tensor_tensor(
                        stg[0:32, 1312:1344], x2, sinck[0:32, 0:32], OP.mult))
                    p.op("dve", lambda E, x1=x1: E.tensor_tensor(
                        stg[0:32, 1344:1376], x1, sinck[0:32, 0:32], OP.mult))
                    p.op("dve", lambda E, x2=x2: E.tensor_tensor(
                        stg[0:32, 1376:1408], x2, cosck[0:32, 0:32], OP.mult))
                    p.op("dve", lambda E, x1=x1: E.tensor_tensor(
                        x1, stg[0:32, 1280:1312], stg[0:32, 1312:1344], OP.subtract))
                    p.op("dve", lambda E, x2=x2: E.tensor_tensor(
                        x2, stg[0:32, 1344:1376], stg[0:32, 1376:1408], OP.add))
                    p.opm("dve", lambda E, b=b, ckv=ckv: E.tensor_copy(
                        ccpack[b * 32:(b + 1) * 32, 0:128], ckv))
                else:
                    tpk = p.opm("dve", lambda E, b=b, bank=bank: E.tensor_copy(
                        ccpack[b * 32:(b + 1) * 32, 128:256], bank[0:32, 0:128]))
                    bres.note_read(tpk)
        p.wait("pool", p.cur("dve"))
        tcc = p.dma("pool", ccin.ap(), ccpack[0:64, :], stream="hx")
        p.wait("pool", tcc)
        p.op("pool", lambda E: E.collective_compute(
            "AllGather", OP.bypass, replica_groups=[list(range(NCORES))],
            ins=[ccin.ap().opt()], outs=[ccout.ap().opt()]), is_dma=True)
        p.milestone("pool", kind="cc", sem="cc2", inc=1)

        p.raw_wait("pool", "cc1", 1)
        tg_last = None
        for b in range(B):
            p.op("pool", lambda E, b=b: E.indirect_dma_start(
                out=halo_tm[:, b, :], out_offset=None, in_=hx_out.ap(),
                in_offset=IndirectOffsetOnAxis(ap=haloidx[:, b:b + 1], axis=0)),
                is_dma=True)
            tg_last = p.milestone("pool", kind="dma", sem="hx_d", inc=16)
        for b in range(B):
            p.wait("pe", tg_last)
            for g in range(2):
                pb = psum_bf[g]
                br = pbank_bf[g]
                p.wait_all("pe", br.r)
                for j in range(8):
                    k = g * 8 + j
                    p.op("pe", lambda E, b=b, k=k, j=j, pb=pb: E.transpose(
                        pb[:, j * 128:(j + 1) * 128], halo_tm[:, b, k * 128:(k + 1) * 128],
                        ident_bf[:, :]))
                tq2 = p.milestone("pe")
                br.note_write(tq2)
                p.wait("dve", tq2)
                tc3 = p.opm("dve", lambda E, b=b, g=g, pb=pb: E.tensor_copy(
                    xT_halo[:, b, g * 8:(g + 1) * 8, :],
                    pb[:, :].rearrange("p (j n) -> p j n", j=8)))
                br.note_read(tc3)
        p.barrier()
        ar.pop()

        wflat = wbuf.rearrange("p a k n -> p (a k n)")
        qTt = wflat[:, 0:2048].rearrange("p (h n) -> p h n", h=16)
        lkTab = wflat[:, 2048:6144].rearrange("p (a h n) -> p a h n", a=2, h=16)
        mst = wflat[:, 6144:8192]
        atn = stg[:, 1040:1556].bitcast(BF16).rearrange("p (s n) -> p s n", s=2)
        ckT_w = wflat[:, 0:0]
        mst_guard = [None]

        def load_ckcv(b, ckT_w, cvall_w):
            p.raw_wait("pool", "cc2", 1)
            p.wait("pool", p.cur("dve"))
            p.wait("pool", p.cur("pe"))
            v4 = ccout.ap().rearrange("(r b q) n -> r b q n", r=NCORES, b=B)
            for cch in range(2):
                for r2 in range(4):
                    r = cch * 4 + r2
                    tr1 = p.dma("pool", stg[r2 * 32:(r2 + 1) * 32, cch * 128:(cch + 1) * 128],
                                v4[r, b, :, 0:128], stream="hx")
                    tr2 = p.dma("pool", stg[r2 * 32:(r2 + 1) * 32,
                                            256 + cch * 128:256 + (cch + 1) * 128],
                                v4[r, b, :, 128:256], stream="hx")
            p.wait("dve", tr1)
            p.wait("dve", tr2)
            p.opm("dve", lambda E, cvall_w=cvall_w: E.tensor_copy(
                cvall_w, stg[:, 256:512].rearrange("p (c n) -> p c n", c=2)))
            tkb = p.opm("dve", lambda E: E.tensor_copy(
                stg[:, 512:640].bitcast(BF16), stg[:, 0:256]))
            p.wait("pe", tkb)
            for c in range(2):
                pb = psum_bf[c]
                br = pbank_bf[c]
                p.wait_all("pe", br.r)
                tt = p.opm("pe", lambda E, c=c, pb=pb: E.transpose(
                    pb[:, 0:128], stg[:, 512:640].bitcast(BF16)[:, c * 128:(c + 1) * 128],
                    ident_bf[:, :]))
                br.note_write(tt)
                p.wait("dve", tt)
                te = p.opm("dve", lambda E, c=c, pb=pb, ckT_w=ckT_w: E.tensor_copy(
                    ckT_w[:, c * 128:(c + 1) * 128], pb[:, 0:128]))
                br.note_read(te)
            p.wait("pe", te)

        def transpose_tile16(src_tile, dst16):
            for g in range(2):
                pb = psum_bf[g]
                br = pbank_bf[g]
                p.wait_all("pe", br.r)
                for j in range(8):
                    k = g * 8 + j
                    p.op("pe", lambda E, k=k, j=j, pb=pb, src_tile=src_tile: E.transpose(
                        pb[:, j * 128:(j + 1) * 128], src_tile[:, k * 128:(k + 1) * 128],
                        ident_bf[:, :]))
                tq = p.milestone("pe")
                br.note_write(tq)
                p.wait("dve", tq)
                tc = p.opm("dve", lambda E, g=g, pb=pb, dst16=dst16: E.tensor_copy(
                    dst16[:, g * 8:(g + 1) * 8, :],
                    pb[:, :].rearrange("p (j n) -> p j n", j=8)))
                br.note_read(tc)
            return tc

        def evac_lv(dstbuf, slot_of):
            def f(jt, ns, bank, tmm):
                slot = slot_of(jt)
                p.wait("act", tmm)
                return p.opm("act", lambda E, slot=slot, ns=ns, bank=bank: E.activation(
                    dstbuf[:, slot, ns * NSL:(ns + 1) * NSL], bank[:, 0:NSL], AF.Copy))
            return f

        def bulk_norm_rope(tile_bf, nb, cosT, sinT, rtmp0):
            rtmp = rtmp0
            t3 = tile_bf.rearrange("p (h d) -> p h d", h=16)
            p.wait("act", p.cur("dve"))
            p.wait("act", p.cur("pe"))
            p.wait("act", p.cur("pool"))
            ta1 = p.opm("act", lambda E, tile_bf=tile_bf: E.activation(
                stg[:, :], tile_bf, AF.Square))
            p.wait("dve", ta1)
            p.op("dve", lambda E: E.tensor_reduce(
                small[:, 0:16].rearrange("p (h u) -> p h u", h=16),
                stg[:, :].rearrange("p (h d) -> p h d", h=16), AX.X, OP.add))
            tsv = p.opm("dve", lambda E: E.tensor_scalar(
                small[:, 0:16], small[:, 0:16], 1.0 / 128, 1e-6, OP.mult, OP.add))
            p.wait("act", tsv)
            ta2 = p.opm("act", lambda E: E.activation(small[:, 16:32], small[:, 0:16], AF.Sqrt))
            p.wait("dve", ta2)
            p.op("dve", lambda E: E.reciprocal(small[:, 16:32], small[:, 16:32]))
            s3 = stg[:, :].rearrange("p (h d) -> p h d", h=16)
            p.op("dve", lambda E, t3=t3, s3=s3: E.tensor_tensor(
                s3, t3, small[:, 16:32].rearrange("p (h u) -> p h u", h=16).broadcast_to(
                    (128, 16, 128)), OP.mult))
            p.op("dve", lambda E, s3=s3, nb=nb: E.tensor_tensor(
                s3, s3, nb.rearrange("p (u d) -> p u d", u=1).broadcast_to((128, 16, 128)),
                OP.mult))
            x1 = s3[:, :, 0:32]
            x2 = s3[:, :, 32:64]
            cB = cosT.rearrange("p (u d) -> p u d", u=1).broadcast_to((128, 16, 32))
            sB = sinT.rearrange("p (u d) -> p u d", u=1).broadcast_to((128, 16, 32))
            p.op("dve", lambda E, x2=x2, sB=sB, rtmp=rtmp: E.tensor_tensor(rtmp, x2, sB, OP.mult))
            p.op("dve", lambda E, t3=t3, x1=x1, cB=cB: E.tensor_tensor(
                t3[:, :, 0:32], x1, cB, OP.mult))
            p.op("dve", lambda E, t3=t3, rtmp=rtmp: E.tensor_tensor(
                t3[:, :, 0:32], t3[:, :, 0:32], rtmp, OP.subtract))
            p.op("dve", lambda E, x2=x2, cB=cB, rtmp=rtmp: E.tensor_tensor(rtmp, x2, cB, OP.mult))
            p.op("dve", lambda E, t3=t3, x1=x1, sB=sB: E.tensor_tensor(
                t3[:, :, 32:64], x1, sB, OP.mult))
            p.op("dve", lambda E, t3=t3, rtmp=rtmp: E.tensor_tensor(
                t3[:, :, 32:64], t3[:, :, 32:64], rtmp, OP.add))
            return p.opm("dve", lambda E, t3=t3, s3=s3: E.tensor_copy(
                t3[:, :, 64:128], s3[:, :, 64:128]))

        for b in range(B):
            ar.push()
            qtm = ar.take_bf16(NTB, D)
            lktm = ar.take_bf16(5, D)
            lvtm = ar.take_bf16(5, D)
            scr = ar.take_bf16(1024)
            rtmp = scr[:, 0:512].rearrange("p (h n) -> p h n", h=16)
            atT = scr[:, 0:1024].rearrange("p (s n) -> p s n", s=2)
            ask = stg[:, 1812:1940].bitcast(BF16).rearrange("p (s n) -> p s n", s=2)
            ckcv = stg[:, 1556:1812].bitcast(BF16)
            ckT_w = ckcv[:, 0:256]
            cvall_w = ckcv[:, 256:512].rearrange("p (c n) -> p c n", c=2)
            tb_q = load_bias(B_l1[0], 0)
            p.wait("pe", tb_q)
            emit_proj(W_l1["wq"], 0, evac_lv(qtm, lambda j: j),
                      src=xT, tok0=b * 512, ntiles=NTB)
            tb_lk = load_bias(B_l1[1], 1)
            p.wait("pe", tb_lk)
            for jt in range(NTB):
                bulk_norm_rope(qtm[:, jt, :], qnb, cosq[:, jt, :], sinq[:, jt, :], rtmp)
            emit_proj(W_l1["wlk"], 1, evac_lv(lktm, lambda j: j + 1),
                      src=xT, tok0=b * 512, ntiles=NTB)
            emit_proj(W_l1["wlk"], 1, evac_lv(lktm, lambda j: 0),
                      src=xT_halo[:, b], tok0=0, ntiles=1)
            tb_lv = load_bias(B_l1[2], 0)
            p.wait("pe", tb_lv)
            for sl5 in range(5):
                bulk_norm_rope(lktm[:, sl5, :], knb,
                               cosh if sl5 == 0 else cosq[:, sl5 - 1, :],
                               sinh if sl5 == 0 else sinq[:, sl5 - 1, :], rtmp)
            emit_proj(W_l1["wlv"], 0, evac_lv(lvtm, lambda j: j + 1),
                      src=xT, tok0=b * 512, ntiles=NTB)
            emit_proj(W_l1["wlv"], 0, evac_lv(lvtm, lambda j: 0),
                      src=xT_halo[:, b], tok0=0, ntiles=1)
            p.barrier()
            load_ckcv(b, ckT_w, cvall_w)
            sres = [Res(), Res()]
            for jt in range(NTB):
                it = b * NTB + jt
                if mst_guard[0] is not None:
                    p.wait("act", mst_guard[0])
                p.wait("pe", p.cur("pool"))
                transpose_tile16(qtm[:, jt, :], qTt)
                if jt == 0:
                    transpose_tile16(lktm[:, 0, :], lkTab[:, 1])
                tlk = transpose_tile16(lktm[:, jt + 1, :], lkTab[:, jt % 2])
                p.wait("pe", tlk)
                lkT_prev = lkTab[:, (jt + 1) % 2]
                lkT_cur = lkTab[:, jt % 2]
                fe = {}

                def frontend_a(h, jt=jt, lkT_prev=lkT_prev, lkT_cur=lkT_cur):
                    sl = h % 2
                    bank_s = psum[sl]
                    bres_s = pbank[sl]
                    p.wait_all("pe", bres_s.r)
                    p.op("pe", lambda E, h=h, bank_s=bank_s: E.matmul(
                        bank_s[:, 0:256], qTt[:, h, :], ckT_w, start=True, stop=False))
                    p.op("pe", lambda E, h=h, bank_s=bank_s: E.matmul(
                        bank_s[:, 256:257], qTt[:, h, :], sinkT_bf[:, h:h + 1],
                        start=False, stop=False))
                    tcs = p.opm("pe", lambda E, bank_s=bank_s, jt=jt: E.matmul(
                        bank_s[:, 0:257], ident_bf[:, :], cmask[:, jt, 0:257],
                        start=False, stop=True))
                    bres_s.note_write(tcs)
                    bank_l = psum[2 + sl]
                    bres_l = pbank[2 + sl]
                    p.wait_all("pe", bres_l.r)
                    p.op("pe", lambda E, h=h, bank_l=bank_l, lkT_prev=lkT_prev: E.matmul(
                        bank_l[:, 0:128], qTt[:, h, :], lkT_prev[:, h, :],
                        start=True, stop=False))
                    p.op("pe", lambda E, h=h, bank_l=bank_l, lkT_cur=lkT_cur: E.matmul(
                        bank_l[:, 128:256], qTt[:, h, :], lkT_cur[:, h, :],
                        start=False, stop=False))
                    tls = p.opm("pe", lambda E, bank_l=bank_l, jt=jt: E.matmul(
                        bank_l[:, 0:256], ident_bf[:, :], lmask[:, min(jt, 1), :],
                        start=False, stop=True))
                    bres_l.note_write(tls)
                    fe[(h, "mm")] = (tcs, tls)

                def frontend_s(h, jt=jt):
                    sl = h % 2
                    bank_s = psum[sl]
                    bres_s = pbank[sl]
                    bank_l = psum[2 + sl]
                    bres_l = pbank[2 + sl]
                    tcs, tls = fe[(h, "mm")]
                    so = sl * 520
                    sc0, sc1 = 40 + sl * 6, 42 + sl * 6
                    sm3 = stg[:, so:so + 514].rearrange("p (w n) -> p w n", w=2)
                    p.wait("dve", tcs)
                    p.wait("dve", tls)
                    p.op("dve", lambda E, bank_s=bank_s, sc0=sc0: E.tensor_reduce(
                        small[:, sc0:sc0 + 1].rearrange("p (w u) -> p w u", w=1),
                        bank_s[:, 0:257].rearrange("p (u n) -> p u n", u=1), AX.X, OP.max))
                    p.op("dve", lambda E, bank_l=bank_l, sc0=sc0: E.tensor_reduce(
                        small[:, sc0 + 1:sc0 + 2].rearrange("p (w u) -> p w u", w=1),
                        bank_l[:, 0:256].rearrange("p (u n) -> p u n", u=1), AX.X, OP.max))
                    tmx = p.opm("dve", lambda E, sc0=sc0: E.tensor_scalar(
                        small[:, sc0:sc0 + 2], small[:, sc0:sc0 + 2], -SCALE, 0.0,
                        OP.mult, OP.bypass))
                    p.wait("act", tmx)
                    if h >= 2:
                        p.wait("act", fe[h - 2])
                    p.op("act", lambda E, bank_s=bank_s, so=so, sc0=sc0, sc1=sc1: E.activation(
                        stg[:, so:so + 257], bank_s[:, 0:257], AF.Exp, scale=SCALE,
                        bias=small[:, sc0:sc0 + 1], accum_out=small[:, sc1:sc1 + 1]))
                    tex = p.opm("act", lambda E, bank_l=bank_l, so=so, sc0=sc0, sc1=sc1: E.activation(
                        stg[:, so + 257:so + 513], bank_l[:, 0:256], AF.Exp, scale=SCALE,
                        bias=small[:, sc0 + 1:sc0 + 2], accum_out=small[:, sc1 + 1:sc1 + 2]))
                    bres_s.note_read(tex)
                    bres_l.note_read(tex)
                    p.wait("dve", tex)
                    trc = p.opm("dve", lambda E, sc1=sc1: E.reciprocal(
                        small[:, sc1 + 2:sc1 + 4], small[:, sc1:sc1 + 2]))
                    p.wait("pool", trc)
                    p.wait_all("pool", sres[sl].r)
                    tat = p.opm("pool", lambda E, sl=sl, sm3=sm3, sc1=sc1: E.tensor_tensor(
                        atn[:, sl, 0:514].rearrange("p (w n) -> p w n", w=2), sm3,
                        small[:, sc1 + 2:sc1 + 4].rearrange(
                            "p (w u) -> p w u", w=2).broadcast_to((128, 2, 257)), OP.mult))
                    fe[h] = tat

                def backend(h, jt=jt, lvtm=lvtm, atT=atT, ask=ask):
                    sl = h % 2
                    pb = psum_bf[sl]
                    br = pbank_bf[sl]
                    tat = fe[h]
                    p.wait("pe", tat)
                    p.wait_all("pe", br.r)
                    p.op("pe", lambda E, sl=sl, pb=pb: E.transpose(
                        pb[:, 0:128], atn[:, sl, 0:128], ident_bf[:, :]))
                    p.op("pe", lambda E, sl=sl, pb=pb: E.transpose(
                        pb[:, 128:256], atn[:, sl, 128:256], ident_bf[:, :]))
                    p.op("pe", lambda E, sl=sl, pb=pb: E.transpose(
                        pb[:, 256:384], atn[:, sl, 257:385], ident_bf[:, :]))
                    p.op("pe", lambda E, sl=sl, pb=pb: E.transpose(
                        pb[:, 384:512], atn[:, sl, 385:513], ident_bf[:, :]))
                    ts5 = p.opm("pe", lambda E, sl=sl, pb=pb: E.transpose(
                        pb[0:1, 512:640], atn[:, sl, 256:257], ident_bf[:, :]))
                    br.note_write(ts5)
                    p.wait("act", ts5)
                    p.op("act", lambda E, sl=sl, pb=pb, atT=atT: E.activation(
                        atT[:, sl, :], pb[:, 0:512], AF.Copy))
                    tev5 = p.opm("act", lambda E, sl=sl, pb=pb, ask=ask: E.activation(
                        ask[0:1, sl, :], pb[0:1, 512:640], AF.Copy))
                    br.note_read(tev5)
                    sres[sl].note_read(ts5)
                    bank_c = psum[4 + sl]
                    bres_c = pbank[4 + sl]
                    p.wait("pe", tev5)
                    p.wait_all("pe", bres_c.r)
                    p.op("pe", lambda E, sl=sl, bank_c=bank_c, atT=atT: E.matmul(
                        bank_c[:, 0:128], atT[:, sl, 0:128], cvall_w[:, 0, :],
                        start=True, stop=False))
                    p.op("pe", lambda E, sl=sl, bank_c=bank_c, atT=atT: E.matmul(
                        bank_c[:, 0:128], atT[:, sl, 128:256], cvall_w[:, 1, :],
                        start=False, stop=False))
                    p.op("pe", lambda E, h=h, sl=sl, bank_c=bank_c, ask=ask: E.matmul(
                        bank_c[:, 0:128], ask[0:1, sl, :], sinkv_row[0:1, h * 128:(h + 1) * 128],
                        start=False, stop=False))
                    p.op("pe", lambda E, h=h, sl=sl, bank_c=bank_c, lvtm=lvtm, jt=jt: E.matmul(
                        bank_c[:, 0:128], atT[:, sl, 256:384], lvtm[:, jt, h * 128:(h + 1) * 128],
                        start=False, stop=False))
                    tcx = p.opm("pe", lambda E, h=h, sl=sl, bank_c=bank_c, lvtm=lvtm, jt=jt: E.matmul(
                        bank_c[:, 0:128], atT[:, sl, 384:512],
                        lvtm[:, jt + 1, h * 128:(h + 1) * 128], start=False, stop=True))
                    bres_c.note_write(tcx)
                    p.wait("act", tcx)
                    tvm = p.opm("act", lambda E, h=h, bank_c=bank_c: E.activation(
                        mst[:, h * 128:(h + 1) * 128], bank_c[:, 0:128], AF.Copy, scale=0.5))
                    bres_c.note_read(tvm)

                for h in range(18):
                    if h >= 1 and h - 1 < 16:
                        frontend_s(h - 1)
                    if h < 16:
                        frontend_a(h)
                    if h >= 2:
                        backend(h - 2)
                p.wait("pe", p.cur("act"))
                for g in range(2):
                    pb = psum_bf[g]
                    br = pbank_bf[g]
                    p.wait_all("pe", br.r)
                    for j in range(8):
                        k = g * 8 + j
                        p.op("pe", lambda E, k=k, j=j, pb=pb: E.transpose(
                            pb[:, j * 128:(j + 1) * 128], mst[:, k * 128:(k + 1) * 128],
                            ident_bf[:, :]))
                    tq6 = p.milestone("pe")
                    br.note_write(tq6)
                    p.wait("dve", tq6)
                    tc6 = p.opm("dve", lambda E, b=b, jt=jt, g=g, pb=pb: E.tensor_copy(
                        xT[:, g * 8:(g + 1) * 8, b * 512 + jt * 128:b * 512 + (jt + 1) * 128],
                        pb[:, :].rearrange("p (j n) -> p j n", j=8)))
                    br.note_read(tc6)
                mst_guard[0] = p.cur("pe")
            p.barrier()
            ar.pop()

        acc_bf = ar.take_bf16(NT, D)
        tb_o = load_bias(B_l1[3], 1)
        p.wait("pe", tb_o)

        def evac_accbf(it, ns, bank, tmm):
            p.wait("act", tmm)
            return p.opm("act", lambda E, it=it, ns=ns, bank=bank: E.activation(
                acc_bf[:, it, ns * NSL:(ns + 1) * NSL], bank[:, 0:NSL], AF.Copy))

        emit_proj(W_l1["wo"], 1, evac_accbf)
        p.wait("dve", p.cur("act"))
        for it in range(NT):
            ln_stats(acc_bf[:, it, :], it, 4)
            mcol = stats[:, it, 4:5]
            rcol = stats[:, it, 5:6]
            p.opm("dve", lambda E, it=it, m=mcol, r=rcol: E.tensor_scalar(
                stg[:, :], acc_bf[:, it, :], m, r, OP.subtract, OP.mult))
            ta = p.opm("dve", lambda E, it=it: E.tensor_tensor(
                xres[:, it, :], xres[:, it, :], stg[:, :], OP.add))
            p.wait("act", ta)
        ar.pop()

    if DEBUG_TAP != "l0_attn":
        emit_moe_ffn(F0_w13, F0_wdn, gtile0, nhc=8, gate_af=AF.Silu,
                     scale_input=True, seed_xln=False, router_softmax=False)
    if DEBUG_TAP not in ("l0_attn", "l0_ffn"):
        emit_l1_attn()
    if DEBUG_TAP not in ("l0_attn", "l0_ffn", "l1_attn"):
        emit_moe_ffn(F1_w13, F1_wdn, gtile1, nhc=11, gate_af=AF.Sigmoid,
                     scale_input=False, seed_xln=True, router_softmax=True)

    t_done = p.cur("dve")
    p.wait("sp", t_done)
    t_outs = []
    tgt = dbg if DEBUG_TAP is not None else OUT
    for it in range(NT):
        t_outs.append(p.dma("sp", tgt[it], xres[:, it, :], stream="xio"))
    if DEBUG_TAP is not None:
        for it in range(NT):
            t_outs.append(p.dma("sp", OUT[it], xres[:, it, :], stream="xio"))
    p.wait_all("sp", t_outs)

    with nc.Block() as block:
        p.emit(nc, block, stack)
    stack.close()
    return nc, p



def _prep_inputs(inputs):
    x = np.asarray(inputs["L0_attn_x"])
    ident = np.eye(128, dtype=np.float32)
    pos = np.arange(S, dtype=np.float32)
    decay_full = np.exp(-pos / S).astype(np.float32)

    wq = tile_proj_weight(np.ascontiguousarray(np.asarray(inputs["L0_attn_Wq"]).T))
    wk = tile_proj_weight(np.ascontiguousarray(np.asarray(inputs["L0_attn_Wk"]).T))
    wv = tile_proj_weight(np.ascontiguousarray(np.asarray(inputs["L0_attn_Wv"]).T))
    wo = tile_proj_weight(np.ascontiguousarray(np.asarray(inputs["L0_attn_Wo"]).T))
    l0_bias = np.stack([np.asarray(inputs[f"L0_attn_{k}"])[None, :] for k in
                        ("bq", "bk", "bv", "bo")])

    def moe_blobs(w1s, w3s, w2s, sw1, sw3, sw2):
        m13, mdn = [], []
        for i in range(9):
            w1m = sw1 if i == 0 else w1s[i - 1]
            w3m = sw3 if i == 0 else w3s[i - 1]
            w2m = sw2 if i == 0 else w2s[i - 1]
            a = tile_proj_weight(np.ascontiguousarray(w1m.T), 128)
            b = tile_proj_weight(np.ascontiguousarray(w3m.T), 128)
            m13.append(np.concatenate([a, b], axis=-1))
            w2t = np.ascontiguousarray(w2m.T)
            nhc = w2t.shape[0] // 128
            halves = [np.ascontiguousarray(
                w2t[:, oh * 1024:(oh + 1) * 1024].reshape(nhc, 128, 1024).transpose(1, 0, 2))
                for oh in range(2)]
            mdn.append(np.stack(halves))
        return np.ascontiguousarray(np.stack(m13)), np.ascontiguousarray(np.stack(mdn))

    f0_w13, f0_wdn = moe_blobs(
        np.asarray(inputs["L0_ffn_w1"]), np.asarray(inputs["L0_ffn_w3"]),
        np.asarray(inputs["L0_ffn_w2"]), np.asarray(inputs["L0_ffn_shared_w1"]),
        np.asarray(inputs["L0_ffn_shared_w3"]), np.asarray(inputs["L0_ffn_shared_w2"]))
    f0_gate = np.ascontiguousarray(
        np.asarray(inputs["L0_ffn_gate"]).T.reshape(KC, 128, 8).transpose(1, 0, 2))
    f0_eb = np.asarray(inputs["L0_ffn_expert_bias"])[None, :]
    iota8 = np.arange(8, dtype=np.float32)[None, :]

    l1_wq = tile_proj_weight(np.ascontiguousarray(np.asarray(inputs["L1_attn_Wq"])))
    l1_wlk = tile_proj_weight(np.ascontiguousarray(np.asarray(inputs["L1_attn_Wlk"])))
    l1_wlv = tile_proj_weight(np.ascontiguousarray(np.asarray(inputs["L1_attn_Wlv"])))
    l1_wo = tile_proj_weight(np.ascontiguousarray(np.asarray(inputs["L1_attn_Wo"])))
    l1_bias = np.stack([np.asarray(inputs[f"L1_attn_{k}"])[None, :] for k in
                        ("bq", "blk", "blv", "bo")])
    l1_wc = np.ascontiguousarray(
        np.asarray(inputs["L1_attn_Wc"]).reshape(KC, 128, 1).transpose(1, 0, 2))
    l1_bc = np.asarray(inputs["L1_attn_bc"]).reshape(1, 1)
    l1_wkv = np.ascontiguousarray(np.concatenate(
        [np.asarray(inputs["L1_attn_Wk"]).reshape(KC, 128, 128),
         np.asarray(inputs["L1_attn_Wv"]).reshape(KC, 128, 128)], -1).transpose(1, 0, 2))
    l1_bkv = np.concatenate([np.asarray(inputs["L1_attn_bk"]),
                             np.asarray(inputs["L1_attn_bv"])])[None, :]
    l1_qn = np.asarray(inputs["L1_attn_qn_w"])[None, :]
    l1_kn = np.asarray(inputs["L1_attn_kn_w"])[None, :]
    l1_sinkT = np.ascontiguousarray(np.asarray(inputs["L1_attn_sink_k"]).T)
    l1_sinkv = np.ascontiguousarray(np.asarray(inputs["L1_attn_sink_v"]).reshape(1, D))
    f1_w13, f1_wdn = moe_blobs(
        np.asarray(inputs["L1_ffn_w_gate"]), np.asarray(inputs["L1_ffn_w_up"]),
        np.asarray(inputs["L1_ffn_w_down"]), np.asarray(inputs["L1_ffn_shared_gate"]),
        np.asarray(inputs["L1_ffn_shared_up"]), np.asarray(inputs["L1_ffn_shared_down"]))
    f1_gate = np.ascontiguousarray(
        np.asarray(inputs["L1_ffn_router_w"]).T.reshape(KC, 128, 8).transpose(1, 0, 2))
    pattern = (np.arange(128)[:, None] // 16 == np.arange(8)[None, :]).astype(np.float32)
    inv_freq = (1.0 / 10000.0 ** (np.arange(32, dtype=np.float32) * 2.0 / 64.0))

    def rope_tabs(pos):
        ang = pos.astype(np.float32)[:, None] * inv_freq[None, :]
        return np.cos(ang).astype(np.float32), np.sin(ang).astype(np.float32)

    maps = []
    for c in range(NCORES):
        sl = slice(c * SPC, (c + 1) * SPC)
        xc = np.ascontiguousarray(
            np.concatenate([x[0, sl], x[1, sl]], axis=0)).reshape(NT, 128, D)
        dec = np.ascontiguousarray(np.tile(decay_full[sl], 2).reshape(NT, 128).T)
        pos_c = np.arange(c * SPC, (c + 1) * SPC)
        cq, sq = rope_tabs(pos_c)
        cosq_c = np.ascontiguousarray(cq.reshape(NTB, 128, 32).transpose(1, 0, 2))
        sinq_c = np.ascontiguousarray(sq.reshape(NTB, 128, 32).transpose(1, 0, 2))
        hpos = np.maximum(np.arange(c * SPC - 128, c * SPC), 0)
        cosh_c, sinh_c = rope_tabs(hpos)
        bends = (c * 32 + np.arange(32)) * 16 + 15
        cosck_c, sinck_c = rope_tabs(bends)
        cm = np.full((128, NTB, 257), -1e9, np.float32)
        cm[:, :, 256] = 0.0
        allends = np.arange(256) * 16 + 15
        for j in range(NTB):
            posj = c * SPC + j * 128 + np.arange(128)
            cm[:, j, 0:256] = np.where(allends[None, :] <= posj[:, None], 0.0, -1e9)
        cmask_c = np.ascontiguousarray(cm)
        lm = np.full((128, 2, 256), -1e9, np.float32)
        ii = np.arange(128)[:, None]
        kk = np.arange(256)[None, :]
        valid = (kk >= ii) & (kk <= ii + 128)
        lm[:, 1, :] = np.where(valid, 0.0, -1e9)
        v0 = valid & ((kk >= 128) if c == 0 else np.ones_like(valid))
        lm[:, 0, :] = np.where(v0, 0.0, -1e9)
        lmask_c = np.ascontiguousarray(lm)
        haloidx_c = np.ascontiguousarray(
            (np.arange(128)[:, None] + np.array([0, 128])[None, :] +
             (((c - 1) % NCORES) * 256)).astype(np.uint32))
        maps.append({
            "x": xc, "ident": ident, "decay": dec,
            "l0_wq": wq, "l0_wk": wk, "l0_wv": wv, "l0_wo": wo,
            "l0_bias": l0_bias, "iota8": iota8,
            "f0_w13": f0_w13, "f0_wdn": f0_wdn, "f0_gate": f0_gate, "f0_eb": f0_eb,
            "l1_wq": l1_wq, "l1_wlk": l1_wlk, "l1_wlv": l1_wlv, "l1_wo": l1_wo,
            "l1_bias": l1_bias, "l1_wc": l1_wc, "l1_bc": l1_bc, "l1_wkv": l1_wkv,
            "l1_bkv": l1_bkv, "l1_qn": l1_qn, "l1_kn": l1_kn,
            "l1_sinkT": l1_sinkT, "l1_sinkv": l1_sinkv,
            "f1_w13": f1_w13, "f1_wdn": f1_wdn, "f1_gate": f1_gate,
            "cosq": cosq_c, "sinq": sinq_c, "cosh": cosh_c, "sinh": sinh_c,
            "cosck": cosck_c, "sinck": sinck_c,
            "cmask": cmask_c, "lmask": lmask_c, "pattern": pattern,
            "haloidx": haloidx_c,
        })
    return maps


def kernel(**inputs):
    maps = _prep_inputs(inputs)
    nc, _ = build_program()
    res = run_bass_kernel_spmd(nc, maps, core_ids=list(range(NCORES)))
    out = np.empty((B, S, D), dtype=np.float32)
    key = "dbg" if DEBUG_TAP is not None else "out"
    for c in range(NCORES):
        oc = res.results[c][key].reshape(TPC, D)
        out[0, c * SPC:(c + 1) * SPC] = oc[:SPC]
        out[1, c * SPC:(c + 1) * SPC] = oc[SPC:]
    return out
